# version 06 (final)
# speedup vs baseline: 1.6793x; 1.1998x over previous
import contextlib
import numpy as np
import ml_dtypes
import concourse.bass as bass
import concourse.mybir as mybir
from concourse.bass_utils import run_bass_kernel_spmd

F32 = mybir.dt.float32
BF16 = mybir.dt.bfloat16
ALU = mybir.AluOpType
AF = mybir.ActivationFunctionType
AX = mybir.AxisListType

NCORES = 8
B = 16
S = 2048
D = 1024
TPC = (B // NCORES) * S
NT = TPC // 128
DCH = D // 128
EPS = 1e-5

SEMS = ["dy", "dg", "v", "s", "t", "g", "st", "cc"]
DEBUG = False


class Ctx:
    def __init__(self, nc, sems):
        self.nc = nc
        self.sems = sems
        self.cnt = {k: 0 for k in sems}
        self.cur = None
        self.eng = None
        self.pend = {}

    def begin(self, eng_name, eng):
        self.cur = eng_name
        self.eng = eng
        self.cnt = {k: 0 for k in self.sems}
        self.pend = {}

    def flush(self):
        for sem, vv in self.pend.items():
            self.eng.wait_ge(self.sems[sem], vv)
        self.pend = {}

    def op(self, eng_name, fn, incs=()):
        if self.cur == eng_name:
            pend = list(self.pend.items())
            self.pend = {}
            for sem, vv in pend[:-1]:
                self.eng.wait_ge(self.sems[sem], vv)
            ins = fn(self.eng)
            if pend:
                ins = ins._wait_ge(self.sems[pend[-1][0]], pend[-1][1])
            for s_, a in incs:
                ins = ins.then_inc(self.sems[s_], a)
        for s_, a in incs:
            self.cnt[s_] += a

    def v(self, fn):
        self.wait("vector", "v")
        self.op("vector", fn, incs=[("v", 1)])

    def s(self, fn):
        self.wait("scalar", "s")
        self.op("scalar", fn, incs=[("s", 1)])

    def t(self, fn):
        self.op("tensor", fn, incs=[("t", 1)])

    def g(self, fn):
        self.op("gpsimd", fn, incs=[("g", 1)])

    def v0(self, fn):
        self.wait("vector", "v")
        self.op("vector", fn, incs=[("v", 1)])

    def s0(self, fn):
        self.wait("scalar", "s")
        self.op("scalar", fn, incs=[("s", 1)])

    def t0(self, fn):
        self.op("tensor", fn)

    def vhaz(self):
        self.wait("vector", "v")

    def shaz(self):
        self.wait("scalar", "s")

    def dy(self, fn):
        self.op("sync", fn, incs=[("dy", 16)])

    def dg(self, fn):
        self.op("gpsimd", fn, incs=[("dg", 16)])

    def store(self, fn):
        self.op("sync", fn, incs=[("st", 16)])

    def wait(self, eng_name, sem, val=None):
        vv = self.cnt[sem] if val is None else val
        if self.cur == eng_name and vv > 0:
            if self.pend.get(sem, 0) < vv:
                self.pend[sem] = vv

    def now(self, sem):
        return self.cnt[sem]


def run_phases(nc, emit_fn):
    with contextlib.ExitStack() as st:
        sems = {s_: st.enter_context(nc.semaphore(s_)) for s_ in SEMS}
        ctx = Ctx(nc, sems)
        with nc.Block() as block:
            @block.sync
            def _(e):
                ctx.begin("sync", e)
                emit_fn(ctx)
                ctx.flush()

            @block.vector
            def _(e):
                ctx.begin("vector", e)
                emit_fn(ctx)
                ctx.flush()

            @block.scalar
            def _(e):
                ctx.begin("scalar", e)
                emit_fn(ctx)
                ctx.flush()

            @block.tensor
            def _(e):
                ctx.begin("tensor", e)
                emit_fn(ctx)
                ctx.flush()

            @block.gpsimd
            def _(e):
                ctx.begin("gpsimd", e)
                emit_fn(ctx)
                ctx.flush()


def run_stage(nc, ctx, emit_fn):
    snap = dict(ctx.cnt)
    with nc.Block() as block:
        def mk(name):
            def f(e):
                ctx.cur = name
                ctx.eng = e
                ctx.cnt = dict(snap)
                ctx.pend = {}
                emit_fn(ctx)
                ctx.flush()
            return f
        block.sync(mk("sync"))
        block.vector(mk("vector"))
        block.scalar(mk("scalar"))
        block.tensor(mk("tensor"))
        block.gpsimd(mk("gpsimd"))
    ctx.cur = None
    ctx.eng = None
    ctx.cnt = dict(snap)
    emit_fn(ctx)


def _mk():
    return bass.Bass("TRN2", target_bir_lowering=False)


def bcast_row(vec_ap, n):
    return vec_ap.unsqueeze(0).broadcast_to([128, n])



def emit_ln(c, x_sb, scr, ln_bf, lnT, pst, ident, ln_f32=None, rms=False,
            gamma_rep=None, d=D):
    cols = scr["cols"]
    xc = scr["xc"]
    dch = d // 128
    if not rms:
        c.v0(lambda e: e.reduce_sum(out=cols[:, 0:1], in_=x_sb[:], axis=AX.X))
        c.vhaz()
        c.v0(lambda e: e.tensor_scalar_mul(out=cols[:, 1:2], in0=cols[:, 0:1], scalar1=1.0 / d))
        c.vhaz()
        c.v(lambda e: e.tensor_scalar_sub(out=xc[:], in0=x_sb[:], scalar1=cols[:, 1:2]))
        src = xc
    else:
        src = x_sb
        c.wait("scalar", "dy", c.now("dy"))
        c.wait("scalar", "dg", c.now("dg"))
    c.wait("scalar", "v")
    c.s(lambda e: e.activation(out=scr["sq"][:], in_=src[:], func=AF.Square,
                               accum_out=cols[:, 2:3]))
    c.wait("vector", "s")
    c.v(lambda e: e.tensor_scalar(out=cols[:, 3:4], in0=cols[:, 2:3],
                                  scalar1=1.0 / d, scalar2=EPS,
                                  op0=ALU.mult, op1=ALU.add))
    c.wait("scalar", "v")
    c.s(lambda e: e.sqrt(out=cols[:, 4:5], in_=cols[:, 3:4]))
    c.wait("vector", "s")
    c.v0(lambda e: e.reciprocal(out=cols[:, 5:6], in_=cols[:, 4:5]))
    c.vhaz()
    if gamma_rep is not None:
        c.v0(lambda e: e.tensor_scalar_mul(out=scr["sq"][:], in0=src[:], scalar1=cols[:, 5:6]))
        c.v(lambda e: e.tensor_mul(out=ln_bf[:], in0=scr["sq"][:], in1=gamma_rep[:]))
        if ln_f32 is not None:
            c.v(lambda e: e.tensor_mul(out=ln_f32[:], in0=scr["sq"][:], in1=gamma_rep[:]))
    else:
        c.v(lambda e: e.tensor_scalar_mul(out=ln_bf[:], in0=src[:], scalar1=cols[:, 5:6]))
        if ln_f32 is not None:
            c.v(lambda e: e.tensor_scalar_mul(out=ln_f32[:], in0=src[:], scalar1=cols[:, 5:6]))
    if lnT is not None:
        c.wait("tensor", "v")
        c.wait("tensor", "s")
        for j in range(dch):
            inc = [("t", 1)] if j == dch - 1 else []
            c.op("tensor", lambda e, j=j: e.transpose(
                out=pst[:, j * 128:(j + 1) * 128],
                in_=ln_bf[:, j * 128:(j + 1) * 128], identity=ident[:]), incs=inc)
        c.wait("scalar", "t")
        c.s(lambda e: e.copy(out=lnT, in_=pst[:].rearrange("p (kc t) -> p kc t", kc=dch)
                             if lnT.ndim == 3 else pst[:]))



def build_l0_ffn(fio=None):
    if fio is None:
        nc = _mk()
        ctx = None
        io = None
        pfx = ""
    else:
        nc, ctx, io, pfx = fio
    x_in = nc.dram_tensor("x_in", [TPC, D], F32, kind="ExternalInput").ap() if fio is None else io["x_in"]
    wT = nc.dram_tensor("wT", [D, D], F32, kind="ExternalInput").ap() if fio is None else io["wT"]
    alpha = nc.dram_tensor("alpha", [1], F32, kind="ExternalInput").ap() if fio is None else io["alpha"]
    bias = nc.dram_tensor("bias", [D], F32, kind="ExternalInput").ap() if fio is None else io["bias"]
    ident_h = nc.dram_tensor("ident", [128, 128], BF16, kind="ExternalInput").ap() if fio is None else io["ident"]
    x_out = nc.dram_tensor("x_out", [TPC, D], F32, kind="ExternalOutput").ap() if fio is None else io["x_out"]
    scal_dram = nc.dram_tensor(pfx + "scal", [16], F32, kind="Internal").ap()

    with contextlib.ExitStack() as stk:
        e_ = stk.enter_context
        xt = [e_(nc.sbuf_tensor(pfx + f"xt{i}", [128, D], F32)) for i in range(2)]
        wf = e_(nc.sbuf_tensor(pfx + "wf", [128, DCH * D], F32))
        wtern = e_(nc.sbuf_tensor(pfx + "wtern", [128, DCH * D], BF16))
        ident = e_(nc.sbuf_tensor(pfx + "identt", [128, 128], BF16))
        xc = e_(nc.sbuf_tensor(pfx + "xc", [128, D], F32))
        sq = e_(nc.sbuf_tensor(pfx + "sq", [128, D], F32))
        cols = e_(nc.sbuf_tensor(pfx + "cols", [128, 16], F32))
        ln_bf = e_(nc.sbuf_tensor(pfx + "ln_bf", [128, D], BF16))
        lnT = e_(nc.sbuf_tensor(pfx + "lnT", [128, D], BF16))
        t1 = e_(nc.sbuf_tensor(pfx + "t1", [128, D], F32))
        t2 = e_(nc.sbuf_tensor(pfx + "t2", [128, D], F32))
        out1 = e_(nc.sbuf_tensor(pfx + "out1", [128, D], F32))
        acol = e_(nc.sbuf_tensor(pfx + "acol", [128, 1], F32))
        dcol = e_(nc.sbuf_tensor(pfx + "dcol", [128, 2], F32))
        brep = e_(nc.sbuf_tensor(pfx + "brep", [128, D], F32))
        ones = e_(nc.sbuf_tensor(pfx + "ones", [128, 1], F32))
        tot = e_(nc.sbuf_tensor(pfx + "tot", [1, 1], F32))
        pst = [e_(nc.psum_tensor(pfx + f"pst{i}", [128, D], BF16)) for i in range(2)]
        pso = [e_(nc.psum_tensor(pfx + "pso0", [128, D], F32))] * 2
        pred = e_(nc.psum_tensor(pfx + "pred", [1, 1], F32))

        def emit(c):
            c.dy(lambda e: e.dma_start(out=ident[:], in_=ident_h))
            c.dy(lambda e: e.dma_start(
                out=wf[:].rearrange("p (kc n) -> p kc n", kc=DCH),
                in_=wT.rearrange("(kc p) n -> p kc n", p=128)))
            c.dy(lambda e: e.dma_start(out=brep[:], in_=bcast_row(bias, D)))
            c.dy(lambda e: e.dma_start(out=acol[:], in_=alpha.unsqueeze(0).broadcast_to([128, 1])))
            c.wait("scalar", "dy")
            for j in range(DCH):
                inc = [("s", 1)] if j == DCH - 1 else []
                c.op("scalar", lambda e, j=j: e.activation(
                    out=sq[:], in_=wf[:, j * D:(j + 1) * D], func=AF.Abs,
                    accum_out=cols[:, j:j + 1]), incs=inc)
            c.wait("vector", "s")
            c.v0(lambda e: e.reduce_sum(out=dcol[:, 0:1], in_=cols[:, 0:DCH], axis=AX.X))
            c.vhaz()
            c.v(lambda e: e.memset(ones[:], 1.0))
            c.wait("tensor", "v")
            c.t(lambda e: e.matmul(out=pred[:], lhsT=dcol[:, 0:1], rhs=ones[:],
                                   start=True, stop=True))
            c.wait("vector", "t")
            c.v(lambda e: e.tensor_copy(out=tot[:], in_=pred[:]))
            c.wait("sync", "v")
            c.dy(lambda e: e.dma_start(out=scal_dram[0:1], in_=tot[0:1, 0:1].squeeze(1)))
            c.wait("sync", "dy")
            c.dy(lambda e: e.dma_start(out=dcol[:, 0:1],
                                       in_=scal_dram[0:1].unsqueeze(0).broadcast_to([128, 1])))
            c.wait("vector", "dy")
            c.v0(lambda e: e.tensor_scalar_mul(out=dcol[:, 0:1], in0=dcol[:, 0:1],
                                               scalar1=0.7 / (D * D)))
            c.vhaz()
            c.v(lambda e: e.tensor_scalar_mul(out=dcol[:, 1:2], in0=dcol[:, 0:1], scalar1=-1.0))
            c.vhaz()
            for j in range(DCH):
                c.v0(lambda e, j=j: e.tensor_scalar(
                    out=t1[:], in0=wf[:, j * D:(j + 1) * D], scalar1=dcol[:, 0:1],
                    scalar2=None, op0=ALU.is_gt))
                c.v0(lambda e, j=j: e.tensor_scalar(
                    out=t2[:], in0=wf[:, j * D:(j + 1) * D], scalar1=dcol[:, 1:2],
                    scalar2=None, op0=ALU.is_lt))
                c.v(lambda e, j=j: e.tensor_sub(
                    out=wtern[:, j * D:(j + 1) * D], in0=t1[:], in1=t2[:]))
            wq_v = c.now("v")

            scr = {"xc": xc, "sq": sq, "cols": cols}
            vmark = {-2: 0, -1: 0}
            stmark = {-2: 0, -1: 0}
            for i in range(NT):
                buf = xt[i % 2]
                c.wait("sync", "v", vmark[i - 2])
                c.wait("sync", "st", stmark[i - 2])
                c.dy(lambda e, i=i, buf=buf: e.dma_start(
                    out=buf[:], in_=x_in[i * 128:(i + 1) * 128, :]))
                c.wait("vector", "dy")
                c.wait("vector", "t")
                emit_ln(c, buf, scr, ln_bf, lnT[:], pst[i % 2], ident)
                c.wait("tensor", "s")
                c.wait("tensor", "v", wq_v)
                ps = pso[i % 2]
                for nh in range(2):
                    for j in range(DCH):
                        inc = [("t", 1)] if (nh == 1 and j == DCH - 1) else []
                        c.op("tensor", lambda e, nh=nh, j=j, ps=ps: e.matmul(
                            out=ps[:, nh * 512:(nh + 1) * 512],
                            lhsT=lnT[:, j * 128:(j + 1) * 128],
                            rhs=wtern[:, j * D + nh * 512:j * D + nh * 512 + 512],
                            start=(j == 0), stop=(j == DCH - 1)), incs=inc)
                c.wait("vector", "t")
                c.v(lambda e, ps=ps: e.scalar_tensor_tensor(
                    out=out1[:], in0=ps[:], scalar=acol[:, 0:1], in1=brep[:],
                    op0=ALU.mult, op1=ALU.add))
                c.v(lambda e, buf=buf: e.scalar_tensor_tensor(
                    out=buf[:], in0=out1[:], scalar=0.5, in1=buf[:],
                    op0=ALU.mult, op1=ALU.add))
                vmark[i] = c.now("v")
                c.wait("sync", "v", vmark[i])
                c.store(lambda e, i=i, buf=buf: e.dma_start(
                    out=x_out[i * 128:(i + 1) * 128, :], in_=buf[:]))
                stmark[i] = c.now("st")
            c.wait("sync", "st", NT * 16)

        if fio is None:
            run_phases(nc, emit)
        else:
            run_stage(nc, ctx, emit)
    return nc



ST = 512
NST = TPC // ST
HCH = 4096 // 128


def build_l3_ffn(fio=None):
    if fio is None:
        nc = _mk()
        ctx = None
        io = None
        pfx = ""
    else:
        nc, ctx, io, pfx = fio
    x_in = nc.dram_tensor("x_in", [TPC, D], F32, kind="ExternalInput").ap() if fio is None else io["x_in"]
    w1b = nc.dram_tensor("w1b", [HCH, 128, DCH, 128], F32, kind="ExternalInput").ap() if fio is None else io["w1b"]
    w2b = nc.dram_tensor("w2b", [HCH, 128, DCH, 128], F32, kind="ExternalInput").ap() if fio is None else io["w2b"]
    h1b = nc.dram_tensor("h1b", [HCH, 128, HCH, 128], F32, kind="ExternalInput").ap() if fio is None else io["h1b"]
    h2b = nc.dram_tensor("h2b", [HCH, 128, HCH, 128], F32, kind="ExternalInput").ap() if fio is None else io["h2b"]
    w3a = nc.dram_tensor("w3a", [128, HCH, D], F32, kind="ExternalInput").ap() if fio is None else io["w3a"]
    s1h = nc.dram_tensor("s1h", [128, HCH], F32, kind="ExternalInput").ap() if fio is None else io["s1h"]
    s2h = nc.dram_tensor("s2h", [128, HCH], F32, kind="ExternalInput").ap() if fio is None else io["s2h"]
    hs1h = nc.dram_tensor("hs1h", [128, HCH], F32, kind="ExternalInput").ap() if fio is None else io["hs1h"]
    hs2h = nc.dram_tensor("hs2h", [128, HCH], F32, kind="ExternalInput").ap() if fio is None else io["hs2h"]
    s3 = nc.dram_tensor("s3", [D], F32, kind="ExternalInput").ap() if fio is None else io["s3"]
    ident_h = nc.dram_tensor("ident", [128, 128], BF16, kind="ExternalInput").ap() if fio is None else io["ident"]
    x_out = nc.dram_tensor("x_out", [TPC, D], F32, kind="ExternalOutput").ap() if fio is None else io["x_out"]

    NTT = ST // 128

    with contextlib.ExitStack() as stk:
        e_ = stk.enter_context
        xst = e_(nc.sbuf_tensor(pfx + "xst", [128, NTT * D], F32))
        ident = e_(nc.sbuf_tensor(pfx + "identt", [128, 128], BF16))
        xc = e_(nc.sbuf_tensor(pfx + "xc", [128, D], F32))
        sq = e_(nc.sbuf_tensor(pfx + "sq", [128, D], F32))
        cols = e_(nc.sbuf_tensor(pfx + "cols", [128, 16], F32))
        ln_bf = e_(nc.sbuf_tensor(pfx + "ln_bf", [128, D], BF16))
        lnT = e_(nc.sbuf_tensor(pfx + "lnT", [128, DCH * ST], BF16))
        h1T = e_(nc.sbuf_tensor(pfx + "h1T", [128, HCH * ST], BF16))
        h2T = e_(nc.sbuf_tensor(pfx + "h2T", [128, HCH * ST], BF16))
        wb1 = [e_(nc.sbuf_tensor(pfx + f"wb1{i}", [128, DCH * 128], BF16)) for i in range(2)]
        wb2 = [e_(nc.sbuf_tensor(pfx + f"wb2{i}", [128, DCH * 128], BF16)) for i in range(2)]
        hb = [e_(nc.sbuf_tensor(pfx + f"hb{i}", [128, HCH * 128], BF16)) for i in range(2)]
        w3t = e_(nc.sbuf_tensor(pfx + "w3t", [128, HCH * D], BF16))
        scol = e_(nc.sbuf_tensor(pfx + "scol", [128, 4 * HCH], F32))
        s3rep = e_(nc.sbuf_tensor(pfx + "s3rep", [128, D], F32))
        tmpA = e_(nc.sbuf_tensor(pfx + "tmpA", [128, ST], F32))
        outw = e_(nc.sbuf_tensor(pfx + "outw", [128, D], F32))
        pst = e_(nc.psum_tensor(pfx + "pst", [128, D], BF16))
        psA = [e_(nc.psum_tensor(pfx + f"psA{i}", [128, ST], F32)) for i in range(2)]
        psB = [e_(nc.psum_tensor(pfx + f"psB{i}", [128, ST], F32)) for i in range(2)]
        pso = [e_(nc.psum_tensor(pfx + "pso0", [128, D], F32))] * 2

        def emit(c):
            c.dy(lambda e: e.dma_start(out=ident[:], in_=ident_h))
            c.dy(lambda e: e.dma_start(out=scol[:, 0:HCH], in_=s1h))
            c.dy(lambda e: e.dma_start(out=scol[:, HCH:2 * HCH], in_=s2h))
            c.dy(lambda e: e.dma_start(out=scol[:, 2 * HCH:3 * HCH], in_=hs1h))
            c.dy(lambda e: e.dma_start(out=scol[:, 3 * HCH:4 * HCH], in_=hs2h))
            c.dy(lambda e: e.dma_start(out=s3rep[:], in_=bcast_row(s3, D)))
            c.dg(lambda e: e.dma_start(
                out=w3t[:].rearrange("p (kc n) -> p kc n", kc=HCH), in_=w3a))
            scr = {"xc": xc, "sq": sq, "cols": cols}
            lnT3 = lnT[:].rearrange("p (kc t) -> p kc t", kc=DCH)
            h1T3 = h1T[:].rearrange("p (kc t) -> p kc t", kc=HCH)
            h2T3 = h2T[:].rearrange("p (kc t) -> p kc t", kc=HCH)

            stage1_tmark = {}
            stage_marks = {}
            for sti in range(NST):
                for tl in range(NTT):
                    i = sti * NTT + tl
                    c.wait("sync", "st", c.now("st"))
                    c.dy(lambda e, i=i, tl=tl: e.dma_start(
                        out=xst[:, tl * D:(tl + 1) * D],
                        in_=x_in[i * 128:(i + 1) * 128, :]))
                    c.wait("vector", "dy")
                    c.wait("vector", "s")
                    c.wait("vector", "t", stage_marks.get(sti - 1, 0))
                    emit_ln(c, xst[:, tl * D:(tl + 1) * D], scr, ln_bf,
                            lnT3[:, :, tl * 128:(tl + 1) * 128], pst, ident)
                ln_v = c.now("v")
                ln_s = c.now("s")
                dgm = {}
                for och in range(HCH):
                    bi = och % 2
                    if och >= 2:
                        c.wait("gpsimd", "t")
                    c.dg(lambda e, och=och, bi=bi: e.dma_start(
                        out=wb1[bi][:].rearrange("p (kc n) -> p kc n", kc=DCH),
                        in_=w1b[och]))
                    c.dg(lambda e, och=och, bi=bi: e.dma_start(
                        out=wb2[bi][:].rearrange("p (kc n) -> p kc n", kc=DCH),
                        in_=w2b[bi if False else och]))
                    c.wait("tensor", "dg")
                    c.wait("tensor", "s", c.now("s"))
                    c.wait("tensor", "v", c.now("v"))
                    for j in range(DCH):
                        c.t0(lambda e, j=j, bi=bi, och=och: e.matmul(
                            out=psA[bi][:], lhsT=wb1[bi][:, j * 128:(j + 1) * 128],
                            rhs=lnT[:, j * ST:(j + 1) * ST],
                            start=(j == 0), stop=(j == DCH - 1)))
                    for j in range(DCH):
                        inc = [("t", 1)] if j == DCH - 1 else []
                        c.op("tensor", lambda e, j=j, bi=bi: e.matmul(
                            out=psB[bi][:], lhsT=wb2[bi][:, j * 128:(j + 1) * 128],
                            rhs=lnT[:, j * ST:(j + 1) * ST],
                            start=(j == 0), stop=(j == DCH - 1)), incs=inc)
                    c.wait("scalar", "t")
                    c.s(lambda e, bi=bi, och=och: e.activation(
                        out=tmpA[:], in_=psA[bi][:], func=AF.Silu,
                        scale=scol[:, och:och + 1]))
                    c.wait("vector", "s")
                    c.v(lambda e, bi=bi, och=och: e.scalar_tensor_tensor(
                        out=h1T3[:, och, :], in0=psB[bi][:],
                        scalar=scol[:, HCH + och:HCH + och + 1], in1=tmpA[:],
                        op0=ALU.mult, op1=ALU.mult))
                h1_v = c.now("v")
                for och in range(HCH):
                    bi = och % 2
                    if och >= 2:
                        c.wait("gpsimd", "t")
                    c.dg(lambda e, och=och, bi=bi: e.dma_start(
                        out=hb[bi][:].rearrange("p (kc n) -> p kc n", kc=HCH),
                        in_=h1b[och]))
                    c.wait("tensor", "dg")
                    c.wait("tensor", "v", c.now("v"))
                    c.wait("tensor", "s", c.now("s"))
                    for j in range(HCH):
                        inc = [("t", 1)] if j == HCH - 1 else []
                        c.op("tensor", lambda e, j=j, bi=bi: e.matmul(
                            out=psA[bi][:], lhsT=hb[bi][:, j * 128:(j + 1) * 128],
                            rhs=h1T[:, j * ST:(j + 1) * ST],
                            start=(j == 0), stop=(j == HCH - 1)), incs=inc)
                    c.wait("scalar", "t")
                    c.s(lambda e, bi=bi, och=och: e.activation(
                        out=h2T3[:, och, :], in_=psA[bi][:], func=AF.Gelu,
                        scale=scol[:, 2 * HCH + och:2 * HCH + och + 1]))
                h2_s = c.now("s")
                for och in range(HCH):
                    bi = och % 2
                    if och >= 2:
                        c.wait("gpsimd", "t")
                    c.dg(lambda e, och=och, bi=bi: e.dma_start(
                        out=hb[bi][:].rearrange("p (kc n) -> p kc n", kc=HCH),
                        in_=h2b[och]))
                    c.wait("tensor", "dg")
                    c.wait("tensor", "s", c.now("s"))
                    for j in range(HCH):
                        inc = [("t", 1)] if j == HCH - 1 else []
                        c.op("tensor", lambda e, j=j, bi=bi: e.matmul(
                            out=psA[bi][:], lhsT=hb[bi][:, j * 128:(j + 1) * 128],
                            rhs=h2T[:, j * ST:(j + 1) * ST],
                            start=(j == 0), stop=(j == HCH - 1)), incs=inc)
                    c.wait("scalar", "t")
                    c.s(lambda e, bi=bi, och=och: e.activation(
                        out=h1T3[:, och, :], in_=psA[bi][:], func=AF.Gelu,
                        scale=scol[:, 3 * HCH + och:3 * HCH + och + 1]))
                h3_s = c.now("s")
                c.wait("tensor", "dg", 16)
                for tl in range(NTT):
                    bi = tl % 2
                    c.wait("tensor", "s", h3_s)
                    c.wait("tensor", "v", c.now("v"))
                    for j in range(HCH):
                        for nh in range(2):
                            inc = [("t", 1)] if (j == HCH - 1 and nh == 1) else []
                            c.op("tensor", lambda e, j=j, nh=nh, bi=bi, tl=tl: e.matmul(
                                out=pso[bi][:, nh * 512:(nh + 1) * 512],
                                lhsT=h1T3[:, j, tl * 128:(tl + 1) * 128],
                                rhs=w3t[:, j * D + nh * 512:j * D + nh * 512 + 512],
                                start=(j == 0), stop=(j == HCH - 1)), incs=inc)
                    c.wait("vector", "t")
                    c.v(lambda e, bi=bi: e.tensor_mul(out=outw[:], in0=pso[bi][:], in1=s3rep[:]))
                    c.v(lambda e, tl=tl: e.scalar_tensor_tensor(
                        out=xst[:, tl * D:(tl + 1) * D], in0=outw[:], scalar=0.5,
                        in1=xst[:, tl * D:(tl + 1) * D], op0=ALU.mult, op1=ALU.add))
                    c.wait("sync", "v")
                    c.store(lambda e, sti=sti, tl=tl: e.dma_start(
                        out=x_out[(sti * NTT + tl) * 128:(sti * NTT + tl + 1) * 128, :],
                        in_=xst[:, tl * D:(tl + 1) * D]))
                stage_marks[sti] = c.now("t")
            c.wait("sync", "st", NST * NTT * 16)

        if fio is None:
            run_phases(nc, emit)
        else:
            run_stage(nc, ctx, emit)
    return nc



E2 = 2048
ECH = E2 // 128


def build_l0_attn(fio=None):
    if fio is None:
        nc = _mk()
        ctx = None
        io = None
        pfx = ""
    else:
        nc, ctx, io, pfx = fio
    x_in = nc.dram_tensor("x_in", [TPC, D], F32, kind="ExternalInput").ap() if fio is None else io["x_in"]
    wqb = nc.dram_tensor("wqb", [ECH, 128, DCH, 128], F32, kind="ExternalInput").ap() if fio is None else io["wqb"]
    wkb = nc.dram_tensor("wkb", [ECH, 128, DCH, 128], F32, kind="ExternalInput").ap() if fio is None else io["wkb"]
    wva = nc.dram_tensor("wva", [128, DCH, E2], F32, kind="ExternalInput").ap() if fio is None else io["wva"]
    woa = nc.dram_tensor("woa", [128, ECH, D], F32, kind="ExternalInput").ap() if fio is None else io["woa"]
    bqh = nc.dram_tensor("bqh", [128, ECH], F32, kind="ExternalInput").ap() if fio is None else io["bqh"]
    bkh = nc.dram_tensor("bkh", [128, ECH], F32, kind="ExternalInput").ap() if fio is None else io["bkh"]
    bvv = nc.dram_tensor("bvv", [E2], F32, kind="ExternalInput").ap() if fio is None else io["bvv"]
    bov = nc.dram_tensor("bov", [D], F32, kind="ExternalInput").ap() if fio is None else io["bov"]
    ident_h = nc.dram_tensor("ident", [128, 128], BF16, kind="ExternalInput").ap() if fio is None else io["ident"]
    x_out = nc.dram_tensor("x_out", [TPC, D], F32, kind="ExternalOutput").ap() if fio is None else io["x_out"]
    if DEBUG:
        d_kT = nc.dram_tensor("d_kT", [128, S], BF16, kind="ExternalOutput").ap()
        d_V = nc.dram_tensor("d_V", [128, E2], BF16, kind="ExternalOutput").ap()
        d_qT = nc.dram_tensor("d_qT", [128, ECH * 128], BF16, kind="ExternalOutput").ap()
        d_P = nc.dram_tensor("d_P", [128, S], BF16, kind="ExternalOutput").ap()
        d_o = nc.dram_tensor("d_o", [128, E2], BF16, kind="ExternalOutput").ap()
        d_cols = nc.dram_tensor("d_cols", [128, 16], F32, kind="ExternalOutput").ap()

    NTS = S // 128

    with contextlib.ExitStack() as stk:
        e_ = stk.enter_context
        xt = [e_(nc.sbuf_tensor(pfx + f"xt{i}", [128, D], F32)) for i in range(2)]
        ident = e_(nc.sbuf_tensor(pfx + "identt", [128, 128], BF16))
        xc = e_(nc.sbuf_tensor(pfx + "xc", [128, D], F32))
        sq = e_(nc.sbuf_tensor(pfx + "sq", [128, D], F32))
        cols = e_(nc.sbuf_tensor(pfx + "cols", [128, 16], F32))
        ln_bf = e_(nc.sbuf_tensor(pfx + "ln_bf", [128, D], BF16))
        lnT = e_(nc.sbuf_tensor(pfx + "lnT", [128, DCH * S], BF16))
        kT = e_(nc.sbuf_tensor(pfx + "kT", [128, ECH * S], BF16))
        V = e_(nc.sbuf_tensor(pfx + "V", [128, NTS * E2], BF16))
        wband = [e_(nc.sbuf_tensor(pfx + "wband0", [128, DCH * 128], BF16))] * 2
        wvband = [e_(nc.sbuf_tensor(pfx + "wvband0", [128, DCH * 512], BF16))] * 2
        woband = [e_(nc.sbuf_tensor(pfx + "woband0", [128, D], BF16))] * 2
        bqc = e_(nc.sbuf_tensor(pfx + "bqc", [128, ECH], F32))
        bkc = e_(nc.sbuf_tensor(pfx + "bkc", [128, ECH], F32))
        bvrep = e_(nc.sbuf_tensor(pfx + "bvrep", [128, E2], BF16))
        borep = e_(nc.sbuf_tensor(pfx + "borep", [128, D], F32))
        qT = e_(nc.sbuf_tensor(pfx + "qT", [128, ECH * 128], BF16))
        P = e_(nc.sbuf_tensor(pfx + "P", [128, S], BF16))
        PT = wvband[0][:, 0:S]
        obf = P
        oT = PT
        outw = xc
        psS = e_(nc.psum_tensor(pfx + "psS", [128, S], F32))
        pstP = e_(nc.psum_tensor(pfx + "pstP", [128, S], BF16))
        psF = e_(nc.psum_tensor(pfx + "psF", [128, D], F32))

        def emit(c):
            c.dy(lambda e: e.dma_start(out=ident[:], in_=ident_h))
            c.dy(lambda e: e.dma_start(out=bqc[:], in_=bqh))
            c.dy(lambda e: e.dma_start(out=bkc[:], in_=bkh))
            c.dg(lambda e: e.dma_start(out=bvrep[:], in_=bcast_row(bvv, E2)))
            c.dy(lambda e: e.dma_start(out=borep[:], in_=bcast_row(bov, D)))
            scr = {"xc": xc, "sq": sq, "cols": cols}
            lnT3 = lnT[:].rearrange("p (kc t) -> p kc t", kc=DCH)
            kT3 = kT[:].rearrange("p (ec t) -> p ec t", ec=ECH)
            V3 = V[:].rearrange("p (tt e) -> p tt e", tt=NTS)
            pst1 = pstP[:, 0:D]

            for sq_i in range(2):
                T0 = sq_i * S
                for tl in range(NTS):
                    i = T0 // 128 + tl
                    c.wait("sync", "st", c.now("st"))
                    c.wait("sync", "v", c.now("v"))
                    c.dy(lambda e, i=i, tl=tl: e.dma_start(
                        out=xt[tl % 2][:], in_=x_in[i * 128:(i + 1) * 128, :]))
                    c.wait("vector", "dy")
                    c.wait("vector", "t", c.now("t"))
                    emit_ln(c, xt[tl % 2], scr, ln_bf,
                            lnT3[:, :, tl * 128:(tl + 1) * 128], pst1, ident)
                ln_done_s = c.now("s")
                for nh in range(4):
                    bi = nh % 2
                    c.wait("gpsimd", "t", c.now("t"))
                    c.dg(lambda e, nh=nh, bi=bi: e.dma_start(
                        out=wvband[bi][:].rearrange("p (kc n) -> p kc n", kc=DCH),
                        in_=wva[:, :, nh * 512:(nh + 1) * 512]))
                    c.wait("tensor", "dg")
                    c.wait("tensor", "s", ln_done_s)
                    for tl in range(NTS):
                        c.wait("tensor", "v", c.now("v"))
                        for j in range(DCH):
                            inc = [("t", 1)] if j == DCH - 1 else []
                            c.op("tensor", lambda e, j=j, tl=tl, bi=bi: e.matmul(
                                out=psS[:, 0:512],
                                lhsT=lnT3[:, j, tl * 128:(tl + 1) * 128],
                                rhs=wvband[bi][:, j * 512:(j + 1) * 512],
                                start=(j == 0), stop=(j == DCH - 1)), incs=inc)
                        c.wait("vector", "t")
                        c.wait("vector", "dg")
                        c.v(lambda e, tl=tl, nh=nh: e.tensor_add(
                            out=V3[:, tl, nh * 512:(nh + 1) * 512],
                            in0=psS[:, 0:512], in1=bvrep[:, nh * 512:(nh + 1) * 512]))
                for ech in range(ECH):
                    bi = ech % 2
                    c.wait("gpsimd", "t", c.now("t"))
                    c.dg(lambda e, ech=ech, bi=bi: e.dma_start(
                        out=wband[bi][:].rearrange("p (kc n) -> p kc n", kc=DCH),
                        in_=wkb[ech]))
                    c.wait("tensor", "dg")
                    for nh in range(4):
                        c.wait("tensor", "v", c.now("v"))
                        for j in range(DCH):
                            inc = [("t", 1)] if j == DCH - 1 else []
                            c.op("tensor", lambda e, j=j, nh=nh, bi=bi: e.matmul(
                                out=psS[:, 512:1024],
                                lhsT=wband[bi][:, j * 128:(j + 1) * 128],
                                rhs=lnT3[:, j, nh * 512:(nh + 1) * 512],
                                start=(j == 0), stop=(j == DCH - 1)), incs=inc)
                        c.wait("vector", "t")
                        c.v(lambda e, ech=ech, nh=nh: e.tensor_scalar_add(
                            out=kT3[:, ech, nh * 512:(nh + 1) * 512],
                            in0=psS[:, 512:1024], in1=None) if False else
                            e.tensor_scalar_add(
                                out=kT3[:, ech, nh * 512:(nh + 1) * 512],
                                in0=psS[:, 512:1024], scalar1=bkc[:, ech:ech + 1]))
                for tl in range(NTS):
                    c.wait("tensor", "v", c.now("v"))
                    for ech in range(ECH):
                        bi = ech % 2
                        c.wait("gpsimd", "t", c.now("t"))
                        c.dg(lambda e, ech=ech, bi=bi: e.dma_start(
                            out=wband[bi][:].rearrange("p (kc n) -> p kc n", kc=DCH),
                            in_=wqb[ech]))
                        c.wait("tensor", "dg")
                        c.wait("tensor", "v", c.now("v"))
                        for j in range(DCH):
                            inc = [("t", 1)] if j == DCH - 1 else []
                            c.op("tensor", lambda e, j=j, bi=bi, tl=tl: e.matmul(
                                out=psF[:, 0:128],
                                lhsT=wband[bi][:, j * 128:(j + 1) * 128],
                                rhs=lnT3[:, j, tl * 128:(tl + 1) * 128],
                                start=(j == 0), stop=(j == DCH - 1)), incs=inc)
                        c.wait("vector", "t")
                        c.v(lambda e, ech=ech: e.tensor_scalar_add(
                            out=qT[:, ech * 128:(ech + 1) * 128], in0=psF[:, 0:128],
                            scalar1=bqc[:, ech:ech + 1]))
                    c.wait("tensor", "v", c.now("v"))
                    for nh in range(4):
                        for ech in range(ECH):
                            inc = [("t", 1)] if (nh == 3 and ech == ECH - 1) else []
                            c.op("tensor", lambda e, nh=nh, ech=ech: e.matmul(
                                out=psS[:, nh * 512:(nh + 1) * 512],
                                lhsT=qT[:, ech * 128:(ech + 1) * 128],
                                rhs=kT3[:, ech, nh * 512:(nh + 1) * 512],
                                start=(ech == 0), stop=(ech == ECH - 1)), incs=inc)
                    c.wait("vector", "t")
                    c.v0(lambda e: e.reduce_max(out=cols[:, 8:9], in_=psS[:], axis=AX.X))
                    c.vhaz()
                    c.v(lambda e: e.tensor_scalar_mul(out=cols[:, 9:10], in0=cols[:, 8:9],
                                                      scalar1=-1.0 / 32.0))
                    c.wait("scalar", "v")
                    c.shaz()
                    c.s(lambda e: e.activation(out=P[:], in_=psS[:], func=AF.Exp,
                                               scale=1.0 / 32.0, bias=cols[:, 9:10],
                                               accum_out=cols[:, 10:11]))
                    c.wait("vector", "s")
                    c.v0(lambda e: e.reciprocal(out=cols[:, 11:12], in_=cols[:, 10:11]))
                    c.vhaz()
                    c.wait("tensor", "s")
                    for tch in range(NTS):
                        inc = [("t", 1)] if tch == NTS - 1 else []
                        c.op("tensor", lambda e, tch=tch: e.transpose(
                            out=pstP[:, tch * 128:(tch + 1) * 128],
                            in_=P[:, tch * 128:(tch + 1) * 128], identity=ident[:]),
                            incs=inc)
                    c.wait("scalar", "t")
                    c.s(lambda e: e.copy(out=PT, in_=pstP[:]))
                    c.wait("tensor", "s")
                    for nh in range(4):
                        for tch in range(NTS):
                            inc = [("t", 1)] if (nh == 3 and tch == NTS - 1) else []
                            c.op("tensor", lambda e, nh=nh, tch=tch: e.matmul(
                                out=psS[:, nh * 512:(nh + 1) * 512],
                                lhsT=PT[:, tch * 128:(tch + 1) * 128],
                                rhs=V3[:, tch, nh * 512:(nh + 1) * 512],
                                start=(tch == 0), stop=(tch == NTS - 1)), incs=inc)
                    c.wait("vector", "t")
                    c.v(lambda e: e.tensor_scalar_mul(out=obf[:], in0=psS[:],
                                                      scalar1=cols[:, 11:12]))
                    if DEBUG and sq_i == 0 and tl == 0:
                        c.wait("sync", "v", c.now("v"))
                        c.wait("sync", "s", c.now("s"))
                        c.store(lambda e: e.dma_start(out=d_kT, in_=kT3[:, 0, :]))
                        c.store(lambda e: e.dma_start(out=d_V, in_=V3[:, 0, :]))
                        c.store(lambda e: e.dma_start(out=d_qT, in_=qT[:]))
                        c.store(lambda e: e.dma_start(out=d_P, in_=P[:]))
                        c.store(lambda e: e.dma_start(out=d_o, in_=obf[:]))
                        c.store(lambda e: e.dma_start(out=d_cols, in_=cols[:]))
                    c.wait("tensor", "v")
                    for ech in range(ECH):
                        inc = [("t", 1)] if ech == ECH - 1 else []
                        c.op("tensor", lambda e, ech=ech: e.transpose(
                            out=pstP[:, ech * 128:(ech + 1) * 128],
                            in_=obf[:, ech * 128:(ech + 1) * 128], identity=ident[:]),
                            incs=inc)
                    c.wait("scalar", "t")
                    c.s(lambda e: e.copy(out=oT, in_=pstP[:]))
                    ot_s = c.now("s")
                    for ech in range(ECH):
                        bi = ech % 2
                        c.wait("gpsimd", "t", c.now("t"))
                        c.dg(lambda e, ech=ech, bi=bi: e.dma_start(
                            out=woband[bi][:], in_=woa[:, ech, :]))
                        c.wait("tensor", "dg")
                        c.wait("tensor", "s", ot_s)
                        for nh in range(2):
                            inc = [("t", 1)] if nh == 1 else []
                            c.op("tensor", lambda e, ech=ech, nh=nh, bi=bi: e.matmul(
                                out=psF[:, nh * 512:(nh + 1) * 512],
                                lhsT=oT[:, ech * 128:(ech + 1) * 128],
                                rhs=woband[bi][:, nh * 512:(nh + 1) * 512],
                                start=(ech == 0), stop=(ech == ECH - 1)), incs=inc)
                    i = T0 // 128 + tl
                    c.wait("sync", "v", c.now("v"))
                    c.wait("sync", "st", c.now("st"))
                    c.dy(lambda e, i=i: e.dma_start(
                        out=xt[0][:], in_=x_in[i * 128:(i + 1) * 128, :]))
                    c.wait("vector", "dy")
                    c.wait("vector", "t")
                    c.v(lambda e: e.tensor_add(out=outw[:], in0=psF[:], in1=borep[:]))
                    c.v(lambda e: e.tensor_add(out=xt[0][:], in0=outw[:], in1=xt[0][:]))
                    c.wait("sync", "v")
                    c.store(lambda e, i=i: e.dma_start(
                        out=x_out[i * 128:(i + 1) * 128, :], in_=xt[0][:]))
            c.wait("sync", "st", c.now("st"))

        if fio is None:
            run_phases(nc, emit)
        else:
            run_stage(nc, ctx, emit)
    return nc



NH1 = 16
AH = 64


def build_l1_attn(fio=None):
    if fio is None:
        nc = _mk()
        ctx = None
        io = None
        pfx = ""
    else:
        nc, ctx, io, pfx = fio
    x_in = nc.dram_tensor("x_in", [TPC, D], F32, kind="ExternalInput").ap() if fio is None else io["x_in"]
    wqb = nc.dram_tensor("wqb", [DCH, 128, DCH, 128], F32, kind="ExternalInput").ap() if fio is None else io["wqb"]
    wkb = nc.dram_tensor("wkb", [DCH, 128, DCH, 128], F32, kind="ExternalInput").ap() if fio is None else io["wkb"]
    wva = nc.dram_tensor("wva", [128, DCH, D], F32, kind="ExternalInput").ap() if fio is None else io["wva"]
    woa = nc.dram_tensor("woa", [128, DCH, D], F32, kind="ExternalInput").ap() if fio is None else io["woa"]
    bqh = nc.dram_tensor("bqh", [128, DCH], F32, kind="ExternalInput").ap() if fio is None else io["bqh"]
    bkh = nc.dram_tensor("bkh", [128, DCH], F32, kind="ExternalInput").ap() if fio is None else io["bkh"]
    bvv = nc.dram_tensor("bvv", [D], F32, kind="ExternalInput").ap() if fio is None else io["bvv"]
    bov = nc.dram_tensor("bov", [D], F32, kind="ExternalInput").ap() if fio is None else io["bov"]
    cmask_h = nc.dram_tensor("cmask", [128, 128], F32, kind="ExternalInput").ap() if fio is None else io["cmask"]
    ident_h = nc.dram_tensor("ident", [128, 128], BF16, kind="ExternalInput").ap() if fio is None else io["ident"]
    x_out = nc.dram_tensor("x_out", [TPC, D], F32, kind="ExternalOutput").ap() if fio is None else io["x_out"]

    NTS = S // 128

    with contextlib.ExitStack() as stk:
        e_ = stk.enter_context
        xt = [e_(nc.sbuf_tensor(pfx + f"xt{i}", [128, D], F32)) for i in range(2)]
        ident = e_(nc.sbuf_tensor(pfx + "identt", [128, 128], BF16))
        cmask = e_(nc.sbuf_tensor(pfx + "cmaskt", [128, 128], F32))
        xc = e_(nc.sbuf_tensor(pfx + "xc", [128, D], F32))
        sq = e_(nc.sbuf_tensor(pfx + "sq", [128, D], F32))
        cols = e_(nc.sbuf_tensor(pfx + "cols", [128, 16], F32))
        ln_bf = e_(nc.sbuf_tensor(pfx + "ln_bf", [128, D], BF16))
        lnT = e_(nc.sbuf_tensor(pfx + "lnT", [128, DCH * S], BF16))
        qT = e_(nc.sbuf_tensor(pfx + "qTs", [128, DCH * S], BF16))
        kT = e_(nc.sbuf_tensor(pfx + "kTs", [128, DCH * S], BF16))
        V = e_(nc.sbuf_tensor(pfx + "V", [128, NTS * D], BF16))
        woT = e_(nc.sbuf_tensor(pfx + "woT", [128, DCH * D], BF16))
        wband = [e_(nc.sbuf_tensor(pfx + "wband0", [128, DCH * 128], BF16))] * 2
        wvband = [e_(nc.sbuf_tensor(pfx + "wvband0", [128, DCH * 512], BF16))] * 2
        bqc = e_(nc.sbuf_tensor(pfx + "bqc", [128, DCH], F32))
        bkc = e_(nc.sbuf_tensor(pfx + "bkc", [128, DCH], F32))
        bvrep = e_(nc.sbuf_tensor(pfx + "bvrep", [128, D], BF16))
        borep = e_(nc.sbuf_tensor(pfx + "borep", [128, D], F32))
        P = e_(nc.sbuf_tensor(pfx + "P", [128, S], BF16))
        PT = e_(nc.sbuf_tensor(pfx + "PT", [128, S], BF16))
        Z = e_(nc.sbuf_tensor(pfx + "Z", [128, D], BF16))
        zT = e_(nc.sbuf_tensor(pfx + "zT", [128, D], BF16))
        psS = e_(nc.psum_tensor(pfx + "psS", [128, S], F32))
        pstP = e_(nc.psum_tensor(pfx + "pstP", [128, S], BF16))
        psF = e_(nc.psum_tensor(pfx + "psF", [128, D], F32))

        def emit(c):
            c.dy(lambda e: e.dma_start(out=ident[:], in_=ident_h))
            c.dy(lambda e: e.dma_start(out=cmask[:], in_=cmask_h))
            c.dy(lambda e: e.dma_start(out=bqc[:], in_=bqh))
            c.dy(lambda e: e.dma_start(out=bkc[:], in_=bkh))
            c.dg(lambda e: e.dma_start(out=bvrep[:], in_=bcast_row(bvv, D)))
            c.dy(lambda e: e.dma_start(out=borep[:], in_=bcast_row(bov, D)))
            c.dg(lambda e: e.dma_start(
                out=woT[:].rearrange("p (kc n) -> p kc n", kc=DCH), in_=woa))
            scr = {"xc": xc, "sq": sq, "cols": cols}
            lnT3 = lnT[:].rearrange("p (kc t) -> p kc t", kc=DCH)
            qT3 = qT[:].rearrange("p (kc t) -> p kc t", kc=DCH)
            kT3 = kT[:].rearrange("p (kc t) -> p kc t", kc=DCH)
            V3 = V[:].rearrange("p (tt e) -> p tt e", tt=NTS)
            pst1 = pstP[:, 0:D]

            for sq_i in range(2):
                T0 = sq_i * S
                for tl in range(NTS):
                    i = T0 // 128 + tl
                    c.wait("sync", "st", c.now("st"))
                    c.wait("sync", "v", c.now("v"))
                    c.dy(lambda e, i=i, tl=tl: e.dma_start(
                        out=xt[tl % 2][:], in_=x_in[i * 128:(i + 1) * 128, :]))
                    c.wait("vector", "dy")
                    c.wait("vector", "t", c.now("t"))
                    emit_ln(c, xt[tl % 2], scr, ln_bf,
                            lnT3[:, :, tl * 128:(tl + 1) * 128], pst1, ident)
                ln_done_s = c.now("s")
                for nh in range(2):
                    bi = nh % 2
                    c.wait("gpsimd", "t", c.now("t"))
                    c.dg(lambda e, nh=nh, bi=bi: e.dma_start(
                        out=wvband[bi][:].rearrange("p (kc n) -> p kc n", kc=DCH),
                        in_=wva[:, :, nh * 512:(nh + 1) * 512]))
                    c.wait("tensor", "dg")
                    c.wait("tensor", "s", ln_done_s)
                    for tl in range(NTS):
                        c.wait("tensor", "v", c.now("v"))
                        for j in range(DCH):
                            inc = [("t", 1)] if j == DCH - 1 else []
                            c.op("tensor", lambda e, j=j, tl=tl, bi=bi: e.matmul(
                                out=psS[:, 0:512],
                                lhsT=lnT3[:, j, tl * 128:(tl + 1) * 128],
                                rhs=wvband[bi][:, j * 512:(j + 1) * 512],
                                start=(j == 0), stop=(j == DCH - 1)), incs=inc)
                        c.wait("vector", "t")
                        c.wait("vector", "dg")
                        c.v(lambda e, tl=tl, nh=nh: e.tensor_add(
                            out=V3[:, tl, nh * 512:(nh + 1) * 512],
                            in0=psS[:, 0:512], in1=bvrep[:, nh * 512:(nh + 1) * 512]))
                for which, wsrc, dst3, bcol in (
                        (0, wqb, qT3, bqc), (1, wkb, kT3, bkc)):
                    for ech in range(DCH):
                        bi = ech % 2
                        c.wait("gpsimd", "t", c.now("t"))
                        c.dg(lambda e, ech=ech, bi=bi, wsrc=wsrc: e.dma_start(
                            out=wband[bi][:].rearrange("p (kc n) -> p kc n", kc=DCH),
                            in_=wsrc[ech]))
                        c.wait("tensor", "dg")
                        for nh in range(4):
                            c.wait("tensor", "v", c.now("v"))
                            for j in range(DCH):
                                inc = [("t", 1)] if j == DCH - 1 else []
                                c.op("tensor", lambda e, j=j, nh=nh, bi=bi: e.matmul(
                                    out=psS[:, 512:1024],
                                    lhsT=wband[bi][:, j * 128:(j + 1) * 128],
                                    rhs=lnT3[:, j, nh * 512:(nh + 1) * 512],
                                    start=(j == 0), stop=(j == DCH - 1)), incs=inc)
                            c.wait("vector", "t")
                            c.v(lambda e, ech=ech, nh=nh, dst3=dst3, bcol=bcol: e.tensor_scalar_add(
                                out=dst3[:, ech, nh * 512:(nh + 1) * 512],
                                in0=psS[:, 512:1024], scalar1=bcol[:, ech:ech + 1]))
                for tl in range(NTS):
                    TQ = (tl + 1) * 128
                    nkt = tl + 1
                    for h in range(NH1):
                        ech = h // 2
                        po = 64 * (h % 2)
                        c.wait("tensor", "v", c.now("v"))
                        c.wait("tensor", "s", c.now("s"))
                        nch = (TQ + 511) // 512
                        for kc5 in range(nch):
                            t0c = kc5 * 512
                            t1c = min(TQ, t0c + 512)
                            inc = [("t", 1)] if kc5 == nch - 1 else []
                            c.op("tensor", lambda e, t0c=t0c, t1c=t1c, ech=ech, po=po, tl=tl: e.matmul(
                                out=psS[:, t0c:t1c],
                                lhsT=qT3[po:po + 64, ech, tl * 128:(tl + 1) * 128],
                                rhs=kT3[po:po + 64, ech, t0c:t1c],
                                start=True, stop=True), incs=inc)
                        c.wait("vector", "t")
                        c.v(lambda e, tl=tl: e.tensor_add(
                            out=psS[:, tl * 128:(tl + 1) * 128],
                            in0=psS[:, tl * 128:(tl + 1) * 128], in1=cmask[:]))
                        c.v0(lambda e, TQ=TQ: e.reduce_max(out=cols[:, 8:9],
                                                           in_=psS[:, 0:TQ], axis=AX.X))
                        c.vhaz()
                        c.v(lambda e: e.tensor_scalar_mul(out=cols[:, 9:10],
                                                          in0=cols[:, 8:9], scalar1=-1.0 / 8.0))
                        c.wait("scalar", "v")
                        c.shaz()
                        c.s(lambda e, TQ=TQ: e.activation(
                            out=P[:, 0:TQ], in_=psS[:, 0:TQ], func=AF.Exp,
                            scale=1.0 / 8.0, bias=cols[:, 9:10],
                            accum_out=cols[:, 10:11]))
                        c.wait("vector", "s")
                        c.v0(lambda e: e.reciprocal(out=cols[:, 11:12], in_=cols[:, 10:11]))
                        c.vhaz()
                        c.wait("tensor", "s")
                        for kt in range(nkt):
                            inc = [("t", 1)] if kt == nkt - 1 else []
                            c.op("tensor", lambda e, kt=kt: e.transpose(
                                out=pstP[:, kt * 128:(kt + 1) * 128],
                                in_=P[:, kt * 128:(kt + 1) * 128], identity=ident[:]),
                                incs=inc)
                        c.wait("scalar", "t")
                        c.s(lambda e, TQ=TQ: e.copy(out=PT[:, 0:TQ], in_=pstP[:, 0:TQ]))
                        c.wait("tensor", "s")
                        for kt in range(nkt):
                            inc = [("t", 1)] if kt == nkt - 1 else []
                            c.op("tensor", lambda e, kt=kt, h=h: e.matmul(
                                out=psF[:, 0:64],
                                lhsT=PT[:, kt * 128:(kt + 1) * 128],
                                rhs=V3[:, kt, h * 64:(h + 1) * 64],
                                start=(kt == 0), stop=(kt == nkt - 1)), incs=inc)
                        c.wait("vector", "t")
                        c.v(lambda e, h=h: e.tensor_scalar_mul(
                            out=Z[:, h * 64:(h + 1) * 64], in0=psF[:, 0:64],
                            scalar1=cols[:, 11:12]))
                    c.wait("tensor", "v")
                    for j in range(DCH):
                        inc = [("t", 1)] if j == DCH - 1 else []
                        c.op("tensor", lambda e, j=j: e.transpose(
                            out=pstP[:, j * 128:(j + 1) * 128],
                            in_=Z[:, j * 128:(j + 1) * 128], identity=ident[:]), incs=inc)
                    c.wait("scalar", "t")
                    c.s(lambda e: e.copy(out=zT[:], in_=pstP[:, 0:D]))
                    c.wait("tensor", "s", c.now("s"))
                    c.wait("tensor", "dg", 32)
                    for j in range(DCH):
                        for nh in range(2):
                            inc = [("t", 1)] if (j == DCH - 1 and nh == 1) else []
                            c.op("tensor", lambda e, j=j, nh=nh: e.matmul(
                                out=psF[:, nh * 512:(nh + 1) * 512],
                                lhsT=zT[:, j * 128:(j + 1) * 128],
                                rhs=woT[:, j * D + nh * 512:j * D + nh * 512 + 512],
                                start=(j == 0), stop=(j == DCH - 1)), incs=inc)
                    i = T0 // 128 + tl
                    c.wait("sync", "v", c.now("v"))
                    c.wait("sync", "st", c.now("st"))
                    c.dy(lambda e, i=i: e.dma_start(
                        out=xt[0][:], in_=x_in[i * 128:(i + 1) * 128, :]))
                    c.wait("vector", "dy")
                    c.wait("vector", "t")
                    c.v(lambda e: e.tensor_add(out=xc[:], in0=psF[:], in1=borep[:]))
                    c.v(lambda e: e.tensor_add(out=xt[0][:], in0=xc[:], in1=xt[0][:]))
                    c.wait("sync", "v")
                    c.store(lambda e, i=i: e.dma_start(
                        out=x_out[i * 128:(i + 1) * 128, :], in_=xt[0][:]))
            c.wait("sync", "st", c.now("st"))

        if fio is None:
            run_phases(nc, emit)
        else:
            run_stage(nc, ctx, emit)
    return nc



CH2 = 1024
E30 = float(np.exp(-30.0))


def build_l2_attn(fio=None):
    if fio is None:
        nc = _mk()
        ctx = None
        io = None
        pfx = ""
    else:
        nc, ctx, io, pfx = fio
    x_in = nc.dram_tensor("x_in", [TPC, D], F32, kind="ExternalInput").ap() if fio is None else io["x_in"]
    wkbh = nc.dram_tensor("wkbh", [DCH, 128, DCH, 128], F32, kind="ExternalInput").ap() if fio is None else io["wkbh"]
    wvbh = nc.dram_tensor("wvbh", [DCH, 128, DCH, 128], F32, kind="ExternalInput").ap() if fio is None else io["wvbh"]
    wrbh = nc.dram_tensor("wrbh", [DCH, 128, DCH, 128], F32, kind="ExternalInput").ap() if fio is None else io["wrbh"]
    woa = nc.dram_tensor("woa", [128, DCH, D], F32, kind="ExternalInput").ap() if fio is None else io["woa"]
    muk = nc.dram_tensor("muk", [128, DCH], F32, kind="ExternalInput").ap() if fio is None else io["muk"]
    muv = nc.dram_tensor("muv", [128, DCH], F32, kind="ExternalInput").ap() if fio is None else io["muv"]
    mur = nc.dram_tensor("mur", [128, DCH], F32, kind="ExternalInput").ap() if fio is None else io["mur"]
    wdec = nc.dram_tensor("wdec", [128, DCH], F32, kind="ExternalInput").ap() if fio is None else io["wdec"]
    ufir = nc.dram_tensor("ufir", [128, DCH], F32, kind="ExternalInput").ap() if fio is None else io["ufir"]
    ident_h = nc.dram_tensor("ident", [128, 128], BF16, kind="ExternalInput").ap() if fio is None else io["ident"]
    x_out = nc.dram_tensor("x_out", [TPC, D], F32, kind="ExternalOutput").ap() if fio is None else io["x_out"]

    NTT = CH2 // 128

    with contextlib.ExitStack() as stk:
        e_ = stk.enter_context
        xt = [e_(nc.sbuf_tensor(pfx + f"xt{i}", [128, D], F32)) for i in range(2)]
        ident = e_(nc.sbuf_tensor(pfx + "identt", [128, 128], BF16))
        xc = e_(nc.sbuf_tensor(pfx + "xc", [128, D], F32))
        sq = e_(nc.sbuf_tensor(pfx + "sq", [128, D], F32))
        cols = e_(nc.sbuf_tensor(pfx + "cols", [128, 16], F32))
        ln_bf = e_(nc.sbuf_tensor(pfx + "ln_bf", [128, D], BF16))
        lnTe = e_(nc.sbuf_tensor(pfx + "lnTe", [128, DCH * (CH2 + 1)], BF16))
        xkT = e_(nc.sbuf_tensor(pfx + "xkT", [128, DCH * CH2], BF16))
        xvT = e_(nc.sbuf_tensor(pfx + "xvT", [128, DCH * CH2], BF16))
        xrT = e_(nc.sbuf_tensor(pfx + "xrT", [128, DCH * CH2], BF16))
        kTf = e_(nc.sbuf_tensor(pfx + "kTf", [128, DCH * CH2], F32))
        vTf = e_(nc.sbuf_tensor(pfx + "vTf", [128, DCH * CH2], F32))
        rsig = xvT
        ybuf = xkT
        mucol = e_(nc.sbuf_tensor(pfx + "mucol", [128, 3 * DCH], F32))
        wcol = e_(nc.sbuf_tensor(pfx + "wcol", [128, DCH], F32))
        ucol = e_(nc.sbuf_tensor(pfx + "ucol", [128, DCH], F32))
        carry = e_(nc.sbuf_tensor(pfx + "carry", [128, 2 * DCH], F32))
        wrep = e_(nc.sbuf_tensor(pfx + "wrep", [128, CH2], F32))
        p_ext = e_(nc.sbuf_tensor(pfx + "p_ext", [128, CH2 + 1], F32))
        q_ext = e_(nc.sbuf_tensor(pfx + "q_ext", [128, CH2 + 1], F32))
        sa = e_(nc.sbuf_tensor(pfx + "sa", [128, CH2], F32))
        sb = e_(nc.sbuf_tensor(pfx + "sb", [128, CH2], F32))
        sm = e_(nc.sbuf_tensor(pfx + "sm", [128, CH2], F32))
        se1 = e_(nc.sbuf_tensor(pfx + "se1", [128, CH2], F32))
        snum = e_(nc.sbuf_tensor(pfx + "snum", [128, CH2], F32))
        st5 = e_(nc.sbuf_tensor(pfx + "st5", [128, CH2], F32))
        wband = [e_(nc.sbuf_tensor(pfx + "wband0", [128, DCH * 128], BF16))] * 2
        woT = e_(nc.sbuf_tensor(pfx + "woT", [128, DCH * D], BF16))
        pst1 = e_(nc.psum_tensor(pfx + "pst1", [128, D], BF16))
        psP = e_(nc.psum_tensor(pfx + "psP", [128, 512], F32))
        psF = e_(nc.psum_tensor(pfx + "psF", [128, D], F32))

        def emit(c):
            c.dy(lambda e: e.dma_start(out=ident[:], in_=ident_h))
            c.dy(lambda e: e.dma_start(out=mucol[:, 0:DCH], in_=muk))
            c.dy(lambda e: e.dma_start(out=mucol[:, DCH:2 * DCH], in_=muv))
            c.dy(lambda e: e.dma_start(out=mucol[:, 2 * DCH:3 * DCH], in_=mur))
            c.dy(lambda e: e.dma_start(out=wcol[:], in_=wdec))
            c.dy(lambda e: e.dma_start(out=ucol[:], in_=ufir))
            c.dg(lambda e: e.dma_start(
                out=woT[:].rearrange("p (kc n) -> p kc n", kc=DCH), in_=woa))
            scr = {"xc": xc, "sq": sq, "cols": cols}
            lnTe3 = lnTe[:].rearrange("p (kc t) -> p kc t", kc=DCH)
            xkT3 = xkT[:].rearrange("p (kc t) -> p kc t", kc=DCH)
            xvT3 = xvT[:].rearrange("p (kc t) -> p kc t", kc=DCH)
            xrT3 = xrT[:].rearrange("p (kc t) -> p kc t", kc=DCH)
            kTf3 = kTf[:].rearrange("p (kc t) -> p kc t", kc=DCH)
            vTf3 = vTf[:].rearrange("p (kc t) -> p kc t", kc=DCH)
            rs3 = rsig[:].rearrange("p (kc t) -> p kc t", kc=DCH)
            yb3 = ybuf[:].rearrange("p (kc t) -> p kc t", kc=DCH)

            for sq_i in range(2):
                for ch in range(S // CH2):
                    T0 = sq_i * S + ch * CH2
                    if ch == 0:
                        c.wait("vector", "s", c.now("s"))
                        c.wait("vector", "t", c.now("t"))
                        c.v(lambda e: e.memset(lnTe3[:, :, 0:1], 0.0))
                    else:
                        c.wait("vector", "t", c.now("t"))
                        c.v(lambda e: e.tensor_copy(out=lnTe3[:, :, 0:1],
                                                    in_=lnTe3[:, :, CH2:CH2 + 1]))
                    for tl in range(NTT):
                        i = T0 // 128 + tl
                        c.wait("sync", "st", c.now("st"))
                        c.wait("sync", "v", c.now("v"))
                        c.dy(lambda e, i=i, tl=tl: e.dma_start(
                            out=xt[tl % 2][:], in_=x_in[i * 128:(i + 1) * 128, :]))
                        c.wait("vector", "dy")
                        c.wait("vector", "t", c.now("t"))
                        emit_ln(c, xt[tl % 2], scr, ln_bf,
                                lnTe3[:, :, 1 + tl * 128:1 + (tl + 1) * 128], pst1, ident)
                    c.wait("vector", "s")
                    for kc in range(DCH):
                        for which, dst3, mof in ((0, xkT3, 0), (1, xvT3, DCH), (2, xrT3, 2 * DCH)):
                            c.v0(lambda e, kc=kc, dst3=dst3, mof=mof: e.tensor_scalar_mul(
                                out=dst3[:, kc, :], in0=lnTe3[:, kc, 1:CH2 + 1],
                                scalar1=mucol[:, mof + kc:mof + kc + 1]))
                    c.vhaz()
                    for kc in range(DCH):
                        for which, dst3, mof in ((0, xkT3, 0), (1, xvT3, DCH), (2, xrT3, 2 * DCH)):
                            c.v0(lambda e, kc=kc, dst3=dst3, mof=mof: e.scalar_tensor_tensor(
                                out=st5[:], in0=lnTe3[:, kc, 0:CH2],
                                scalar=mucol[:, mof + kc:mof + kc + 1],
                                in1=lnTe3[:, kc, 0:CH2], op0=ALU.mult, op1=ALU.subtract))
                            c.v0(lambda e, kc=kc, dst3=dst3: e.tensor_sub(
                                out=dst3[:, kc, :], in0=dst3[:, kc, :], in1=st5[:]))
                    c.op("vector", lambda e: e.tensor_copy(out=cols[:, 15:16], in_=cols[:, 15:16]),
                         incs=[("v", 1)])
                    mix_v = c.now("v")
                    for which, wsrc, ev in ((0, wkbh, 0), (1, wvbh, 1), (2, wrbh, 2)):
                        for ech in range(DCH):
                            bi = 0
                            c.wait("gpsimd", "t", c.now("t"))
                            c.dg(lambda e, ech=ech, wsrc=wsrc: e.dma_start(
                                out=wband[0][:].rearrange("p (kc n) -> p kc n", kc=DCH),
                                in_=wsrc[ech]))
                            src3 = (xkT3, xvT3, xrT3)[which]
                            for nh in range(2):
                                c.wait("tensor", "dg")
                                c.wait("tensor", "v", c.now("v"))
                                c.wait("tensor", "s", c.now("s"))
                                for j in range(DCH):
                                    inc = [("t", 1)] if j == DCH - 1 else []
                                    c.op("tensor", lambda e, j=j, nh=nh, src3=src3: e.matmul(
                                        out=psP[:],
                                        lhsT=wband[0][:, j * 128:(j + 1) * 128],
                                        rhs=src3[:, j, nh * 512:(nh + 1) * 512],
                                        start=(j == 0), stop=(j == DCH - 1)), incs=inc)
                                if which == 0:
                                    c.wait("vector", "t")
                                    c.v(lambda e, ech=ech, nh=nh: e.tensor_copy(
                                        out=kTf3[:, ech, nh * 512:(nh + 1) * 512], in_=psP[:]))
                                elif which == 1:
                                    c.wait("vector", "t")
                                    c.v(lambda e, ech=ech, nh=nh: e.tensor_copy(
                                        out=vTf3[:, ech, nh * 512:(nh + 1) * 512], in_=psP[:]))
                                else:
                                    c.wait("scalar", "t")
                                    c.s(lambda e, ech=ech, nh=nh: e.activation(
                                        out=rs3[:, ech, nh * 512:(nh + 1) * 512], in_=psP[:],
                                        func=AF.Sigmoid))
                    for ech in range(DCH):
                        kv = kTf3[:, ech, :]
                        vv = vTf3[:, ech, :]
                        c.wait("vector", "s", c.now("s"))
                        c.v0(lambda e: e.memset(wrep[:], 0.0))
                        c.vhaz()
                        c.v0(lambda e, ech=ech: e.tensor_scalar_add(
                            out=wrep[:], in0=wrep[:], scalar1=wcol[:, ech:ech + 1]))
                        if ch == 0:
                            c.v0(lambda e, ech=ech: e.memset(carry[:, ech:ech + 1], -1e30))
                            c.v0(lambda e, ech=ech: e.memset(carry[:, DCH + ech:DCH + ech + 1], 0.0))
                        c.vhaz()
                        c.v0(lambda e, kv=kv, ech=ech: e.tensor_tensor_scan(
                            out=p_ext[:, 1:CH2 + 1], data0=wrep[:], data1=kv,
                            initial=carry[:, ech:ech + 1], op0=ALU.add, op1=ALU.max))
                        c.vhaz()
                        c.v0(lambda e, ech=ech: e.tensor_copy(out=p_ext[:, 0:1],
                                                              in_=carry[:, ech:ech + 1]))
                        c.vhaz()
                        c.v0(lambda e: e.tensor_sub(out=sa[:], in0=p_ext[:, 0:CH2],
                                                    in1=p_ext[:, 1:CH2 + 1]))
                        c.v0(lambda e, ech=ech: e.tensor_scalar(
                            out=sa[:], in0=sa[:], scalar1=wcol[:, ech:ech + 1],
                            scalar2=-30.0, op0=ALU.add, op1=ALU.max))
                        c.wait("scalar", "v")
                        c.s(lambda e: e.activation(out=sa[:], in_=sa[:], func=AF.Exp))
                        c.wait("vector", "s")
                        c.v0(lambda e, kv=kv: e.tensor_sub(out=sb[:], in0=kv,
                                                           in1=p_ext[:, 1:CH2 + 1]))
                        c.v0(lambda e: e.tensor_scalar_max(out=sb[:], in0=sb[:], scalar1=-30.0))
                        c.wait("scalar", "v")
                        c.s(lambda e: e.activation(out=sb[:], in_=sb[:], func=AF.Exp))
                        c.wait("vector", "s")
                        c.v0(lambda e, vv=vv: e.tensor_mul(out=sb[:], in0=sb[:], in1=vv))
                        c.vhaz()
                        c.v0(lambda e, ech=ech: e.tensor_tensor_scan(
                            out=q_ext[:, 1:CH2 + 1], data0=sa[:], data1=sb[:],
                            initial=carry[:, DCH + ech:DCH + ech + 1], op0=ALU.mult, op1=ALU.add))
                        c.vhaz()
                        c.v0(lambda e, ech=ech: e.tensor_copy(out=q_ext[:, 0:1],
                                                              in_=carry[:, DCH + ech:DCH + ech + 1]))
                        c.vhaz()
                        c.v0(lambda e, ech=ech: e.tensor_copy(
                            out=carry[:, ech:ech + 1], in_=p_ext[:, CH2:CH2 + 1]))
                        c.v0(lambda e, ech=ech: e.tensor_copy(
                            out=carry[:, DCH + ech:DCH + ech + 1], in_=q_ext[:, CH2:CH2 + 1]))
                        c.v0(lambda e, kv=kv, ech=ech: e.tensor_scalar_add(
                            out=sm[:], in0=kv, scalar1=ucol[:, ech:ech + 1]))
                        c.v0(lambda e: e.tensor_max(out=sm[:], in0=sm[:], in1=p_ext[:, 0:CH2]))
                        c.v0(lambda e: e.tensor_sub(out=se1[:], in0=p_ext[:, 0:CH2], in1=sm[:]))
                        c.v0(lambda e: e.tensor_scalar_max(out=se1[:], in0=se1[:], scalar1=-30.0))
                        c.wait("scalar", "v")
                        c.s(lambda e: e.activation(out=se1[:], in_=se1[:], func=AF.Exp))
                        c.v0(lambda e, kv=kv, ech=ech: e.tensor_scalar_add(
                            out=st5[:], in0=kv, scalar1=ucol[:, ech:ech + 1]))
                        c.v0(lambda e: e.tensor_sub(out=st5[:], in0=st5[:], in1=sm[:]))
                        c.v0(lambda e: e.tensor_scalar_max(out=st5[:], in0=st5[:], scalar1=-30.0))
                        c.wait("scalar", "v")
                        c.s(lambda e: e.activation(out=st5[:], in_=st5[:], func=AF.Exp))
                        c.wait("vector", "s")
                        c.v0(lambda e, vv=vv: e.tensor_mul(out=snum[:], in0=st5[:], in1=vv))
                        c.v0(lambda e: e.tensor_mul(out=sm[:], in0=se1[:], in1=q_ext[:, 0:CH2]))
                        c.v0(lambda e: e.tensor_add(out=snum[:], in0=snum[:], in1=sm[:]))
                        c.v0(lambda e: e.tensor_add(out=se1[:], in0=se1[:], in1=st5[:]))
                        c.v0(lambda e: e.tensor_scalar_add(out=se1[:], in0=se1[:], scalar1=1e-9))
                        c.wait("scalar", "v")
                        c.s(lambda e: e.activation(out=se1[:], in_=se1[:], func=AF.Ln))
                        c.s(lambda e: e.activation(out=se1[:], in_=se1[:], func=AF.Exp, scale=-1.0))
                        c.wait("vector", "s")
                        c.v0(lambda e: e.tensor_mul(out=snum[:], in0=snum[:], in1=se1[:]))
                        c.v(lambda e, ech=ech: e.tensor_mul(
                            out=yb3[:, ech, :], in0=snum[:], in1=rs3[:, ech, :]))
                    y_v = c.now("v")
                    c.wait("tensor", "dg", 16)
                    for tl in range(NTT):
                        i = T0 // 128 + tl
                        c.wait("tensor", "v", c.now("v"))
                        for j in range(DCH):
                            for nh in range(2):
                                inc = [("t", 1)] if (j == DCH - 1 and nh == 1) else []
                                c.op("tensor", lambda e, j=j, nh=nh, tl=tl: e.matmul(
                                    out=psF[:, nh * 512:(nh + 1) * 512],
                                    lhsT=yb3[:, j, tl * 128:(tl + 1) * 128],
                                    rhs=woT[:, j * D + nh * 512:j * D + nh * 512 + 512],
                                    start=(j == 0), stop=(j == DCH - 1)), incs=inc)
                        c.wait("sync", "v", c.now("v"))
                        c.wait("sync", "st", c.now("st"))
                        c.dy(lambda e, i=i: e.dma_start(
                            out=xt[0][:], in_=x_in[i * 128:(i + 1) * 128, :]))
                        c.wait("vector", "dy")
                        c.wait("vector", "t")
                        c.v(lambda e: e.tensor_add(out=xt[0][:], in0=psF[:], in1=xt[0][:]))
                        c.wait("sync", "v")
                        c.store(lambda e, i=i: e.dma_start(
                            out=x_out[i * 128:(i + 1) * 128, :], in_=xt[0][:]))
            c.wait("sync", "st", c.now("st"))

        if fio is None:
            run_phases(nc, emit)
        else:
            run_stage(nc, ctx, emit)
    return nc



NEXP = 8
NTT_M = 4
MST = 512
NMST = TPC // MST


def build_l1_ffn(fio=None):
    if fio is None:
        nc = _mk()
        ctx = None
        io = None
        pfx = ""
    else:
        nc, ctx, io, pfx = fio
    x_in = nc.dram_tensor("x_in", [TPC, D], F32, kind="ExternalInput").ap() if fio is None else io["x_in"]
    w1b = nc.dram_tensor("w1b", [NEXP, 128, HCH, DCH, 128], F32, kind="ExternalInput").ap() if fio is None else io["w1b"]
    w2a = nc.dram_tensor("w2a", [NEXP, 128, HCH, D], F32, kind="ExternalInput").ap() if fio is None else io["w2a"]
    wrt = nc.dram_tensor("wrt", [128, DCH, NEXP], F32, kind="ExternalInput").ap() if fio is None else io["wrt"]
    b1h = nc.dram_tensor("b1h", [128, NEXP * HCH], F32, kind="ExternalInput").ap() if fio is None else io["b1h"]
    b2h = nc.dram_tensor("b2h", [NEXP, D], F32, kind="ExternalInput").ap() if fio is None else io["b2h"]
    ident_h = nc.dram_tensor("ident", [128, 128], BF16, kind="ExternalInput").ap() if fio is None else io["ident"]
    x_out = nc.dram_tensor("x_out", [TPC, D], F32, kind="ExternalOutput").ap() if fio is None else io["x_out"]
    ln_scr = nc.dram_tensor(pfx + "lnscr", [D, TPC], BF16, kind="Internal").ap()
    acc_scr = nc.dram_tensor(pfx + "accscr", [TPC, D], F32, kind="Internal").ap()

    TS = 512
    NTS_ = NT // 4

    with contextlib.ExitStack() as stk:
        e_ = stk.enter_context
        xt = [e_(nc.sbuf_tensor(pfx + f"xt{i}", [128, D], F32)) for i in range(2)]
        ident = e_(nc.sbuf_tensor(pfx + "identt", [128, 128], BF16))
        xc = e_(nc.sbuf_tensor(pfx + "xc", [128, D], F32))
        sq = e_(nc.sbuf_tensor(pfx + "sq", [128, D], F32))
        cols = e_(nc.sbuf_tensor(pfx + "cols", [128, 32], F32))
        ln_bf = e_(nc.sbuf_tensor(pfx + "ln_bf", [128, D], BF16))
        ln_f = e_(nc.sbuf_tensor(pfx + "ln_f", [128, D], F32))
        lo_bf = e_(nc.sbuf_tensor(pfx + "lo_bf", [128, D], BF16))
        lnT = e_(nc.sbuf_tensor(pfx + "lnT", [128, D], BF16))
        loT = e_(nc.sbuf_tensor(pfx + "loT", [128, D], BF16))
        lnsl = e_(nc.sbuf_tensor(pfx + "lnsl", [128, DCH * TS], BF16))
        w1sb = e_(nc.sbuf_tensor(pfx + "w1sb", [128, HCH * DCH * 128], BF16))
        w2sb = e_(nc.sbuf_tensor(pfx + "w2sb", [128, HCH * D], BF16))
        hT = e_(nc.sbuf_tensor(pfx + "hT", [128, HCH * TS], BF16))
        gate = e_(nc.sbuf_tensor(pfx + "gate", [128, NT * NEXP], F32))
        wrf = e_(nc.sbuf_tensor(pfx + "wrf", [128, DCH * NEXP], F32))
        wrhi = e_(nc.sbuf_tensor(pfx + "wrhi", [128, DCH * NEXP], BF16))
        wrlo = e_(nc.sbuf_tensor(pfx + "wrlo", [128, DCH * NEXP], BF16))
        wrtmp = e_(nc.sbuf_tensor(pfx + "wrtmp", [128, DCH * NEXP], F32))
        b1c = e_(nc.sbuf_tensor(pfx + "b1c", [128, NEXP * HCH], F32))
        b2rep = e_(nc.sbuf_tensor(pfx + "b2rep", [128, D], F32))
        zero8 = e_(nc.sbuf_tensor(pfx + "zero8", [128, NEXP], F32))
        r1 = e_(nc.sbuf_tensor(pfx + "r1", [128, NEXP], F32))
        r2 = e_(nc.sbuf_tensor(pfx + "r2", [128, NEXP], F32))
        acct = [xc, sq]
        outw = ln_f
        pst1 = e_(nc.psum_tensor(pfx + "pst1", [128, D], BF16))
        psH = [e_(nc.psum_tensor(pfx + f"psH{i}", [128, TS], F32)) for i in range(2)]
        psF = [e_(nc.psum_tensor(pfx + f"psF{i}", [128, D], F32)) for i in range(2)]
        psR = e_(nc.psum_tensor(pfx + "psR", [128, NEXP], F32))

        def emit(c):
            c.dy(lambda e: e.dma_start(out=ident[:], in_=ident_h))
            c.dy(lambda e: e.dma_start(out=b1c[:], in_=b1h))
            c.dy(lambda e: e.dma_start(
                out=wrf[:].rearrange("p (kc n) -> p kc n", kc=DCH), in_=wrt))
            c.wait("vector", "dy")
            c.v(lambda e: e.tensor_copy(out=wrhi[:], in_=wrf[:]))
            c.v(lambda e: e.tensor_copy(out=wrtmp[:], in_=wrhi[:]))
            c.v(lambda e: e.tensor_sub(out=wrtmp[:], in0=wrf[:], in1=wrtmp[:]))
            c.v(lambda e: e.tensor_copy(out=wrlo[:], in_=wrtmp[:]))
            c.v(lambda e: e.memset(zero8[:], 0.0))
            scr = {"xc": xc, "sq": sq, "cols": cols}
            lnT3 = lnT[:].rearrange("p (kc t) -> p kc t", kc=DCH)
            loT3 = loT[:].rearrange("p (kc t) -> p kc t", kc=DCH)
            ln_scr3 = ln_scr.rearrange("(kc p) t -> p kc t", p=128)
            lnsl3 = lnsl[:].rearrange("p (kc t) -> p kc t", kc=DCH)
            hT3 = hT[:].rearrange("p (kc t) -> p kc t", kc=HCH)
            w1s4 = w1sb[:].rearrange("p (o kc n) -> p o kc n", o=HCH, kc=DCH)

            for i in range(NT):
                c.wait("sync", "st", c.now("st"))
                c.wait("sync", "v", c.now("v"))
                c.dy(lambda e, i=i: e.dma_start(
                    out=xt[i % 2][:], in_=x_in[i * 128:(i + 1) * 128, :]))
                c.wait("vector", "dy")
                c.wait("vector", "t", c.now("t"))
                c.wait("vector", "st", c.now("st"))
                emit_ln(c, xt[i % 2], scr, ln_bf, lnT3[:, :, :], pst1, ident, ln_f32=ln_f)
                c.v(lambda e: e.tensor_copy(out=xc[:], in_=ln_bf[:]))
                c.v(lambda e: e.tensor_sub(out=xc[:], in0=ln_f[:], in1=xc[:]))
                c.v(lambda e: e.tensor_copy(out=lo_bf[:], in_=xc[:]))
                c.wait("tensor", "v")
                c.wait("tensor", "s")
                for j in range(DCH):
                    inc = [("t", 1)] if j == DCH - 1 else []
                    c.op("tensor", lambda e, j=j: e.transpose(
                        out=pst1[:, j * 128:(j + 1) * 128],
                        in_=lo_bf[:, j * 128:(j + 1) * 128], identity=ident[:]), incs=inc)
                c.wait("scalar", "t")
                c.s(lambda e: e.copy(out=loT3[:, :, :],
                                     in_=pst1[:].rearrange("p (kc t) -> p kc t", kc=DCH)))
                c.wait("sync", "s", c.now("s"))
                c.store(lambda e, i=i: e.dma_start(
                    out=ln_scr3[:, :, i * 128:(i + 1) * 128], in_=lnT3[:, :, :]))
                c.wait("tensor", "s")
                for pi, (lt, wt_) in enumerate(((lnT3, wrhi), (loT3, wrhi), (lnT3, wrlo))):
                    for j in range(DCH):
                        inc = [("t", 1)] if (pi == 2 and j == DCH - 1) else []
                        c.op("tensor", lambda e, j=j, lt=lt, wt_=wt_: e.matmul(
                            out=psR[:], lhsT=lt[:, j, :],
                            rhs=wt_[:, j * NEXP:(j + 1) * NEXP],
                            start=(pi == 0 and j == 0), stop=(pi == 2 and j == DCH - 1)),
                            incs=inc)
                c.wait("vector", "t")
                lg = sq[:, 0:NEXP]
                c.v(lambda e, lg=lg: e.tensor_copy(out=lg, in_=psR[:]))
                c.v(lambda e, lg=lg: e.reduce_max(out=cols[:, 16:17], in_=lg, axis=AX.X))
                c.v(lambda e, lg=lg: e.tensor_scalar(out=r1[:], in0=lg,
                                                     scalar1=cols[:, 16:17], scalar2=None,
                                                     op0=ALU.is_equal))
                c.v(lambda e: e.tensor_tensor_scan(out=r2[:], data0=r1[:], data1=zero8[:],
                                                   initial=0.0, op0=ALU.add, op1=ALU.max))
                c.v(lambda e: e.tensor_single_scalar(out=r2[:], in_=r2[:], scalar=1.0,
                                                     op=ALU.is_equal))
                c.v(lambda e: e.tensor_mul(out=r1[:], in0=r1[:], in1=r2[:]))
                c.v(lambda e, lg=lg: e.scalar_tensor_tensor(
                    out=r2[:], in0=r1[:], scalar=-1e30, in1=lg, op0=ALU.mult, op1=ALU.add))
                c.v(lambda e: e.reduce_max(out=cols[:, 17:18], in_=r2[:], axis=AX.X))
                c.v(lambda e: e.tensor_scalar(out=r2[:], in0=r2[:],
                                              scalar1=cols[:, 17:18], scalar2=None,
                                              op0=ALU.is_equal))
                c.v(lambda e: e.tensor_tensor_scan(out=sq[:, 8:16], data0=r2[:],
                                                   data1=zero8[:], initial=0.0,
                                                   op0=ALU.add, op1=ALU.max))
                c.v(lambda e: e.tensor_single_scalar(out=sq[:, 8:16], in_=sq[:, 8:16],
                                                     scalar=1.0, op=ALU.is_equal))
                c.v(lambda e: e.tensor_mul(out=r2[:], in0=r2[:], in1=sq[:, 8:16]))
                c.v(lambda e: e.tensor_sub(out=cols[:, 18:19], in0=cols[:, 16:17],
                                           in1=cols[:, 17:18]))
                c.wait("scalar", "v")
                c.s(lambda e: e.activation(out=cols[:, 19:20], in_=cols[:, 18:19],
                                           func=AF.Sigmoid))
                c.wait("vector", "s")
                c.v(lambda e: e.tensor_scalar(out=cols[:, 20:21], in0=cols[:, 19:20],
                                              scalar1=-1.0, scalar2=1.0,
                                              op0=ALU.mult, op1=ALU.add))
                c.v(lambda e, i=i: e.tensor_scalar_mul(
                    out=gate[:, i * NEXP:(i + 1) * NEXP], in0=r1[:],
                    scalar1=cols[:, 19:20]))
                c.v(lambda e, i=i: e.scalar_tensor_tensor(
                    out=gate[:, i * NEXP:(i + 1) * NEXP], in0=r2[:],
                    scalar=cols[:, 20:21], in1=gate[:, i * NEXP:(i + 1) * NEXP],
                    op0=ALU.mult, op1=ALU.add))

            for ex in range(NEXP):
                c.wait("gpsimd", "t", c.now("t"))
                c.dg(lambda e, ex=ex: e.dma_start(
                    out=w1s4[:, :, :, :], in_=w1b[ex]))
                c.dg(lambda e, ex=ex: e.dma_start(
                    out=w2sb[:].rearrange("p (kc n) -> p kc n", kc=HCH), in_=w2a[ex]))
                c.wait("gpsimd", "v", c.now("v"))
                c.dg(lambda e, ex=ex: e.dma_start(out=b2rep[:], in_=bcast_row(b2h[ex], D)))
                for ts in range(NTS_):
                    c.wait("sync", "t", c.now("t"))
                    c.dy(lambda e, ts=ts: e.dma_start(
                        out=lnsl3[:, :, :], in_=ln_scr3[:, :, ts * TS:(ts + 1) * TS]))
                    hmark = {}
                    for och in range(HCH):
                        bi = och % 2
                        c.wait("tensor", "dg")
                        c.wait("tensor", "dy")
                        c.wait("tensor", "s", hmark.get(och - 2, 0))
                        for j in range(DCH):
                            inc = [("t", 1)] if j == DCH - 1 else []
                            c.op("tensor", lambda e, j=j, bi=bi, och=och: e.matmul(
                                out=psH[bi][:], lhsT=w1s4[:, och, j, :],
                                rhs=lnsl3[:, j, :],
                                start=(j == 0), stop=(j == DCH - 1)), incs=inc)
                        c.wait("scalar", "t")
                        c.s(lambda e, och=och, ex=ex, bi=bi: e.activation(
                            out=hT3[:, och, :], in_=psH[bi][:], func=AF.Gelu,
                            bias=b1c[:, ex * HCH + och:ex * HCH + och + 1]))
                        hmark[och] = c.now("s")
                    h_s = c.now("s")
                    for tl in range(4):
                        gi = ts * 4 + tl
                        bi = tl % 2
                        if ex > 0:
                            c.wait("sync", "v", c.now("v"))
                            c.dy(lambda e, gi=gi, bi=bi: e.dma_start(
                                out=acct[bi][:], in_=acc_scr[gi * 128:(gi + 1) * 128, :]))
                        c.wait("tensor", "s", h_s)
                        c.wait("tensor", "v", c.now("v"))
                        for j in range(HCH):
                            for nh in range(2):
                                inc = [("t", 1)] if (j == HCH - 1 and nh == 1) else []
                                c.op("tensor", lambda e, j=j, nh=nh, tl=tl, bi=bi: e.matmul(
                                    out=psF[bi][:, nh * 512:(nh + 1) * 512],
                                    lhsT=hT3[:, j, tl * 128:(tl + 1) * 128],
                                    rhs=w2sb[:, j * D + nh * 512:j * D + nh * 512 + 512],
                                    start=(j == 0), stop=(j == HCH - 1)), incs=inc)
                        c.wait("vector", "t")
                        c.wait("vector", "dg")
                        c.wait("vector", "dy")
                        c.wait("vector", "st", c.now("st"))
                        c.v(lambda e, bi=bi: e.tensor_add(out=outw[:], in0=psF[bi][:],
                                                          in1=b2rep[:]))
                        if ex > 0:
                            c.v(lambda e, gi=gi, ex=ex, bi=bi: e.scalar_tensor_tensor(
                                out=acct[bi][:], in0=outw[:],
                                scalar=gate[:, gi * NEXP + ex:gi * NEXP + ex + 1],
                                in1=acct[bi][:], op0=ALU.mult, op1=ALU.add))
                        else:
                            c.v(lambda e, gi=gi, ex=ex, bi=bi: e.tensor_scalar_mul(
                                out=acct[bi][:], in0=outw[:],
                                scalar1=gate[:, gi * NEXP + ex:gi * NEXP + ex + 1]))
                        c.wait("sync", "v")
                        c.store(lambda e, gi=gi, bi=bi: e.dma_start(
                            out=acc_scr[gi * 128:(gi + 1) * 128, :], in_=acct[bi][:]))
            for i in range(NT):
                c.wait("sync", "st", c.now("st"))
                c.wait("sync", "v", c.now("v"))
                c.dy(lambda e, i=i: e.dma_start(
                    out=xt[0][:], in_=x_in[i * 128:(i + 1) * 128, :]))
                c.dy(lambda e, i=i: e.dma_start(
                    out=xt[1][:], in_=acc_scr[i * 128:(i + 1) * 128, :]))
                c.wait("vector", "dy")
                c.v(lambda e: e.scalar_tensor_tensor(
                    out=xt[0][:], in0=xt[1][:], scalar=0.5, in1=xt[0][:],
                    op0=ALU.mult, op1=ALU.add))
                c.wait("sync", "v")
                c.store(lambda e, i=i: e.dma_start(
                    out=x_out[i * 128:(i + 1) * 128, :], in_=xt[0][:]))
            c.wait("sync", "st", c.now("st"))

        if fio is None:
            run_phases(nc, emit)
        else:
            run_stage(nc, ctx, emit)
    return nc



CAP2 = int(1.25 * B * S / NEXP)


def build_l2_ffn_route(fio=None):
    if fio is None:
        nc = _mk()
        ctx = None
        io = None
        pfx = ""
    else:
        nc, ctx, io, pfx = fio
    x_in = nc.dram_tensor("x_in", [TPC, D], F32, kind="ExternalInput").ap() if fio is None else io["x_in"]
    wrt = nc.dram_tensor("wrt", [128, DCH, NEXP], F32, kind="ExternalInput").ap() if fio is None else io["wrt"]
    ident_h = nc.dram_tensor("ident", [128, 128], BF16, kind="ExternalInput").ap() if fio is None else io["ident"]
    triu_h = nc.dram_tensor("triu", [128, 128], F32, kind="ExternalInput").ap() if fio is None else io["triu"]
    oh_out = nc.dram_tensor("oh_out", [TPC, NEXP], F32, kind="ExternalOutput").ap() if fio is None else io["oh_out"]
    pm_out = nc.dram_tensor("pm_out", [TPC, 1], F32, kind="ExternalOutput").ap() if fio is None else io["pm_out"]
    pos_out = nc.dram_tensor("pos_out", [TPC, 1], F32, kind="ExternalOutput").ap() if fio is None else io["pos_out"]
    cnt_out = nc.dram_tensor("cnt_out", [1, NEXP], F32, kind="ExternalOutput").ap() if fio is None else io["cnt_out"]
    scr_dram = nc.dram_tensor(pfx + "scr", [16], F32, kind="Internal").ap()

    with contextlib.ExitStack() as stk:
        e_ = stk.enter_context
        xt = [e_(nc.sbuf_tensor(pfx + f"xt{i}", [128, D], F32)) for i in range(2)]
        ident = e_(nc.sbuf_tensor(pfx + "identt", [128, 128], BF16))
        triu = e_(nc.sbuf_tensor(pfx + "triut", [128, 128], F32))
        xc = e_(nc.sbuf_tensor(pfx + "xc", [128, D], F32))
        sq = e_(nc.sbuf_tensor(pfx + "sq", [128, D], F32))
        cols = e_(nc.sbuf_tensor(pfx + "cols", [128, 32], F32))
        ln_bf = e_(nc.sbuf_tensor(pfx + "ln_bf", [128, D], BF16))
        ln_f = e_(nc.sbuf_tensor(pfx + "ln_f", [128, D], F32))
        lo_bf = e_(nc.sbuf_tensor(pfx + "lo_bf", [128, D], BF16))
        lnT = e_(nc.sbuf_tensor(pfx + "lnT", [128, D], BF16))
        loT = e_(nc.sbuf_tensor(pfx + "loT", [128, D], BF16))
        wrf = e_(nc.sbuf_tensor(pfx + "wrf", [128, DCH * NEXP], F32))
        wrhi = e_(nc.sbuf_tensor(pfx + "wrhi", [128, DCH * NEXP], BF16))
        wrlo = e_(nc.sbuf_tensor(pfx + "wrlo", [128, DCH * NEXP], BF16))
        wrtmp = e_(nc.sbuf_tensor(pfx + "wrtmp", [128, DCH * NEXP], F32))
        lg = e_(nc.sbuf_tensor(pfx + "lg", [128, NEXP], F32))
        r1 = e_(nc.sbuf_tensor(pfx + "r1", [128, NEXP], F32))
        r2 = e_(nc.sbuf_tensor(pfx + "r2", [128, NEXP], F32))
        zero8 = e_(nc.sbuf_tensor(pfx + "zero8", [128, NEXP], F32))
        pref = e_(nc.sbuf_tensor(pfx + "pref", [128, NEXP], F32))
        brep = e_(nc.sbuf_tensor(pfx + "brep", [128, NEXP], F32))
        btot = e_(nc.sbuf_tensor(pfx + "btot", [1, NEXP], F32))
        ctot = e_(nc.sbuf_tensor(pfx + "ctot", [1, NEXP], F32))
        onesc = e_(nc.sbuf_tensor(pfx + "onesc", [128, 1], F32))
        pst1 = e_(nc.psum_tensor(pfx + "pst1", [128, D], BF16))
        psR = e_(nc.psum_tensor(pfx + "psR", [128, NEXP], F32))
        psPr = e_(nc.psum_tensor(pfx + "psPr", [128, NEXP], F32))
        psT = e_(nc.psum_tensor(pfx + "psT", [1, NEXP], F32))

        def emit(c):
            c.dy(lambda e: e.dma_start(out=ident[:], in_=ident_h))
            c.dy(lambda e: e.dma_start(out=triu[:], in_=triu_h))
            c.dy(lambda e: e.dma_start(
                out=wrf[:].rearrange("p (kc n) -> p kc n", kc=DCH), in_=wrt))
            c.wait("vector", "dy")
            c.v(lambda e: e.tensor_copy(out=wrhi[:], in_=wrf[:]))
            c.v(lambda e: e.tensor_copy(out=wrtmp[:], in_=wrhi[:]))
            c.v(lambda e: e.tensor_sub(out=wrtmp[:], in0=wrf[:], in1=wrtmp[:]))
            c.v(lambda e: e.tensor_copy(out=wrlo[:], in_=wrtmp[:]))
            c.v(lambda e: e.memset(zero8[:], 0.0))
            c.v(lambda e: e.memset(brep[:], 0.0))
            c.v(lambda e: e.memset(ctot[:], 0.0))
            c.v(lambda e: e.memset(onesc[:], 1.0))
            scr = {"xc": xc, "sq": sq, "cols": cols}
            lnT3 = lnT[:].rearrange("p (kc t) -> p kc t", kc=DCH)
            loT3 = loT[:].rearrange("p (kc t) -> p kc t", kc=DCH)

            for i in range(NT):
                c.wait("sync", "st", c.now("st"))
                c.wait("sync", "v", c.now("v"))
                c.dy(lambda e, i=i: e.dma_start(
                    out=xt[i % 2][:], in_=x_in[i * 128:(i + 1) * 128, :]))
                c.wait("vector", "dy")
                c.wait("vector", "t", c.now("t"))
                emit_ln(c, xt[i % 2], scr, ln_bf, lnT3[:, :, :], pst1, ident,
                        ln_f32=ln_f)
                c.v(lambda e: e.tensor_copy(out=xc[:], in_=ln_bf[:]))
                c.v(lambda e: e.tensor_sub(out=xc[:], in0=ln_f[:], in1=xc[:]))
                c.v(lambda e: e.tensor_copy(out=lo_bf[:], in_=xc[:]))
                c.wait("tensor", "v")
                c.wait("tensor", "s")
                for j in range(DCH):
                    inc = [("t", 1)] if j == DCH - 1 else []
                    c.op("tensor", lambda e, j=j: e.transpose(
                        out=pst1[:, j * 128:(j + 1) * 128],
                        in_=lo_bf[:, j * 128:(j + 1) * 128], identity=ident[:]), incs=inc)
                c.wait("scalar", "t")
                c.s(lambda e: e.copy(out=loT3[:, :, :],
                                     in_=pst1[:].rearrange("p (kc t) -> p kc t", kc=DCH)))
                c.wait("tensor", "s")
                for pi, (lt, wt_) in enumerate(((lnT3, wrhi), (loT3, wrhi), (lnT3, wrlo))):
                    for j in range(DCH):
                        inc = [("t", 1)] if (pi == 2 and j == DCH - 1) else []
                        c.op("tensor", lambda e, j=j, lt=lt, wt_=wt_: e.matmul(
                            out=psR[:], lhsT=lt[:, j, :],
                            rhs=wt_[:, j * NEXP:(j + 1) * NEXP],
                            start=(pi == 0 and j == 0), stop=(pi == 2 and j == DCH - 1)),
                            incs=inc)
                c.wait("vector", "t")
                c.v(lambda e: e.tensor_copy(out=lg[:], in_=psR[:]))
                c.v(lambda e: e.reduce_max(out=cols[:, 16:17], in_=lg[:], axis=AX.X))
                c.v(lambda e: e.tensor_scalar_mul(out=cols[:, 17:18], in0=cols[:, 16:17],
                                                  scalar1=-1.0))
                c.wait("scalar", "v")
                c.s(lambda e: e.activation(out=r2[:], in_=lg[:], func=AF.Exp,
                                           bias=cols[:, 17:18], accum_out=cols[:, 18:19]))
                c.wait("vector", "s")
                c.v(lambda e: e.reciprocal(out=cols[:, 19:20], in_=cols[:, 18:19]))
                c.v(lambda e: e.tensor_scalar(out=r1[:], in0=lg[:],
                                              scalar1=cols[:, 16:17], scalar2=None,
                                              op0=ALU.is_equal))
                c.v(lambda e: e.tensor_tensor_scan(out=r2[:], data0=r1[:], data1=zero8[:],
                                                   initial=0.0, op0=ALU.add, op1=ALU.max))
                c.v(lambda e: e.tensor_single_scalar(out=r2[:], in_=r2[:], scalar=1.0,
                                                     op=ALU.is_equal))
                c.v(lambda e: e.tensor_mul(out=r1[:], in0=r1[:], in1=r2[:]))
                c.wait("tensor", "v")
                c.t(lambda e: e.matmul(out=psPr[:], lhsT=triu[:], rhs=r1[:],
                                       start=True, stop=True))
                c.t(lambda e: e.matmul(out=psT[:], lhsT=onesc[:], rhs=r1[:],
                                       start=True, stop=True))
                c.wait("vector", "t")
                c.v(lambda e: e.tensor_add(out=pref[:], in0=psPr[:], in1=brep[:]))
                c.v(lambda e: e.tensor_mul(out=pref[:], in0=pref[:], in1=r1[:]))
                c.v(lambda e: e.reduce_sum(out=cols[:, 20:21], in_=pref[:], axis=AX.X))
                c.v(lambda e: e.tensor_add(out=ctot[:], in0=ctot[:], in1=psT[:]))
                c.v(lambda e: e.tensor_copy(out=btot[:], in_=psT[:]))
                c.wait("sync", "v")
                c.dy(lambda e: e.dma_start(out=scr_dram[0:NEXP].unsqueeze(0), in_=btot[0:1, :]))
                c.wait("sync", "dy")
                c.dy(lambda e: e.dma_start(
                    out=r2[:], in_=scr_dram[0:NEXP].unsqueeze(0).broadcast_to([128, NEXP])))
                c.wait("vector", "dy")
                c.v(lambda e: e.tensor_add(out=brep[:], in0=brep[:], in1=r2[:]))
                c.wait("sync", "v")
                c.store(lambda e, i=i: e.dma_start(
                    out=oh_out[i * 128:(i + 1) * 128, :], in_=r1[:]))
                c.store(lambda e, i=i: e.dma_start(
                    out=pm_out[i * 128:(i + 1) * 128, :], in_=cols[:, 19:20]))
                c.store(lambda e, i=i: e.dma_start(
                    out=pos_out[i * 128:(i + 1) * 128, :], in_=cols[:, 20:21]))
            c.wait("sync", "v")
            c.store(lambda e: e.dma_start(out=cnt_out, in_=ctot[:]))
            c.wait("sync", "st", c.now("st"))

        if fio is None:
            run_phases(nc, emit)
        else:
            run_stage(nc, ctx, emit)
    return nc


def build_l2_ffn_main(fio=None):
    if fio is None:
        nc = _mk()
        ctx = None
        io = None
        pfx = ""
    else:
        nc, ctx, io, pfx = fio
    x_in = nc.dram_tensor("x_in", [TPC, D], F32, kind="ExternalInput").ap() if fio is None else io["x_in"]
    w1b = nc.dram_tensor("w1b", [NEXP, 128, HCH, DCH, 128], F32, kind="ExternalInput").ap() if fio is None else io["w1b"]
    w2a = nc.dram_tensor("w2a", [NEXP, 128, HCH, D], F32, kind="ExternalInput").ap() if fio is None else io["w2a"]
    oh_in = nc.dram_tensor("oh_in", [TPC, NEXP], F32, kind="ExternalInput").ap() if fio is None else io["oh_in"]
    pm_in = nc.dram_tensor("pm_in", [TPC, 1], F32, kind="ExternalInput").ap() if fio is None else io["pm_in"]
    pos_in = nc.dram_tensor("pos_in", [TPC, 1], F32, kind="ExternalInput").ap() if fio is None else io["pos_in"]
    prevT = nc.dram_tensor("prevT", [1, NEXP * NCORES], F32, kind="ExternalInput").ap() if fio is None else None
    ident_h = nc.dram_tensor("ident", [128, 128], BF16, kind="ExternalInput").ap() if fio is None else io["ident"]
    x_out = nc.dram_tensor("x_out", [TPC, D], F32, kind="ExternalOutput").ap() if fio is None else io["x_out"]
    ln_scr = nc.dram_tensor(pfx + "lnscr", [D, TPC], BF16, kind="Internal").ap()
    acc_scr = nc.dram_tensor(pfx + "accscr", [TPC, D], F32, kind="Internal").ap()
    scr_dram = nc.dram_tensor(pfx + "scr", [16], F32, kind="Internal").ap()

    TS = 512
    NTS_ = NT // 4

    with contextlib.ExitStack() as stk:
        e_ = stk.enter_context
        xt = [e_(nc.sbuf_tensor(pfx + f"xt{i}", [128, D], F32)) for i in range(2)]
        ident = e_(nc.sbuf_tensor(pfx + "identt", [128, 128], BF16))
        xc = e_(nc.sbuf_tensor(pfx + "xc", [128, D], F32))
        sq = e_(nc.sbuf_tensor(pfx + "sq", [128, D], F32))
        cols = e_(nc.sbuf_tensor(pfx + "cols", [128, 32], F32))
        ln_bf = e_(nc.sbuf_tensor(pfx + "ln_bf", [128, D], BF16))
        ln_f = e_(nc.sbuf_tensor(pfx + "ln_f", [128, D], F32))
        lnT = e_(nc.sbuf_tensor(pfx + "lnT", [128, D], BF16))
        lnsl = e_(nc.sbuf_tensor(pfx + "lnsl", [128, DCH * TS], BF16))
        w1sb = e_(nc.sbuf_tensor(pfx + "w1sb", [128, HCH * DCH * 128], BF16))
        w2sb = e_(nc.sbuf_tensor(pfx + "w2sb", [128, HCH * D], BF16))
        hT = e_(nc.sbuf_tensor(pfx + "hT", [128, HCH * TS], BF16))
        gate = e_(nc.sbuf_tensor(pfx + "gate", [128, NT * NEXP], F32))
        ptc = e_(nc.sbuf_tensor(pfx + "ptc", [128, NT], F32))
        prevsb = e_(nc.sbuf_tensor(pfx + "prevsb", [1, NEXP * NCORES], F32))
        offrep = e_(nc.sbuf_tensor(pfx + "offrep", [128, NEXP], F32))
        ohm = e_(nc.sbuf_tensor(pfx + "ohm", [128, NEXP], F32))
        acct = [xc, sq]
        outw = ln_f
        pst1 = e_(nc.psum_tensor(pfx + "pst1", [128, D], BF16))
        psH = [e_(nc.psum_tensor(pfx + f"psH{i}", [128, TS], F32)) for i in range(2)]
        psF = [e_(nc.psum_tensor(pfx + f"psF{i}", [128, D], F32)) for i in range(2)]

        def emit(c):
            c.dy(lambda e: e.dma_start(out=ident[:], in_=ident_h))
            if fio is None:
                c.dy(lambda e: e.dma_start(out=prevsb[:], in_=prevT))
                c.wait("vector", "dy")
                c.v(lambda e: e.reduce_sum(
                    out=ohm[0:1, 0:NEXP],
                    in_=prevsb[:].rearrange("p (e c) -> p e c", e=NEXP), axis=AX.X))
            else:
                c.wait("gpsimd", "st", c.now("st"))
                c.op("gpsimd", lambda e: e.collective_compute(
                    "AllGather", ALU.bypass,
                    replica_groups=[[0, 1, 2, 3, 4, 5, 6, 7]],
                    ins=[io["cnt_loc"].opt()], outs=[io["cnt_all"].opt()]),
                    incs=[("cc", 1)])
                c.wait("sync", "cc", c.now("cc"))
                c.dy(lambda e: e.dma_start(
                    out=prevsb[:], in_=io["cnt_all"].rearrange("c e -> (c e)").unsqueeze(0)))
                c.dy(lambda e: e.dma_start(out=xc[0:1, 0:64], in_=io["maskrep"]))
                c.wait("vector", "dy")
                c.v(lambda e: e.tensor_mul(out=prevsb[:], in0=prevsb[:], in1=xc[0:1, 0:64]))
                c.v(lambda e: e.reduce_sum(
                    out=ohm[0:1, 0:NEXP],
                    in_=prevsb[:].rearrange("p (c e) -> p e c", c=NCORES), axis=AX.X))
            c.wait("sync", "v")
            c.dy(lambda e: e.dma_start(out=scr_dram[0:NEXP].unsqueeze(0), in_=ohm[0:1, 0:NEXP]))
            c.wait("sync", "dy")
            c.dy(lambda e: e.dma_start(
                out=offrep[:], in_=scr_dram[0:NEXP].unsqueeze(0).broadcast_to([128, NEXP])))
            scr = {"xc": xc, "sq": sq, "cols": cols}
            lnT3 = lnT[:].rearrange("p (kc t) -> p kc t", kc=DCH)
            ln_scr3 = ln_scr.rearrange("(kc p) t -> p kc t", p=128)
            lnsl3 = lnsl[:].rearrange("p (kc t) -> p kc t", kc=DCH)
            hT3 = hT[:].rearrange("p (kc t) -> p kc t", kc=HCH)
            w1s4 = w1sb[:].rearrange("p (o kc n) -> p o kc n", o=HCH, kc=DCH)

            for i in range(NT):
                c.wait("sync", "st", c.now("st"))
                c.wait("sync", "v", c.now("v"))
                c.dy(lambda e, i=i: e.dma_start(
                    out=xt[i % 2][:], in_=x_in[i * 128:(i + 1) * 128, :]))
                c.dy(lambda e, i=i: e.dma_start(
                    out=ohm[:], in_=oh_in[i * 128:(i + 1) * 128, :]))
                c.dy(lambda e, i=i: e.dma_start(
                    out=cols[:, 21:22], in_=pm_in[i * 128:(i + 1) * 128, :]))
                c.dy(lambda e, i=i: e.dma_start(
                    out=cols[:, 22:23], in_=pos_in[i * 128:(i + 1) * 128, :]))
                c.wait("vector", "dy")
                c.wait("vector", "t", c.now("t"))
                c.wait("vector", "st", c.now("st"))
                emit_ln(c, xt[i % 2], scr, ln_bf, lnT3[:, :, :], pst1, ident)
                c.wait("sync", "s", c.now("s"))
                c.store(lambda e, i=i: e.dma_start(
                    out=ln_scr3[:, :, i * 128:(i + 1) * 128], in_=lnT3[:, :, :]))
                c.v(lambda e: e.tensor_mul(out=sq[:, 0:NEXP], in0=ohm[:], in1=offrep[:]))
                c.v(lambda e: e.reduce_sum(out=cols[:, 23:24], in_=sq[:, 0:NEXP], axis=AX.X))
                c.v(lambda e: e.tensor_add(out=cols[:, 23:24], in0=cols[:, 23:24],
                                           in1=cols[:, 22:23]))
                c.v(lambda e: e.tensor_scalar(out=cols[:, 24:25], in0=cols[:, 23:24],
                                              scalar1=1.0, scalar2=float(CAP2),
                                              op0=ALU.subtract, op1=ALU.is_lt))
                c.v(lambda e: e.tensor_scalar(out=cols[:, 25:26], in0=cols[:, 24:25],
                                              scalar1=cols[:, 21:22], scalar2=None,
                                              op0=ALU.mult))
                c.v(lambda e, i=i: e.tensor_scalar_mul(
                    out=gate[:, i * NEXP:(i + 1) * NEXP], in0=ohm[:],
                    scalar1=cols[:, 25:26]))
                c.v(lambda e, i=i: e.tensor_scalar(
                    out=ptc[:, i:i + 1], in0=cols[:, 24:25],
                    scalar1=-1.0, scalar2=1.0, op0=ALU.mult, op1=ALU.add))
            for ex in range(NEXP):
                c.wait("gpsimd", "t", c.now("t"))
                c.dg(lambda e, ex=ex: e.dma_start(out=w1s4[:, :, :, :], in_=w1b[ex]))
                c.dg(lambda e, ex=ex: e.dma_start(
                    out=w2sb[:].rearrange("p (kc n) -> p kc n", kc=HCH), in_=w2a[ex]))
                for ts in range(NTS_):
                    c.wait("sync", "t", c.now("t"))
                    c.dy(lambda e, ts=ts: e.dma_start(
                        out=lnsl3[:, :, :], in_=ln_scr3[:, :, ts * TS:(ts + 1) * TS]))
                    hmark = {}
                    for och in range(HCH):
                        bi = och % 2
                        c.wait("tensor", "dg")
                        c.wait("tensor", "dy")
                        c.wait("tensor", "s", hmark.get(och - 2, 0))
                        for j in range(DCH):
                            inc = [("t", 1)] if j == DCH - 1 else []
                            c.op("tensor", lambda e, j=j, bi=bi, och=och: e.matmul(
                                out=psH[bi][:], lhsT=w1s4[:, och, j, :],
                                rhs=lnsl3[:, j, :],
                                start=(j == 0), stop=(j == DCH - 1)), incs=inc)
                        c.wait("scalar", "t")
                        c.s(lambda e, och=och, bi=bi: e.activation(
                            out=hT3[:, och, :], in_=psH[bi][:], func=AF.Relu))
                        hmark[och] = c.now("s")
                    h_s = c.now("s")
                    for tl in range(4):
                        gi = ts * 4 + tl
                        bi = tl % 2
                        if ex > 0:
                            c.wait("sync", "v", c.now("v"))
                            c.dy(lambda e, gi=gi, bi=bi: e.dma_start(
                                out=acct[bi][:], in_=acc_scr[gi * 128:(gi + 1) * 128, :]))
                        c.wait("tensor", "s", h_s)
                        c.wait("tensor", "v", c.now("v"))
                        for j in range(HCH):
                            for nh in range(2):
                                inc = [("t", 1)] if (j == HCH - 1 and nh == 1) else []
                                c.op("tensor", lambda e, j=j, nh=nh, tl=tl, bi=bi: e.matmul(
                                    out=psF[bi][:, nh * 512:(nh + 1) * 512],
                                    lhsT=hT3[:, j, tl * 128:(tl + 1) * 128],
                                    rhs=w2sb[:, j * D + nh * 512:j * D + nh * 512 + 512],
                                    start=(j == 0), stop=(j == HCH - 1)), incs=inc)
                        c.wait("vector", "t")
                        c.wait("vector", "dy")
                        c.wait("vector", "st", c.now("st"))
                        if ex > 0:
                            c.v(lambda e, gi=gi, ex=ex, bi=bi: e.scalar_tensor_tensor(
                                out=acct[bi][:], in0=psF[bi][:],
                                scalar=gate[:, gi * NEXP + ex:gi * NEXP + ex + 1],
                                in1=acct[bi][:], op0=ALU.mult, op1=ALU.add))
                        else:
                            c.v(lambda e, gi=gi, ex=ex, bi=bi: e.tensor_scalar_mul(
                                out=acct[bi][:], in0=psF[bi][:],
                                scalar1=gate[:, gi * NEXP + ex:gi * NEXP + ex + 1]))
                        c.wait("sync", "v")
                        c.store(lambda e, gi=gi, bi=bi: e.dma_start(
                            out=acc_scr[gi * 128:(gi + 1) * 128, :], in_=acct[bi][:]))
            for i in range(NT):
                c.wait("sync", "st", c.now("st"))
                c.wait("sync", "v", c.now("v"))
                c.wait("sync", "s", c.now("s"))
                c.dy(lambda e, i=i: e.dma_start(
                    out=xt[0][:], in_=x_in[i * 128:(i + 1) * 128, :]))
                c.dy(lambda e, i=i: e.dma_start(
                    out=xt[1][:], in_=acc_scr[i * 128:(i + 1) * 128, :]))
                c.wait("vector", "dy")
                c.wait("vector", "st", c.now("st"))
                emit_ln(c, xt[0], scr, ln_bf, None, None, ident, ln_f32=ln_f)
                c.v(lambda e, i=i: e.scalar_tensor_tensor(
                    out=xt[1][:], in0=ln_f[:], scalar=ptc[:, i:i + 1],
                    in1=xt[1][:], op0=ALU.mult, op1=ALU.add))
                c.v(lambda e: e.scalar_tensor_tensor(
                    out=xt[0][:], in0=xt[1][:], scalar=0.5, in1=xt[0][:],
                    op0=ALU.mult, op1=ALU.add))
                c.wait("sync", "v")
                c.store(lambda e, i=i: e.dma_start(
                    out=x_out[i * 128:(i + 1) * 128, :], in_=xt[0][:]))
            c.wait("sync", "st", c.now("st"))

        if fio is None:
            run_phases(nc, emit)
        else:
            run_stage(nc, ctx, emit)
    return nc



DI = 2048
ICH = DI // 128
TC3 = 512
NC3 = S // TC3
NDB = DI // 16


def build_l3_attn(fio=None):
    if fio is None:
        nc = _mk()
        ctx = None
        io = None
        pfx = ""
    else:
        nc, ctx, io, pfx = fio
    x_in = nc.dram_tensor("x_in", [TPC, D], F32, kind="ExternalInput").ap() if fio is None else io["x_in"]
    inwb = nc.dram_tensor("inwb", [2, 2 * ICH, 128, DCH, 128], F32, kind="ExternalInput").ap() if fio is None else io["inwb"]
    convw = nc.dram_tensor("convw", [2, 128, ICH * 4], F32, kind="ExternalInput").ap() if fio is None else io["convw"]
    convb = nc.dram_tensor("convb", [2, 128, ICH], F32, kind="ExternalInput").ap() if fio is None else io["convb"]
    xprjb = nc.dram_tensor("xprjb", [2, 2, 128, ICH, 128], F32, kind="ExternalInput").ap() if fio is None else io["xprjb"]
    dtwb = nc.dram_tensor("dtwb", [2, ICH, 128, 1, 128], F32, kind="ExternalInput").ap() if fio is None else io["dtwb"]
    dtbh = nc.dram_tensor("dtbh", [2, 128, ICH], F32, kind="ExternalInput").ap() if fio is None else io["dtbh"]
    alogt = nc.dram_tensor("alogt", [2, 2, 128, NDB], F32, kind="ExternalInput").ap() if fio is None else io["alogt"]
    dph = nc.dram_tensor("dph", [2, 128, ICH], F32, kind="ExternalInput").ap() if fio is None else io["dph"]
    outwa = nc.dram_tensor("outwa", [2, 128, ICH, D], F32, kind="ExternalInput").ap() if fio is None else io["outwa"]
    normw = nc.dram_tensor("normw", [2, D], F32, kind="ExternalInput").ap() if fio is None else io["normw"]
    normf = nc.dram_tensor("normf", [D], F32, kind="ExternalInput").ap() if fio is None else io["normf"]
    selpad_h = nc.dram_tensor("selpad", [128, 240], BF16, kind="ExternalInput").ap() if fio is None else io["selpad"]
    ident_h = nc.dram_tensor("ident", [128, 128], BF16, kind="ExternalInput").ap() if fio is None else io["ident"]
    x_out = nc.dram_tensor("x_out", [TPC, D], F32, kind="ExternalOutput").ap() if fio is None else io["x_out"]
    xm_scr = nc.dram_tensor(pfx + "xm_scr", [TPC, D], F32, kind="Internal").ap()
    dl_scr = nc.dram_tensor(pfx + "dl_scr", [DI, TC3], F32, kind="Internal").ap()
    du_scr = nc.dram_tensor(pfx + "du_scr", [DI, TC3], F32, kind="Internal").ap()
    bc_scr = nc.dram_tensor(pfx + "bc_scr", [32, TC3], F32, kind="Internal").ap()

    NTT = TC3 // 128

    with contextlib.ExitStack() as stk:
        e_ = stk.enter_context
        xt = [e_(nc.sbuf_tensor(pfx + f"xt{i}", [128, D], F32)) for i in range(2)]
        ident = e_(nc.sbuf_tensor(pfx + "identt", [128, 128], BF16))
        selp = e_(nc.sbuf_tensor(pfx + "selp", [128, 240], BF16))
        xc = e_(nc.sbuf_tensor(pfx + "xc", [128, D], F32))
        sq = e_(nc.sbuf_tensor(pfx + "sq", [128, D], F32))
        cols = e_(nc.sbuf_tensor(pfx + "cols", [128, 16], F32))
        ln_bf = e_(nc.sbuf_tensor(pfx + "ln_bf", [128, D], BF16))
        rnT = e_(nc.sbuf_tensor(pfx + "rnT", [128, DCH * TC3], BF16))
        gam = e_(nc.sbuf_tensor(pfx + "gam", [128, D], F32))
        uc_ext = e_(nc.sbuf_tensor(pfx + "uc_ext", [128, ICH * (TC3 + 3)], F32))
        ucv = e_(nc.sbuf_tensor(pfx + "ucv", [128, ICH * TC3], F32))
        zT = e_(nc.sbuf_tensor(pfx + "zT", [128, ICH * TC3], BF16))
        ucbs = e_(nc.sbuf_tensor(pfx + "ucbs", [128, TC3], BF16))
        yT = e_(nc.sbuf_tensor(pfx + "yT", [128, ICH * TC3], BF16))
        stg = e_(nc.sbuf_tensor(pfx + "stg", [128, TC3], F32))
        stg2 = e_(nc.sbuf_tensor(pfx + "stg2", [128, TC3], F32))
        xdb = e_(nc.sbuf_tensor(pfx + "xdb", [128, TC3], F32))
        cwcol = e_(nc.sbuf_tensor(pfx + "cwcol", [128, ICH * 4], F32))
        cbcol = e_(nc.sbuf_tensor(pfx + "cbcol", [128, ICH], F32))
        dtbc = e_(nc.sbuf_tensor(pfx + "dtbc", [128, ICH], F32))
        dpc = e_(nc.sbuf_tensor(pfx + "dpc", [128, ICH], F32))
        atab = e_(nc.sbuf_tensor(pfx + "atab", [128, 2 * NDB], F32))
        carr = e_(nc.sbuf_tensor(pfx + "carr", [128, 2 * NDB], F32))
        drep = e_(nc.sbuf_tensor(pfx + "drep", [128, TC3], F32))
        durep = e_(nc.sbuf_tensor(pfx + "durep", [128, TC3], F32))
        brep = e_(nc.sbuf_tensor(pfx + "brep2", [128, 2 * TC3], F32))
        crep = e_(nc.sbuf_tensor(pfx + "crep2", [128, 2 * TC3], F32))
        da = e_(nc.sbuf_tensor(pfx + "da", [128, TC3], F32))
        dub = e_(nc.sbuf_tensor(pfx + "dub", [128, TC3], F32))
        hst = e_(nc.sbuf_tensor(pfx + "hst", [128, TC3], F32))
        wcc = e_(nc.sbuf_tensor(pfx + "wcc", [128, TC3], BF16))
        wband = [e_(nc.sbuf_tensor(pfx + "wband0", [128, DCH * 128], BF16))] * 2
        xpb = e_(nc.sbuf_tensor(pfx + "xpb", [128, 2 * ICH * 128], BF16))
        dtwsb = e_(nc.sbuf_tensor(pfx + "dtwsb", [128, ICH * 128], BF16))
        owT = e_(nc.sbuf_tensor(pfx + "owT", [128, ICH * D], BF16))
        pst1 = e_(nc.psum_tensor(pfx + "pst1", [128, D], BF16))
        psP = [e_(nc.psum_tensor(pfx + f"psP{i}", [128, TC3], F32)) for i in range(2)]
        psY = e_(nc.psum_tensor(pfx + "psY", [128, TC3], F32))
        psF = e_(nc.psum_tensor(pfx + "psF", [128, D], F32))

        def emit(c):
            c.dy(lambda e: e.dma_start(out=ident[:], in_=ident_h))
            c.dy(lambda e: e.dma_start(out=selp[:], in_=selpad_h))
            scr = {"xc": xc, "sq": sq, "cols": cols}
            rnT3 = rnT[:].rearrange("p (kc t) -> p kc t", kc=DCH)
            uce3 = uc_ext[:].rearrange("p (kc t) -> p kc t", kc=ICH)
            ucv3 = ucv[:].rearrange("p (kc t) -> p kc t", kc=ICH)
            zT3 = zT[:].rearrange("p (kc t) -> p kc t", kc=ICH)
            yT3 = yT[:].rearrange("p (kc t) -> p kc t", kc=ICH)

            for i in range(NT):
                c.wait("sync", "st", c.now("st"))
                c.wait("sync", "v", c.now("v"))
                c.dy(lambda e, i=i: e.dma_start(
                    out=xt[i % 2][:], in_=x_in[i * 128:(i + 1) * 128, :]))
                c.wait("vector", "dy")
                c.wait("vector", "st", c.now("st"))
                emit_ln(c, xt[i % 2], scr, ln_bf, None, None, ident, ln_f32=xc)
                c.wait("sync", "v")
                c.store(lambda e, i=i: e.dma_start(
                    out=xm_scr[i * 128:(i + 1) * 128, :], in_=xc[:]))

            for li in range(2):
                c.dy(lambda e, li=li: e.dma_start(out=gam[:], in_=bcast_row(normw[li], D)))
                c.dy(lambda e, li=li: e.dma_start(out=cwcol[:], in_=convw[li]))
                c.dy(lambda e, li=li: e.dma_start(out=cbcol[:], in_=convb[li]))
                c.dy(lambda e, li=li: e.dma_start(out=dtbc[:], in_=dtbh[li]))
                c.dy(lambda e, li=li: e.dma_start(out=dpc[:], in_=dph[li]))
                c.dy(lambda e, li=li: e.dma_start(
                    out=atab[:].rearrange("p (h n) -> p h n", h=2), in_=alogt[li]))
                c.wait("gpsimd", "t", c.now("t"))
                c.dg(lambda e, li=li: e.dma_start(
                    out=owT[:].rearrange("p (kc n) -> p kc n", kc=ICH), in_=outwa[li]))
                c.dg(lambda e, li=li: e.dma_start(
                    out=xpb[:].rearrange("p (m j n) -> p m j n", m=2, j=ICH),
                    in_=xprjb[li]))
                c.dg(lambda e, li=li: e.dma_start(
                    out=dtwsb[:].rearrange("p (j o n) -> p j o n", j=ICH, o=1),
                    in_=dtwb[li]))
                c.wait("scalar", "dy")
                c.s(lambda e: e.activation(out=atab[:], in_=atab[:], func=AF.Exp))
                c.wait("vector", "s")
                c.v(lambda e: e.tensor_scalar_mul(out=atab[:], in0=atab[:], scalar1=-1.0))

                for sq_i in range(2):
                    c.wait("vector", "dy")
                    c.v(lambda e: e.memset(carr[:], 0.0))
                    for kc in range(ICH):
                        c.v0(lambda e, kc=kc: e.memset(uce3[:, kc, 0:3], 0.0))
                    for ch in range(NC3):
                        T0 = sq_i * S + ch * TC3
                        for tl in range(NTT):
                            i = T0 // 128 + tl
                            c.wait("sync", "st", c.now("st"))
                            c.wait("sync", "v", c.now("v"))
                            c.wait("sync", "s", c.now("s"))
                            c.dy(lambda e, i=i, tl=tl: e.dma_start(
                                out=xt[tl % 2][:], in_=xm_scr[i * 128:(i + 1) * 128, :]))
                            c.wait("vector", "dy")
                            c.wait("vector", "t", c.now("t"))
                            emit_ln(c, xt[tl % 2], scr, ln_bf,
                                    rnT3[:, :, tl * 128:(tl + 1) * 128], pst1, ident,
                                    rms=True, gamma_rep=gam)
                        if ch > 0:
                            c.wait("vector", "t", c.now("t"))
                            for kc in range(ICH):
                                c.v0(lambda e, kc=kc: e.tensor_copy(
                                    out=uce3[:, kc, 0:3], in_=uce3[:, kc, TC3:TC3 + 3]))
                        for och in range(2 * ICH):
                            bi = och % 2
                            c.wait("gpsimd", "t", c.now("t"))
                            c.dg(lambda e, li=li, och=och: e.dma_start(
                                out=wband[0][:].rearrange("p (kc n) -> p kc n", kc=DCH),
                                in_=inwb[li, och]))
                            c.wait("tensor", "dg")
                            c.wait("tensor", "s", c.now("s"))
                            c.wait("tensor", "v", c.now("v"))
                            for j in range(DCH):
                                inc = [("t", 1)] if j == DCH - 1 else []
                                c.op("tensor", lambda e, j=j, bi=bi: e.matmul(
                                    out=psP[bi][:], lhsT=wband[0][:, j * 128:(j + 1) * 128],
                                    rhs=rnT[:, j * TC3:(j + 1) * TC3],
                                    start=(j == 0), stop=(j == DCH - 1)), incs=inc)
                            if och < ICH:
                                c.wait("vector", "t")
                                c.v(lambda e, och=och, bi=bi: e.tensor_copy(
                                    out=uce3[:, och, 3:TC3 + 3], in_=psP[bi][:]))
                            else:
                                c.wait("scalar", "t")
                                c.s(lambda e, och=och, bi=bi: e.copy(
                                    out=zT3[:, och - ICH, :], in_=psP[bi][:]))
                        for kc in range(ICH):
                            c.v(lambda e, kc=kc: e.tensor_scalar_mul(
                                out=ucv3[:, kc, :], in0=uce3[:, kc, 0:TC3],
                                scalar1=cwcol[:, 4 * kc:4 * kc + 1]))
                            for k in range(1, 4):
                                c.v(lambda e, kc=kc, k=k: e.scalar_tensor_tensor(
                                    out=ucv3[:, kc, :], in0=uce3[:, kc, k:TC3 + k],
                                    scalar=cwcol[:, 4 * kc + k:4 * kc + k + 1],
                                    in1=ucv3[:, kc, :], op0=ALU.mult, op1=ALU.add))
                            c.v(lambda e, kc=kc: e.tensor_scalar_add(
                                out=ucv3[:, kc, :], in0=ucv3[:, kc, :],
                                scalar1=cbcol[:, kc:kc + 1]))
                            c.wait("scalar", "v")
                            c.s(lambda e, kc=kc: e.activation(
                                out=ucv3[:, kc, :], in_=ucv3[:, kc, :], func=AF.Silu))
                        c.wait("tensor", "s", c.now("s"))
                        c.wait("tensor", "v", c.now("v"))
                        for m in range(2):
                            c.wait("tensor", "v", c.now("v"))
                            for j in range(ICH):
                                c.wait("scalar", "t", c.now("t"))
                                c.s(lambda e, j=j: e.copy(out=ucbs[:], in_=ucv3[:, j, :]))
                                c.wait("tensor", "s", c.now("s"))
                                c.op("tensor", lambda e, m=m, j=j: e.matmul(
                                    out=psY[:],
                                    lhsT=xpb[:, (m * ICH + j) * 128:(m * ICH + j) * 128 + 128],
                                    rhs=ucbs[:],
                                    start=(j == 0), stop=(j == ICH - 1)), incs=[("t", 1)])
                            if m == 0:
                                c.wait("vector", "t")
                                c.v(lambda e: e.tensor_copy(out=wcc[:], in_=psY[:]))
                            else:
                                c.wait("vector", "t")
                                c.v(lambda e: e.tensor_copy(out=xdb[:], in_=psY[:]))
                        c.wait("sync", "v", c.now("v"))
                        c.dy(lambda e: e.dma_start(out=bc_scr[:, :], in_=xdb[0:32, :]))
                        c.wait("sync", "dg", c.now("dg"))
                        for och in range(ICH):
                            c.wait("tensor", "v", c.now("v"))
                            c.t(lambda e, och=och: e.matmul(
                                out=psP[0][:], lhsT=dtwsb[:, och * 128:(och + 1) * 128],
                                rhs=wcc[:], start=True, stop=True))
                            c.wait("vector", "t")
                            c.wait("vector", "dy", c.now("dy"))
                            c.v(lambda e, och=och: e.tensor_scalar_add(
                                out=stg[:], in0=psP[0][:], scalar1=dtbc[:, och:och + 1]))
                            c.v(lambda e: e.tensor_single_scalar(
                                out=stg2[:], in_=stg[:], scalar=0.0, op=ALU.max))
                            c.wait("scalar", "v")
                            c.s(lambda e: e.activation(out=da[:], in_=stg[:], func=AF.Abs))
                            c.s(lambda e: e.activation(out=da[:], in_=da[:], func=AF.Exp,
                                                       scale=-1.0))
                            c.s(lambda e: e.activation(out=da[:], in_=da[:], func=AF.Ln,
                                                       bias=1.0))
                            c.wait("vector", "s")
                            c.v(lambda e: e.tensor_add(out=stg[:], in0=stg2[:], in1=da[:]))
                            c.v(lambda e, och=och: e.tensor_mul(
                                out=stg2[:], in0=stg[:], in1=ucv3[:, och, :]))
                            c.wait("sync", "v", c.now("v"))
                            c.dy(lambda e, och=och: e.dma_start(
                                out=dl_scr[och * 128:(och + 1) * 128, :], in_=stg[:]))
                            c.dy(lambda e, och=och: e.dma_start(
                                out=du_scr[och * 128:(och + 1) * 128, :], in_=stg2[:]))
                        c.wait("sync", "dy", c.now("dy"))
                        c.wait("gpsimd", "dy", c.now("dy"))
                        for half in range(2):
                            c.dy(lambda e, half=half: e.dma_start(
                                out=brep[:, half * TC3:(half + 1) * TC3],
                                in_=bc_scr[(half * 8):(half * 8 + 8), :]
                                    .unsqueeze(1).broadcast_to([8, 16, TC3])))
                            c.dy(lambda e, half=half: e.dma_start(
                                out=crep[:, half * TC3:(half + 1) * TC3],
                                in_=bc_scr[(16 + half * 8):(16 + half * 8 + 8), :]
                                    .unsqueeze(1).broadcast_to([8, 16, TC3])))
                        dr2 = [drep, xc[:, TC3:2 * TC3]]
                        du2 = [durep, sq[:, 0:TC3]]
                        da2 = [da, xc[:, 0:TC3]]
                        v0_ = c.now("v")
                        s0_ = c.now("s")
                        vmk = {-2: v0_, -1: v0_}
                        smk = {-2: s0_, -1: s0_}
                        scm = {-2: v0_, -1: v0_}
                        it = 0
                        for g in range(NDB // 8):
                            for half in range(2):
                                for k in range(8):
                                    db = g * 8 + k
                                    tile = half * NDB + db
                                    bi2 = it % 2
                                    q = c.dy if (k % 2 == 0) else c.dg
                                    qe = "sync" if k % 2 == 0 else "gpsimd"
                                    c.wait(qe, "v", vmk[it - 2])
                                    c.wait(qe, "s", smk[it - 2])
                                    q(lambda e, db=db, bi2=bi2: e.dma_start(
                                        out=dr2[bi2][:],
                                        in_=dl_scr[db * 16:(db + 1) * 16, :]
                                            .unsqueeze(0).broadcast_to([8, 16, TC3])))
                                    q(lambda e, db=db, bi2=bi2: e.dma_start(
                                        out=du2[bi2][:],
                                        in_=du_scr[db * 16:(db + 1) * 16, :]
                                            .unsqueeze(0).broadcast_to([8, 16, TC3])))
                                    c.wait("scalar", "dy")
                                    c.wait("scalar", "dg")
                                    c.wait("scalar", "v", scm[it - 2])
                                    c.s(lambda e, half=half, db=db, bi2=bi2: e.activation(
                                        out=da2[bi2][:], in_=dr2[bi2][:], func=AF.Exp,
                                        scale=atab[:, half * NDB + db:half * NDB + db + 1]))
                                    smk[it] = c.now("s")
                                    c.wait("vector", "dy")
                                    c.wait("vector", "dg")
                                    c.v(lambda e, half=half, bi2=bi2: e.tensor_mul(
                                        out=dub[:], in0=du2[bi2][:],
                                        in1=brep[:, half * TC3:(half + 1) * TC3]))
                                    vmk[it] = c.now("v")
                                    c.wait("vector", "s")
                                    c.v(lambda e, tile=tile, bi2=bi2: e.tensor_tensor_scan(
                                        out=hst[:], data0=da2[bi2][:], data1=dub[:],
                                        initial=carr[:, tile:tile + 1],
                                        op0=ALU.mult, op1=ALU.add))
                                    scm[it] = c.now("v")
                                    c.v(lambda e, tile=tile: e.tensor_copy(
                                        out=carr[:, tile:tile + 1], in_=hst[:, TC3 - 1:TC3]))
                                    c.wait("vector", "t", c.now("t"))
                                    c.v(lambda e, half=half: e.tensor_mul(
                                        out=wcc[:], in0=hst[:],
                                        in1=crep[:, half * TC3:(half + 1) * TC3]))
                                    c.wait("tensor", "v", c.now("v"))
                                    c.op("tensor", lambda e, k=k, half=half: e.matmul(
                                        out=psY[:],
                                        lhsT=selp[:, 112 - 16 * k:240 - 16 * k],
                                        rhs=wcc[:],
                                        start=(half == 0 and k == 0),
                                        stop=(half == 1 and k == 7)), incs=[("t", 1)])
                                    it += 1
                            c.wait("vector", "t")
                            c.v(lambda e, g=g: e.scalar_tensor_tensor(
                                out=stg[:], in0=ucv3[:, g, :], scalar=dpc[:, g:g + 1],
                                in1=psY[:], op0=ALU.mult, op1=ALU.add))
                            c.wait("scalar", "v")
                            c.s(lambda e, g=g: e.activation(
                                out=stg2[:], in_=zT3[:, g, :], func=AF.Silu))
                            c.wait("vector", "s")
                            c.v(lambda e, g=g: e.tensor_mul(
                                out=yT3[:, g, :], in0=stg[:], in1=stg2[:]))
                        y_v = c.now("v")
                        for tl in range(NTT):
                            i = T0 // 128 + tl
                            c.wait("tensor", "v", c.now("v"))
                            c.wait("tensor", "dg", c.now("dg"))
                            for j in range(ICH):
                                for nh in range(2):
                                    inc = [("t", 1)] if (j == ICH - 1 and nh == 1) else []
                                    c.op("tensor", lambda e, j=j, nh=nh, tl=tl: e.matmul(
                                        out=psF[:, nh * 512:(nh + 1) * 512],
                                        lhsT=yT3[:, j, tl * 128:(tl + 1) * 128],
                                        rhs=owT[:, j * D + nh * 512:j * D + nh * 512 + 512],
                                        start=(j == 0), stop=(j == ICH - 1)), incs=inc)
                            c.wait("sync", "st", c.now("st"))
                            c.wait("sync", "v", c.now("v"))
                            c.dy(lambda e, i=i: e.dma_start(
                                out=xt[0][:], in_=xm_scr[i * 128:(i + 1) * 128, :]))
                            c.wait("vector", "dy")
                            c.wait("vector", "t")
                            c.v(lambda e: e.tensor_add(out=xt[0][:], in0=psF[:], in1=xt[0][:]))
                            c.wait("sync", "v")
                            c.store(lambda e, i=i: e.dma_start(
                                out=xm_scr[i * 128:(i + 1) * 128, :], in_=xt[0][:]))
            c.wait("sync", "dy", c.now("dy"))
            c.dy(lambda e: e.dma_start(out=gam[:], in_=bcast_row(normf, D)))
            for i in range(NT):
                c.wait("sync", "st", c.now("st"))
                c.wait("sync", "v", c.now("v"))
                c.wait("sync", "s", c.now("s"))
                c.dy(lambda e, i=i: e.dma_start(
                    out=xt[0][:], in_=xm_scr[i * 128:(i + 1) * 128, :]))
                c.dy(lambda e, i=i: e.dma_start(
                    out=xt[1][:], in_=x_in[i * 128:(i + 1) * 128, :]))
                c.wait("vector", "dy")
                c.wait("vector", "s", c.now("s"))
                c.wait("vector", "st", c.now("st"))
                emit_ln(c, xt[0], scr, ln_bf, None, None, ident, ln_f32=xc,
                        rms=True, gamma_rep=gam)
                c.v(lambda e: e.tensor_add(out=xt[1][:], in0=xc[:], in1=xt[1][:]))
                c.wait("sync", "v")
                c.store(lambda e, i=i: e.dma_start(
                    out=x_out[i * 128:(i + 1) * 128, :], in_=xt[1][:]))
            c.wait("sync", "st", c.now("st"))

        if fio is None:
            run_phases(nc, emit)
        else:
            run_stage(nc, ctx, emit)
    return nc



IDENT = np.eye(128, dtype=ml_dtypes.bfloat16)


def prep_b(w):
    N, Kk = w.shape
    return np.ascontiguousarray(
        w.reshape(N // 128, 128, Kk // 128, 128).transpose(0, 3, 2, 1))


def prep_a(w):
    N, Kk = w.shape
    return np.ascontiguousarray(w.T.reshape(Kk // 128, 128, N).transpose(1, 0, 2))


def prep_b_eo(w):
    return np.ascontiguousarray(prep_b(w).transpose(1, 0, 2, 3))


def prep_scol(s_vec):
    return np.ascontiguousarray(s_vec.reshape(-1, 128).T)


CMASK = np.where(np.tril(np.ones((128, 128))) > 0, 0.0, -1e6).astype(np.float32)
TRIU = np.triu(np.ones((128, 128), np.float32))
SELPAD = np.zeros((128, 240), np.float32)
SELPAD[np.arange(128), 112 + np.arange(128) % 16] = 1.0
SELPAD = SELPAD.astype(ml_dtypes.bfloat16)


def _shards(x):
    return [np.ascontiguousarray(x[c * TPC:(c + 1) * TPC]) for c in range(NCORES)]


def _run(nc, in_maps, outkey="x_out"):
    res = run_bass_kernel_spmd(nc, in_maps, core_ids=list(range(NCORES)))
    return res.results


def _prep_l3_attn(I):
    li_in = []
    for li in range(2):
        xp = np.zeros((2048, 256), np.float32)
        xp[:, :160] = I["L3_attn_xproj_w"][li]
        alog = np.asarray(I["L3_attn_A_log"][li], np.float32)
        tab = np.zeros((2, 128, 128), np.float32)
        for h in range(2):
            p = np.arange(128)
            db = np.arange(128)
            tab[h] = alog[(db[None, :] * 16 + (p % 16)[:, None]), (h * 8 + p // 16)[:, None]]
        li_in.append(dict(
            inwb=prep_b(np.asarray(I["L3_attn_in_w"][li], np.float32).T),
            convw=np.ascontiguousarray(np.asarray(I["L3_attn_conv_w"][li], np.float32)
                                       .reshape(16, 128, 4).transpose(1, 0, 2).reshape(128, 64)),
            convb=prep_scol(np.asarray(I["L3_attn_conv_b"][li], np.float32)),
            xprjb=prep_b(xp.T),
            dtwb=prep_b(np.asarray(I["L3_attn_dt_w"][li], np.float32).T),
            dtbh=prep_scol(np.asarray(I["L3_attn_dt_b"][li], np.float32)),
            alogt=tab, dph=prep_scol(np.asarray(I["L3_attn_D_p"][li], np.float32)),
            outwa=prep_a(np.asarray(I["L3_attn_out_w"][li], np.float32).T)))
    st = {k: np.stack([li_in[0][k], li_in[1][k]]) for k in li_in[0]}
    st.update(normw=np.asarray(I["L3_attn_norm_w"], np.float32),
              normf=np.asarray(I["L3_attn_normf_w"], np.float32),
              selpad=SELPAD, ident=IDENT)
    return st




def build_fused():
    nc = bass.Bass("TRN2", target_bir_lowering=False, num_devices=8)
    F, Bq = F32, BF16
    di = lambda n, sh, dt=F32: nc.dram_tensor(n, sh, dt, kind="ExternalInput").ap()
    x_ext = di("x_in", [TPC, D])
    out_ext = nc.dram_tensor("x_out", [TPC, D], F32, kind="ExternalOutput").ap()
    ident_h = di("ident", [128, 128], BF16)
    cmask_h = di("cmask", [128, 128])
    triu_h = di("triu", [128, 128])
    selpad_h = di("selpad", [128, 240], BF16)
    maskrep_h = di("maskrep", [1, 64])
    xa = nc.dram_tensor("xa", [TPC, D], F32, kind="Internal").ap()
    xb = nc.dram_tensor("xb", [TPC, D], F32, kind="Internal").ap()

    io_l0f = dict(wT=di("l0f_wT", [D, D]), alpha=di("l0f_alpha", [1]),
                  bias=di("l0f_bias", [D]), ident=ident_h)
    io_l0a = dict(wqb=di("l0a_wqb", [ECH, 128, DCH, 128]), wkb=di("l0a_wkb", [ECH, 128, DCH, 128]),
                  wva=di("l0a_wva", [128, DCH, E2]), woa=di("l0a_woa", [128, ECH, D]),
                  bqh=di("l0a_bqh", [128, ECH]), bkh=di("l0a_bkh", [128, ECH]),
                  bvv=di("l0a_bvv", [E2]), bov=di("l0a_bov", [D]), ident=ident_h)
    io_l1f = dict(w1b=di("l1f_w1b", [NEXP, 128, HCH, DCH, 128]),
                  w2a=di("l1f_w2a", [NEXP, 128, HCH, D]),
                  wrt=di("l1f_wrt", [128, DCH, NEXP]),
                  b1h=di("l1f_b1h", [128, NEXP * HCH]), b2h=di("l1f_b2h", [NEXP, D]),
                  ident=ident_h)
    io_l1a = dict(wqb=di("l1a_wqb", [DCH, 128, DCH, 128]), wkb=di("l1a_wkb", [DCH, 128, DCH, 128]),
                  wva=di("l1a_wva", [128, DCH, D]), woa=di("l1a_woa", [128, DCH, D]),
                  bqh=di("l1a_bqh", [128, DCH]), bkh=di("l1a_bkh", [128, DCH]),
                  bvv=di("l1a_bvv", [D]), bov=di("l1a_bov", [D]),
                  cmask=cmask_h, ident=ident_h)
    io_l2r = dict(wrt=di("l2r_wrt", [128, DCH, NEXP]), ident=ident_h, triu=triu_h)
    io_l2m = dict(w1b=di("l2m_w1b", [NEXP, 128, HCH, DCH, 128]),
                  w2a=di("l2m_w2a", [NEXP, 128, HCH, D]),
                  maskrep=maskrep_h, ident=ident_h)
    io_l2a = dict(wkbh=di("l2a_wkbh", [DCH, 128, DCH, 128]), wvbh=di("l2a_wvbh", [DCH, 128, DCH, 128]),
                  wrbh=di("l2a_wrbh", [DCH, 128, DCH, 128]), woa=di("l2a_woa", [128, DCH, D]),
                  muk=di("l2a_muk", [128, DCH]), muv=di("l2a_muv", [128, DCH]),
                  mur=di("l2a_mur", [128, DCH]), wdec=di("l2a_wdec", [128, DCH]),
                  ufir=di("l2a_ufir", [128, DCH]), ident=ident_h)
    io_l3f = dict(w1b=di("l3f_w1b", [HCH, 128, DCH, 128]), w2b=di("l3f_w2b", [HCH, 128, DCH, 128]),
                  h1b=di("l3f_h1b", [HCH, 128, HCH, 128]), h2b=di("l3f_h2b", [HCH, 128, HCH, 128]),
                  w3a=di("l3f_w3a", [128, HCH, D]),
                  s1h=di("l3f_s1h", [128, HCH]), s2h=di("l3f_s2h", [128, HCH]),
                  hs1h=di("l3f_hs1h", [128, HCH]), hs2h=di("l3f_hs2h", [128, HCH]),
                  s3=di("l3f_s3", [D]), ident=ident_h)
    io_l3a = dict(inwb=di("l3a_inwb", [2, 2 * ICH, 128, DCH, 128]),
                  convw=di("l3a_convw", [2, 128, ICH * 4]), convb=di("l3a_convb", [2, 128, ICH]),
                  xprjb=di("l3a_xprjb", [2, 2, 128, ICH, 128]),
                  dtwb=di("l3a_dtwb", [2, ICH, 128, 1, 128]), dtbh=di("l3a_dtbh", [2, 128, ICH]),
                  alogt=di("l3a_alogt", [2, 2, 128, NDB]), dph=di("l3a_dph", [2, 128, ICH]),
                  outwa=di("l3a_outwa", [2, 128, ICH, D]),
                  normw=di("l3a_normw", [2, D]), normf=di("l3a_normf", [D]),
                  selpad=selpad_h, ident=ident_h)
    route_scr = []
    for tag in ("A", "B"):
        route_scr.append(dict(
            oh=nc.dram_tensor(f"oh{tag}", [TPC, NEXP], F32, kind="Internal").ap(),
            pm=nc.dram_tensor(f"pm{tag}", [TPC, 1], F32, kind="Internal").ap(),
            pos=nc.dram_tensor(f"pos{tag}", [TPC, 1], F32, kind="Internal").ap(),
            cnt=nc.dram_tensor(f"cnt{tag}", [1, NEXP], F32, kind="Internal").ap(),
            cnt_all=nc.dram_tensor(f"cntall{tag}", [NCORES, NEXP], F32,
                                   kind="Internal", addr_space="Shared").ap()))

    with contextlib.ExitStack() as st_:
        sems = {k: st_.enter_context(nc.semaphore(k)) for k in SEMS}
        ctx = Ctx(nc, sems)
        ctx.cur = None

        def st_io(xi, xo, extra):
            d = dict(extra)
            d["x_in"] = xi
            d["x_out"] = xo
            return d

        build_l0_ffn(fio=(nc, ctx, st_io(x_ext, xa, io_l0f), "a_"))
        build_l0_attn(fio=(nc, ctx, st_io(xa, xb, io_l0a), "b_"))
        build_l0_ffn(fio=(nc, ctx, st_io(xb, xa, io_l0f), "c_"))
        build_l1_ffn(fio=(nc, ctx, st_io(xa, xb, io_l1f), "d_"))
        build_l1_attn(fio=(nc, ctx, st_io(xb, xa, io_l1a), "e_"))
        build_l1_ffn(fio=(nc, ctx, st_io(xa, xb, io_l1f), "f_"))
        for ci, (xi, xo) in enumerate(((xb, xa), (xb, xa))):
            pass
        r = route_scr[0]
        build_l2_ffn_route(fio=(nc, ctx, dict(io_l2r, x_in=xb, oh_out=r["oh"],
                                              pm_out=r["pm"], pos_out=r["pos"],
                                              cnt_out=r["cnt"]), "g_"))
        build_l2_ffn_main(fio=(nc, ctx, dict(io_l2m, x_in=xb, x_out=xa,
                                             oh_in=r["oh"], pm_in=r["pm"], pos_in=r["pos"],
                                             cnt_loc=r["cnt"], cnt_all=r["cnt_all"]), "h_"))
        build_l2_attn(fio=(nc, ctx, st_io(xa, xb, io_l2a), "i_"))
        r = route_scr[1]
        build_l2_ffn_route(fio=(nc, ctx, dict(io_l2r, x_in=xb, oh_out=r["oh"],
                                              pm_out=r["pm"], pos_out=r["pos"],
                                              cnt_out=r["cnt"]), "j_"))
        build_l2_ffn_main(fio=(nc, ctx, dict(io_l2m, x_in=xb, x_out=xa,
                                             oh_in=r["oh"], pm_in=r["pm"], pos_in=r["pos"],
                                             cnt_loc=r["cnt"], cnt_all=r["cnt_all"]), "k_"))
        build_l3_ffn(fio=(nc, ctx, st_io(xa, xb, io_l3f), "l_"))
        build_l3_attn(fio=(nc, ctx, st_io(xb, xa, io_l3a), "m_"))
        build_l3_ffn(fio=(nc, ctx, st_io(xa, out_ext, io_l3f), "n_"))
    return nc


FUSED = True


def kernel(**inputs):
    I = {k: np.asarray(v, np.float32) for k, v in inputs.items()}
    x = np.ascontiguousarray(I["L0_attn_hidden_states"].reshape(B * S, D))
    cores = list(range(NCORES))

    nc_l0f = build_l0_ffn()
    nc_l0a = build_l0_attn()
    nc_l1f = build_l1_ffn()
    nc_l1a = build_l1_attn()
    nc_l2r = build_l2_ffn_route()
    nc_l2m = build_l2_ffn_main()
    nc_l2a = build_l2_attn()
    nc_l3f = build_l3_ffn()
    nc_l3a = build_l3_attn()

    l0f_w = dict(wT=np.ascontiguousarray(I["L0_ffn_weight"].T),
                 alpha=I["L0_ffn_alpha"].reshape(1), bias=I["L0_ffn_bias"], ident=IDENT)
    l0a_w = dict(wqb=prep_b(I["L0_attn_Wq"]), wkb=prep_b(I["L0_attn_Wk"]),
                 wva=prep_a(I["L0_attn_Wv"]), woa=prep_a(I["L0_attn_Wo"]),
                 bqh=prep_scol(I["L0_attn_bq"]), bkh=prep_scol(I["L0_attn_bk"]),
                 bvv=I["L0_attn_bv"], bov=I["L0_attn_bo"], ident=IDENT)
    l1f_w = dict(w1b=np.stack([prep_b_eo(I["L1_ffn_W1"][e]) for e in range(8)]),
                 w2a=np.stack([prep_a(I["L1_ffn_W2"][e]) for e in range(8)]),
                 wrt=prep_a(I["L1_ffn_Wr"]),
                 b1h=np.concatenate([prep_scol(I["L1_ffn_b1"][e]) for e in range(8)], axis=1),
                 b2h=I["L1_ffn_b2"], ident=IDENT)
    wq1 = np.ascontiguousarray(I["L1_attn_Wq"].transpose(0, 2, 1).reshape(1024, 1024))
    wk1 = np.ascontiguousarray(I["L1_attn_Wk"].transpose(0, 2, 1).reshape(1024, 1024))
    wv1 = np.ascontiguousarray(I["L1_attn_Wv"].transpose(0, 2, 1).reshape(1024, 1024))
    wo1 = np.ascontiguousarray(I["L1_attn_Wo"].reshape(1024, 1024))
    l1a_w = dict(wqb=prep_b(wq1), wkb=prep_b(wk1), wva=prep_a(wv1), woa=prep_a(wo1.T),
                 bqh=prep_scol(I["L1_attn_bq"].reshape(-1)),
                 bkh=prep_scol(I["L1_attn_bk"].reshape(-1)),
                 bvv=I["L1_attn_bv"].reshape(-1), bov=I["L1_attn_bo"],
                 cmask=CMASK, ident=IDENT)
    l2f_wr = prep_a(I["L2_ffn_Wr"].T)
    l2f_w1 = np.stack([prep_b_eo(I["L2_ffn_W1"][e].T) for e in range(8)])
    l2f_w2 = np.stack([prep_a(I["L2_ffn_W2"][e].T) for e in range(8)])
    l2a_w = dict(wkbh=prep_b(I["L2_attn_W_k"]), wvbh=prep_b(I["L2_attn_W_v"]),
                 wrbh=prep_b(I["L2_attn_W_r"]), woa=prep_a(I["L2_attn_W_o"]),
                 muk=prep_scol(I["L2_attn_time_mix_k"]), muv=prep_scol(I["L2_attn_time_mix_v"]),
                 mur=prep_scol(I["L2_attn_time_mix_r"]), wdec=prep_scol(I["L2_attn_time_decay"]),
                 ufir=prep_scol(I["L2_attn_time_first"]), ident=IDENT)
    l3f_w = dict(w1b=prep_b(I["L3_ffn_w1_w"]), w2b=prep_b(I["L3_ffn_w2_w"]),
                 h1b=prep_b(I["L3_ffn_h1_w"]), h2b=prep_b(I["L3_ffn_h2_w"]),
                 w3a=prep_a(I["L3_ffn_w3_w"]),
                 s1h=prep_scol(I["L3_ffn_w1_s"]), s2h=prep_scol(I["L3_ffn_w2_s"]),
                 hs1h=prep_scol(I["L3_ffn_h1_s"]), hs2h=prep_scol(I["L3_ffn_h2_s"]),
                 s3=I["L3_ffn_w3_s"], ident=IDENT)
    l3a_w = _prep_l3_attn(I)

    if FUSED:
        nc_f = build_fused()
        base = {}
        for pfx, wd in (("l0f_", l0f_w), ("l0a_", l0a_w), ("l1f_", l1f_w),
                        ("l1a_", l1a_w), ("l2a_", l2a_w), ("l3f_", l3f_w),
                        ("l3a_", l3a_w)):
            for k, v in wd.items():
                if k in ("ident", "cmask", "selpad"):
                    continue
                base[pfx + k] = v
        base["l2r_wrt"] = l2f_wr
        base["l2m_w1b"] = l2f_w1
        base["l2m_w2a"] = l2f_w2
        base["ident"] = IDENT
        base["cmask"] = CMASK
        base["triu"] = TRIU
        base["selpad"] = SELPAD
        xs = _shards(x)
        in_maps = []
        for c in cores:
            mask = np.zeros(8, np.float32)
            mask[0:c] = 1.0
            in_maps.append({**base, "x_in": xs[c],
                            "maskrep": np.repeat(mask, 8).reshape(1, 64)})
        res = run_bass_kernel_spmd(nc_f, in_maps, core_ids=cores)
        xo = np.concatenate([r["x_out"] for r in res.results], 0)
        return xo.reshape(B, S, D)

    def step(nc, w):
        nonlocal x
        xs = _shards(x)
        res = _run(nc, [{"x_in": xs[c], **w} for c in cores])
        x = np.concatenate([r["x_out"] for r in res], 0)

    def step_l2ffn():
        nonlocal x
        xs = _shards(x)
        r1 = _run(nc_l2r, [{"x_in": xs[c], "wrt": l2f_wr, "ident": IDENT, "triu": TRIU}
                           for c in cores])
        counts = np.stack([r["cnt_out"][0] for r in r1])
        im2 = []
        for c in cores:
            prevT = np.zeros((8, 8), np.float32)
            if c > 0:
                prevT[:, 0:c] = counts[0:c].T
            im2.append({"x_in": xs[c], "w1b": l2f_w1, "w2a": l2f_w2,
                        "oh_in": r1[c]["oh_out"], "pm_in": r1[c]["pm_out"],
                        "pos_in": r1[c]["pos_out"], "prevT": prevT.reshape(1, 64),
                        "ident": IDENT})
        res = _run(nc_l2m, im2)
        x = np.concatenate([r["x_out"] for r in res], 0)

    step(nc_l0f, l0f_w)
    step(nc_l0a, l0a_w)
    step(nc_l0f, l0f_w)
    step(nc_l1f, l1f_w)
    step(nc_l1a, l1a_w)
    step(nc_l1f, l1f_w)
    step_l2ffn()
    step(nc_l2a, l2a_w)
    step_l2ffn()
    step(nc_l3f, l3f_w)
    step(nc_l3a, l3a_w)
    step(nc_l3f, l3f_w)
    return x.reshape(B, S, D)





# version 07
# speedup vs baseline: 1.7273x; 1.0285x over previous
import contextlib
import numpy as np
import ml_dtypes
import concourse.bass as bass
import concourse.mybir as mybir
from concourse.bass_utils import run_bass_kernel_spmd

F32 = mybir.dt.float32
BF16 = mybir.dt.bfloat16
ALU = mybir.AluOpType
AF = mybir.ActivationFunctionType
AX = mybir.AxisListType

NCORES = 8
B = 16
S = 2048
D = 1024
TPC = (B // NCORES) * S
NT = TPC // 128
DCH = D // 128
EPS = 1e-5

SEMS = ["dy", "dg", "v", "s", "t", "g", "st", "cc"]
DEBUG = False


class Ctx:
    def __init__(self, nc, sems):
        self.nc = nc
        self.sems = sems
        self.cnt = {k: 0 for k in sems}
        self.cur = None
        self.eng = None
        self.pend = {}

    def begin(self, eng_name, eng):
        self.cur = eng_name
        self.eng = eng
        self.cnt = {k: 0 for k in self.sems}
        self.pend = {}

    def flush(self):
        for sem, vv in self.pend.items():
            self.eng.wait_ge(self.sems[sem], vv)
        self.pend = {}

    def op(self, eng_name, fn, incs=()):
        if self.cur == eng_name:
            pend = list(self.pend.items())
            self.pend = {}
            for sem, vv in pend[:-1]:
                self.eng.wait_ge(self.sems[sem], vv)
            ins = fn(self.eng)
            if pend:
                ins = ins._wait_ge(self.sems[pend[-1][0]], pend[-1][1])
            for s_, a in incs:
                ins = ins.then_inc(self.sems[s_], a)
        for s_, a in incs:
            self.cnt[s_] += a

    def v(self, fn):
        self.wait("vector", "v")
        self.op("vector", fn, incs=[("v", 1)])

    def s(self, fn):
        self.wait("scalar", "s")
        self.op("scalar", fn, incs=[("s", 1)])

    def t(self, fn):
        self.op("tensor", fn, incs=[("t", 1)])

    def g(self, fn):
        self.op("gpsimd", fn, incs=[("g", 1)])

    def v0(self, fn):
        self.wait("vector", "v")
        self.op("vector", fn, incs=[("v", 1)])

    def s0(self, fn):
        self.wait("scalar", "s")
        self.op("scalar", fn, incs=[("s", 1)])

    def t0(self, fn):
        self.op("tensor", fn)

    def vhaz(self):
        self.wait("vector", "v")

    def shaz(self):
        self.wait("scalar", "s")

    def dy(self, fn):
        self.op("sync", fn, incs=[("dy", 16)])

    def dg(self, fn):
        self.op("gpsimd", fn, incs=[("dg", 16)])

    def store(self, fn):
        self.op("sync", fn, incs=[("st", 16)])

    def wait(self, eng_name, sem, val=None):
        vv = self.cnt[sem] if val is None else val
        if self.cur == eng_name and vv > 0:
            if self.pend.get(sem, 0) < vv:
                self.pend[sem] = vv

    def now(self, sem):
        return self.cnt[sem]


def run_phases(nc, emit_fn):
    with contextlib.ExitStack() as st:
        sems = {s_: st.enter_context(nc.semaphore(s_)) for s_ in SEMS}
        ctx = Ctx(nc, sems)
        with nc.Block() as block:
            @block.sync
            def _(e):
                ctx.begin("sync", e)
                emit_fn(ctx)
                ctx.flush()

            @block.vector
            def _(e):
                ctx.begin("vector", e)
                emit_fn(ctx)
                ctx.flush()

            @block.scalar
            def _(e):
                ctx.begin("scalar", e)
                emit_fn(ctx)
                ctx.flush()

            @block.tensor
            def _(e):
                ctx.begin("tensor", e)
                emit_fn(ctx)
                ctx.flush()

            @block.gpsimd
            def _(e):
                ctx.begin("gpsimd", e)
                emit_fn(ctx)
                ctx.flush()


def run_stage(nc, ctx, emit_fn):
    snap = dict(ctx.cnt)
    with nc.Block() as block:
        def mk(name):
            def f(e):
                ctx.cur = name
                ctx.eng = e
                ctx.cnt = dict(snap)
                ctx.pend = {}
                emit_fn(ctx)
                ctx.flush()
            return f
        block.sync(mk("sync"))
        block.vector(mk("vector"))
        block.scalar(mk("scalar"))
        block.tensor(mk("tensor"))
        block.gpsimd(mk("gpsimd"))
    ctx.cur = None
    ctx.eng = None
    ctx.cnt = dict(snap)
    emit_fn(ctx)


def _mk():
    return bass.Bass("TRN2", target_bir_lowering=False)


def bcast_row(vec_ap, n):
    return vec_ap.unsqueeze(0).broadcast_to([128, n])



def emit_ln(c, x_sb, scr, ln_bf, lnT, pst, ident, ln_f32=None, rms=False,
            gamma_rep=None, d=D):
    cols = scr["cols"]
    xc = scr["xc"]
    dch = d // 128
    if not rms:
        c.v0(lambda e: e.reduce_sum(out=cols[:, 0:1], in_=x_sb[:], axis=AX.X))
        c.vhaz()
        c.v0(lambda e: e.tensor_scalar_mul(out=cols[:, 1:2], in0=cols[:, 0:1], scalar1=1.0 / d))
        c.vhaz()
        c.v(lambda e: e.tensor_scalar_sub(out=xc[:], in0=x_sb[:], scalar1=cols[:, 1:2]))
        src = xc
    else:
        src = x_sb
        c.wait("scalar", "dy", c.now("dy"))
        c.wait("scalar", "dg", c.now("dg"))
    c.wait("scalar", "v")
    c.s(lambda e: e.activation(out=scr["sq"][:], in_=src[:], func=AF.Square,
                               accum_out=cols[:, 2:3]))
    c.wait("vector", "s")
    c.v(lambda e: e.tensor_scalar(out=cols[:, 3:4], in0=cols[:, 2:3],
                                  scalar1=1.0 / d, scalar2=EPS,
                                  op0=ALU.mult, op1=ALU.add))
    c.wait("scalar", "v")
    c.s(lambda e: e.sqrt(out=cols[:, 4:5], in_=cols[:, 3:4]))
    c.wait("vector", "s")
    c.v0(lambda e: e.reciprocal(out=cols[:, 5:6], in_=cols[:, 4:5]))
    c.vhaz()
    if gamma_rep is not None:
        c.v0(lambda e: e.tensor_scalar_mul(out=scr["sq"][:], in0=src[:], scalar1=cols[:, 5:6]))
        c.v(lambda e: e.tensor_mul(out=ln_bf[:], in0=scr["sq"][:], in1=gamma_rep[:]))
        if ln_f32 is not None:
            c.v(lambda e: e.tensor_mul(out=ln_f32[:], in0=scr["sq"][:], in1=gamma_rep[:]))
    else:
        c.v(lambda e: e.tensor_scalar_mul(out=ln_bf[:], in0=src[:], scalar1=cols[:, 5:6]))
        if ln_f32 is not None:
            c.v(lambda e: e.tensor_scalar_mul(out=ln_f32[:], in0=src[:], scalar1=cols[:, 5:6]))
    if lnT is not None:
        c.wait("tensor", "v")
        c.wait("tensor", "s")
        for j in range(dch):
            inc = [("t", 1)] if j == dch - 1 else []
            c.op("tensor", lambda e, j=j: e.transpose(
                out=pst[:, j * 128:(j + 1) * 128],
                in_=ln_bf[:, j * 128:(j + 1) * 128], identity=ident[:]), incs=inc)
        c.wait("scalar", "t")
        c.s(lambda e: e.copy(out=lnT, in_=pst[:].rearrange("p (kc t) -> p kc t", kc=dch)
                             if lnT.ndim == 3 else pst[:]))



def build_l0_ffn(fio=None):
    if fio is None:
        nc = _mk()
        ctx = None
        io = None
        pfx = ""
    else:
        nc, ctx, io, pfx = fio
    x_in = nc.dram_tensor("x_in", [TPC, D], F32, kind="ExternalInput").ap() if fio is None else io["x_in"]
    wT = nc.dram_tensor("wT", [D, D], F32, kind="ExternalInput").ap() if fio is None else io["wT"]
    alpha = nc.dram_tensor("alpha", [1], F32, kind="ExternalInput").ap() if fio is None else io["alpha"]
    bias = nc.dram_tensor("bias", [D], F32, kind="ExternalInput").ap() if fio is None else io["bias"]
    ident_h = nc.dram_tensor("ident", [128, 128], BF16, kind="ExternalInput").ap() if fio is None else io["ident"]
    x_out = nc.dram_tensor("x_out", [TPC, D], F32, kind="ExternalOutput").ap() if fio is None else io["x_out"]
    scal_dram = nc.dram_tensor(pfx + "scal", [16], F32, kind="Internal").ap()

    with contextlib.ExitStack() as stk:
        e_ = stk.enter_context
        xt = [e_(nc.sbuf_tensor(pfx + f"xt{i}", [128, D], F32)) for i in range(2)]
        wf = e_(nc.sbuf_tensor(pfx + "wf", [128, DCH * D], F32))
        wtern = e_(nc.sbuf_tensor(pfx + "wtern", [128, DCH * D], BF16))
        ident = e_(nc.sbuf_tensor(pfx + "identt", [128, 128], BF16))
        xc = e_(nc.sbuf_tensor(pfx + "xc", [128, D], F32))
        sq = e_(nc.sbuf_tensor(pfx + "sq", [128, D], F32))
        cols = e_(nc.sbuf_tensor(pfx + "cols", [128, 16], F32))
        ln_bf = e_(nc.sbuf_tensor(pfx + "ln_bf", [128, D], BF16))
        lnT = e_(nc.sbuf_tensor(pfx + "lnT", [128, D], BF16))
        t1 = e_(nc.sbuf_tensor(pfx + "t1", [128, D], F32))
        t2 = e_(nc.sbuf_tensor(pfx + "t2", [128, D], F32))
        out1 = e_(nc.sbuf_tensor(pfx + "out1", [128, D], F32))
        acol = e_(nc.sbuf_tensor(pfx + "acol", [128, 1], F32))
        dcol = e_(nc.sbuf_tensor(pfx + "dcol", [128, 2], F32))
        brep = e_(nc.sbuf_tensor(pfx + "brep", [128, D], F32))
        ones = e_(nc.sbuf_tensor(pfx + "ones", [128, 1], F32))
        tot = e_(nc.sbuf_tensor(pfx + "tot", [1, 1], F32))
        pst = [e_(nc.psum_tensor(pfx + f"pst{i}", [128, D], BF16)) for i in range(2)]
        pso = [e_(nc.psum_tensor(pfx + "pso0", [128, D], F32))] * 2
        pred = e_(nc.psum_tensor(pfx + "pred", [1, 1], F32))

        def emit(c):
            c.dy(lambda e: e.dma_start(out=ident[:], in_=ident_h))
            c.dy(lambda e: e.dma_start(
                out=wf[:].rearrange("p (kc n) -> p kc n", kc=DCH),
                in_=wT.rearrange("(kc p) n -> p kc n", p=128)))
            c.dy(lambda e: e.dma_start(out=brep[:], in_=bcast_row(bias, D)))
            c.dy(lambda e: e.dma_start(out=acol[:], in_=alpha.unsqueeze(0).broadcast_to([128, 1])))
            c.wait("scalar", "dy")
            for j in range(DCH):
                inc = [("s", 1)] if j == DCH - 1 else []
                c.op("scalar", lambda e, j=j: e.activation(
                    out=sq[:], in_=wf[:, j * D:(j + 1) * D], func=AF.Abs,
                    accum_out=cols[:, j:j + 1]), incs=inc)
            c.wait("vector", "s")
            c.v0(lambda e: e.reduce_sum(out=dcol[:, 0:1], in_=cols[:, 0:DCH], axis=AX.X))
            c.vhaz()
            c.v(lambda e: e.memset(ones[:], 1.0))
            c.wait("tensor", "v")
            c.t(lambda e: e.matmul(out=pred[:], lhsT=dcol[:, 0:1], rhs=ones[:],
                                   start=True, stop=True))
            c.wait("vector", "t")
            c.v(lambda e: e.tensor_copy(out=tot[:], in_=pred[:]))
            c.wait("sync", "v")
            c.dy(lambda e: e.dma_start(out=scal_dram[0:1], in_=tot[0:1, 0:1].squeeze(1)))
            c.wait("sync", "dy")
            c.dy(lambda e: e.dma_start(out=dcol[:, 0:1],
                                       in_=scal_dram[0:1].unsqueeze(0).broadcast_to([128, 1])))
            c.wait("vector", "dy")
            c.v0(lambda e: e.tensor_scalar_mul(out=dcol[:, 0:1], in0=dcol[:, 0:1],
                                               scalar1=0.7 / (D * D)))
            c.vhaz()
            c.v(lambda e: e.tensor_scalar_mul(out=dcol[:, 1:2], in0=dcol[:, 0:1], scalar1=-1.0))
            c.vhaz()
            for j in range(DCH):
                c.v0(lambda e, j=j: e.tensor_scalar(
                    out=t1[:], in0=wf[:, j * D:(j + 1) * D], scalar1=dcol[:, 0:1],
                    scalar2=None, op0=ALU.is_gt))
                c.v0(lambda e, j=j: e.tensor_scalar(
                    out=t2[:], in0=wf[:, j * D:(j + 1) * D], scalar1=dcol[:, 1:2],
                    scalar2=None, op0=ALU.is_lt))
                c.v(lambda e, j=j: e.tensor_sub(
                    out=wtern[:, j * D:(j + 1) * D], in0=t1[:], in1=t2[:]))
            wq_v = c.now("v")

            scr = {"xc": xc, "sq": sq, "cols": cols}
            vmark = {-2: 0, -1: 0}
            stmark = {-2: 0, -1: 0}
            for i in range(NT):
                buf = xt[i % 2]
                c.wait("sync", "v", vmark[i - 2])
                c.wait("sync", "st", stmark[i - 2])
                c.dy(lambda e, i=i, buf=buf: e.dma_start(
                    out=buf[:], in_=x_in[i * 128:(i + 1) * 128, :]))
                c.wait("vector", "dy")
                c.wait("vector", "t")
                emit_ln(c, buf, scr, ln_bf, lnT[:], pst[i % 2], ident)
                c.wait("tensor", "s")
                c.wait("tensor", "v", wq_v)
                ps = pso[i % 2]
                for nh in range(2):
                    for j in range(DCH):
                        inc = [("t", 1)] if (nh == 1 and j == DCH - 1) else []
                        c.op("tensor", lambda e, nh=nh, j=j, ps=ps: e.matmul(
                            out=ps[:, nh * 512:(nh + 1) * 512],
                            lhsT=lnT[:, j * 128:(j + 1) * 128],
                            rhs=wtern[:, j * D + nh * 512:j * D + nh * 512 + 512],
                            start=(j == 0), stop=(j == DCH - 1)), incs=inc)
                c.wait("vector", "t")
                c.v(lambda e, ps=ps: e.scalar_tensor_tensor(
                    out=out1[:], in0=ps[:], scalar=acol[:, 0:1], in1=brep[:],
                    op0=ALU.mult, op1=ALU.add))
                c.v(lambda e, buf=buf: e.scalar_tensor_tensor(
                    out=buf[:], in0=out1[:], scalar=0.5, in1=buf[:],
                    op0=ALU.mult, op1=ALU.add))
                vmark[i] = c.now("v")
                c.wait("sync", "v", vmark[i])
                c.store(lambda e, i=i, buf=buf: e.dma_start(
                    out=x_out[i * 128:(i + 1) * 128, :], in_=buf[:]))
                stmark[i] = c.now("st")
            c.wait("sync", "st", NT * 16)

        if fio is None:
            run_phases(nc, emit)
        else:
            run_stage(nc, ctx, emit)
    return nc



ST = 512
NST = TPC // ST
HCH = 4096 // 128


def build_l3_ffn(fio=None):
    if fio is None:
        nc = _mk()
        ctx = None
        io = None
        pfx = ""
    else:
        nc, ctx, io, pfx = fio
    x_in = nc.dram_tensor("x_in", [TPC, D], F32, kind="ExternalInput").ap() if fio is None else io["x_in"]
    w1b = nc.dram_tensor("w1b", [HCH, 128, DCH, 128], F32, kind="ExternalInput").ap() if fio is None else io["w1b"]
    w2b = nc.dram_tensor("w2b", [HCH, 128, DCH, 128], F32, kind="ExternalInput").ap() if fio is None else io["w2b"]
    h1b = nc.dram_tensor("h1b", [HCH, 128, HCH, 128], F32, kind="ExternalInput").ap() if fio is None else io["h1b"]
    h2b = nc.dram_tensor("h2b", [HCH, 128, HCH, 128], F32, kind="ExternalInput").ap() if fio is None else io["h2b"]
    w3a = nc.dram_tensor("w3a", [128, HCH, D], F32, kind="ExternalInput").ap() if fio is None else io["w3a"]
    s1h = nc.dram_tensor("s1h", [128, HCH], F32, kind="ExternalInput").ap() if fio is None else io["s1h"]
    s2h = nc.dram_tensor("s2h", [128, HCH], F32, kind="ExternalInput").ap() if fio is None else io["s2h"]
    hs1h = nc.dram_tensor("hs1h", [128, HCH], F32, kind="ExternalInput").ap() if fio is None else io["hs1h"]
    hs2h = nc.dram_tensor("hs2h", [128, HCH], F32, kind="ExternalInput").ap() if fio is None else io["hs2h"]
    s3 = nc.dram_tensor("s3", [D], F32, kind="ExternalInput").ap() if fio is None else io["s3"]
    ident_h = nc.dram_tensor("ident", [128, 128], BF16, kind="ExternalInput").ap() if fio is None else io["ident"]
    x_out = nc.dram_tensor("x_out", [TPC, D], F32, kind="ExternalOutput").ap() if fio is None else io["x_out"]

    NTT = ST // 128

    with contextlib.ExitStack() as stk:
        e_ = stk.enter_context
        xst = e_(nc.sbuf_tensor(pfx + "xst", [128, NTT * D], F32))
        ident = e_(nc.sbuf_tensor(pfx + "identt", [128, 128], BF16))
        xc = e_(nc.sbuf_tensor(pfx + "xc", [128, D], F32))
        sq = e_(nc.sbuf_tensor(pfx + "sq", [128, D], F32))
        cols = e_(nc.sbuf_tensor(pfx + "cols", [128, 16], F32))
        ln_bf = e_(nc.sbuf_tensor(pfx + "ln_bf", [128, D], BF16))
        lnT = e_(nc.sbuf_tensor(pfx + "lnT", [128, DCH * ST], BF16))
        h1T = e_(nc.sbuf_tensor(pfx + "h1T", [128, HCH * ST], BF16))
        h2T = e_(nc.sbuf_tensor(pfx + "h2T", [128, HCH * ST], BF16))
        wb1 = [e_(nc.sbuf_tensor(pfx + f"wb1{i}", [128, DCH * 128], BF16)) for i in range(2)]
        wb2 = [e_(nc.sbuf_tensor(pfx + f"wb2{i}", [128, DCH * 128], BF16)) for i in range(2)]
        hb = [e_(nc.sbuf_tensor(pfx + f"hb{i}", [128, HCH * 128], BF16)) for i in range(2)]
        w3t = e_(nc.sbuf_tensor(pfx + "w3t", [128, HCH * D], BF16))
        scol = e_(nc.sbuf_tensor(pfx + "scol", [128, 4 * HCH], F32))
        s3rep = e_(nc.sbuf_tensor(pfx + "s3rep", [128, D], F32))
        tmpA = e_(nc.sbuf_tensor(pfx + "tmpA", [128, ST], F32))
        outw = e_(nc.sbuf_tensor(pfx + "outw", [128, D], F32))
        pst = e_(nc.psum_tensor(pfx + "pst", [128, D], BF16))
        psA = [e_(nc.psum_tensor(pfx + f"psA{i}", [128, ST], F32)) for i in range(2)]
        psB = [e_(nc.psum_tensor(pfx + f"psB{i}", [128, ST], F32)) for i in range(2)]
        pso = [e_(nc.psum_tensor(pfx + "pso0", [128, D], F32))] * 2

        def emit(c):
            c.dy(lambda e: e.dma_start(out=ident[:], in_=ident_h))
            c.dy(lambda e: e.dma_start(out=scol[:, 0:HCH], in_=s1h))
            c.dy(lambda e: e.dma_start(out=scol[:, HCH:2 * HCH], in_=s2h))
            c.dy(lambda e: e.dma_start(out=scol[:, 2 * HCH:3 * HCH], in_=hs1h))
            c.dy(lambda e: e.dma_start(out=scol[:, 3 * HCH:4 * HCH], in_=hs2h))
            c.dy(lambda e: e.dma_start(out=s3rep[:], in_=bcast_row(s3, D)))
            c.dg(lambda e: e.dma_start(
                out=w3t[:].rearrange("p (kc n) -> p kc n", kc=HCH), in_=w3a))
            scr = {"xc": xc, "sq": sq, "cols": cols}
            lnT3 = lnT[:].rearrange("p (kc t) -> p kc t", kc=DCH)
            h1T3 = h1T[:].rearrange("p (kc t) -> p kc t", kc=HCH)
            h2T3 = h2T[:].rearrange("p (kc t) -> p kc t", kc=HCH)

            stage1_tmark = {}
            stage_marks = {}
            for sti in range(NST):
                for tl in range(NTT):
                    i = sti * NTT + tl
                    c.wait("sync", "st", c.now("st"))
                    c.dy(lambda e, i=i, tl=tl: e.dma_start(
                        out=xst[:, tl * D:(tl + 1) * D],
                        in_=x_in[i * 128:(i + 1) * 128, :]))
                    c.wait("vector", "dy")
                    c.wait("vector", "s")
                    c.wait("vector", "t", stage_marks.get(sti - 1, 0))
                    emit_ln(c, xst[:, tl * D:(tl + 1) * D], scr, ln_bf,
                            lnT3[:, :, tl * 128:(tl + 1) * 128], pst, ident)
                ln_v = c.now("v")
                ln_s = c.now("s")
                dgm = {}
                for och in range(HCH):
                    bi = och % 2
                    if och >= 2:
                        c.wait("gpsimd", "t")
                    c.dg(lambda e, och=och, bi=bi: e.dma_start(
                        out=wb1[bi][:].rearrange("p (kc n) -> p kc n", kc=DCH),
                        in_=w1b[och]))
                    c.dg(lambda e, och=och, bi=bi: e.dma_start(
                        out=wb2[bi][:].rearrange("p (kc n) -> p kc n", kc=DCH),
                        in_=w2b[bi if False else och]))
                    c.wait("tensor", "dg")
                    c.wait("tensor", "s", c.now("s"))
                    c.wait("tensor", "v", c.now("v"))
                    for j in range(DCH):
                        c.t0(lambda e, j=j, bi=bi, och=och: e.matmul(
                            out=psA[bi][:], lhsT=wb1[bi][:, j * 128:(j + 1) * 128],
                            rhs=lnT[:, j * ST:(j + 1) * ST],
                            start=(j == 0), stop=(j == DCH - 1)))
                    for j in range(DCH):
                        inc = [("t", 1)] if j == DCH - 1 else []
                        c.op("tensor", lambda e, j=j, bi=bi: e.matmul(
                            out=psB[bi][:], lhsT=wb2[bi][:, j * 128:(j + 1) * 128],
                            rhs=lnT[:, j * ST:(j + 1) * ST],
                            start=(j == 0), stop=(j == DCH - 1)), incs=inc)
                    c.wait("scalar", "t")
                    c.s(lambda e, bi=bi, och=och: e.activation(
                        out=tmpA[:], in_=psA[bi][:], func=AF.Silu,
                        scale=scol[:, och:och + 1]))
                    c.wait("vector", "s")
                    c.v(lambda e, bi=bi, och=och: e.scalar_tensor_tensor(
                        out=h1T3[:, och, :], in0=psB[bi][:],
                        scalar=scol[:, HCH + och:HCH + och + 1], in1=tmpA[:],
                        op0=ALU.mult, op1=ALU.mult))
                h1_v = c.now("v")
                for och in range(HCH):
                    bi = och % 2
                    if och >= 2:
                        c.wait("gpsimd", "t")
                    c.dg(lambda e, och=och, bi=bi: e.dma_start(
                        out=hb[bi][:].rearrange("p (kc n) -> p kc n", kc=HCH),
                        in_=h1b[och]))
                    c.wait("tensor", "dg")
                    c.wait("tensor", "v", c.now("v"))
                    c.wait("tensor", "s", c.now("s"))
                    for j in range(HCH):
                        inc = [("t", 1)] if j == HCH - 1 else []
                        c.op("tensor", lambda e, j=j, bi=bi: e.matmul(
                            out=psA[bi][:], lhsT=hb[bi][:, j * 128:(j + 1) * 128],
                            rhs=h1T[:, j * ST:(j + 1) * ST],
                            start=(j == 0), stop=(j == HCH - 1)), incs=inc)
                    c.wait("scalar", "t")
                    c.s(lambda e, bi=bi, och=och: e.activation(
                        out=h2T3[:, och, :], in_=psA[bi][:], func=AF.Gelu,
                        scale=scol[:, 2 * HCH + och:2 * HCH + och + 1]))
                h2_s = c.now("s")
                for och in range(HCH):
                    bi = och % 2
                    if och >= 2:
                        c.wait("gpsimd", "t")
                    c.dg(lambda e, och=och, bi=bi: e.dma_start(
                        out=hb[bi][:].rearrange("p (kc n) -> p kc n", kc=HCH),
                        in_=h2b[och]))
                    c.wait("tensor", "dg")
                    c.wait("tensor", "s", c.now("s"))
                    for j in range(HCH):
                        inc = [("t", 1)] if j == HCH - 1 else []
                        c.op("tensor", lambda e, j=j, bi=bi: e.matmul(
                            out=psA[bi][:], lhsT=hb[bi][:, j * 128:(j + 1) * 128],
                            rhs=h2T[:, j * ST:(j + 1) * ST],
                            start=(j == 0), stop=(j == HCH - 1)), incs=inc)
                    c.wait("scalar", "t")
                    c.s(lambda e, bi=bi, och=och: e.activation(
                        out=h1T3[:, och, :], in_=psA[bi][:], func=AF.Gelu,
                        scale=scol[:, 3 * HCH + och:3 * HCH + och + 1]))
                h3_s = c.now("s")
                c.wait("tensor", "dg", 16)
                for tl in range(NTT):
                    bi = tl % 2
                    c.wait("tensor", "s", h3_s)
                    c.wait("tensor", "v", c.now("v"))
                    for j in range(HCH):
                        for nh in range(2):
                            inc = [("t", 1)] if (j == HCH - 1 and nh == 1) else []
                            c.op("tensor", lambda e, j=j, nh=nh, bi=bi, tl=tl: e.matmul(
                                out=pso[bi][:, nh * 512:(nh + 1) * 512],
                                lhsT=h1T3[:, j, tl * 128:(tl + 1) * 128],
                                rhs=w3t[:, j * D + nh * 512:j * D + nh * 512 + 512],
                                start=(j == 0), stop=(j == HCH - 1)), incs=inc)
                    c.wait("vector", "t")
                    c.v(lambda e, bi=bi: e.tensor_mul(out=outw[:], in0=pso[bi][:], in1=s3rep[:]))
                    c.v(lambda e, tl=tl: e.scalar_tensor_tensor(
                        out=xst[:, tl * D:(tl + 1) * D], in0=outw[:], scalar=0.5,
                        in1=xst[:, tl * D:(tl + 1) * D], op0=ALU.mult, op1=ALU.add))
                    c.wait("sync", "v")
                    c.store(lambda e, sti=sti, tl=tl: e.dma_start(
                        out=x_out[(sti * NTT + tl) * 128:(sti * NTT + tl + 1) * 128, :],
                        in_=xst[:, tl * D:(tl + 1) * D]))
                stage_marks[sti] = c.now("t")
            c.wait("sync", "st", NST * NTT * 16)

        if fio is None:
            run_phases(nc, emit)
        else:
            run_stage(nc, ctx, emit)
    return nc



E2 = 2048
ECH = E2 // 128


def build_l0_attn(fio=None):
    if fio is None:
        nc = _mk()
        ctx = None
        io = None
        pfx = ""
    else:
        nc, ctx, io, pfx = fio
    x_in = nc.dram_tensor("x_in", [TPC, D], F32, kind="ExternalInput").ap() if fio is None else io["x_in"]
    wqb = nc.dram_tensor("wqb", [ECH, 128, DCH, 128], F32, kind="ExternalInput").ap() if fio is None else io["wqb"]
    wkb = nc.dram_tensor("wkb", [ECH, 128, DCH, 128], F32, kind="ExternalInput").ap() if fio is None else io["wkb"]
    wva = nc.dram_tensor("wva", [128, DCH, E2], F32, kind="ExternalInput").ap() if fio is None else io["wva"]
    woa = nc.dram_tensor("woa", [128, ECH, D], F32, kind="ExternalInput").ap() if fio is None else io["woa"]
    bqh = nc.dram_tensor("bqh", [128, ECH], F32, kind="ExternalInput").ap() if fio is None else io["bqh"]
    bkh = nc.dram_tensor("bkh", [128, ECH], F32, kind="ExternalInput").ap() if fio is None else io["bkh"]
    bvv = nc.dram_tensor("bvv", [E2], F32, kind="ExternalInput").ap() if fio is None else io["bvv"]
    bov = nc.dram_tensor("bov", [D], F32, kind="ExternalInput").ap() if fio is None else io["bov"]
    ident_h = nc.dram_tensor("ident", [128, 128], BF16, kind="ExternalInput").ap() if fio is None else io["ident"]
    x_out = nc.dram_tensor("x_out", [TPC, D], F32, kind="ExternalOutput").ap() if fio is None else io["x_out"]
    if DEBUG:
        d_kT = nc.dram_tensor("d_kT", [128, S], BF16, kind="ExternalOutput").ap()
        d_V = nc.dram_tensor("d_V", [128, E2], BF16, kind="ExternalOutput").ap()
        d_qT = nc.dram_tensor("d_qT", [128, ECH * 128], BF16, kind="ExternalOutput").ap()
        d_P = nc.dram_tensor("d_P", [128, S], BF16, kind="ExternalOutput").ap()
        d_o = nc.dram_tensor("d_o", [128, E2], BF16, kind="ExternalOutput").ap()
        d_cols = nc.dram_tensor("d_cols", [128, 16], F32, kind="ExternalOutput").ap()

    NTS = S // 128

    with contextlib.ExitStack() as stk:
        e_ = stk.enter_context
        xt = [e_(nc.sbuf_tensor(pfx + f"xt{i}", [128, D], F32)) for i in range(2)]
        ident = e_(nc.sbuf_tensor(pfx + "identt", [128, 128], BF16))
        xc = e_(nc.sbuf_tensor(pfx + "xc", [128, D], F32))
        sq = e_(nc.sbuf_tensor(pfx + "sq", [128, D], F32))
        cols = e_(nc.sbuf_tensor(pfx + "cols", [128, 16], F32))
        ln_bf = e_(nc.sbuf_tensor(pfx + "ln_bf", [128, D], BF16))
        lnT = e_(nc.sbuf_tensor(pfx + "lnT", [128, DCH * S], BF16))
        kT = e_(nc.sbuf_tensor(pfx + "kT", [128, ECH * S], BF16))
        V = e_(nc.sbuf_tensor(pfx + "V", [128, NTS * E2], BF16))
        wband = [e_(nc.sbuf_tensor(pfx + "wband0", [128, DCH * 128], BF16))] * 2
        wvband = [e_(nc.sbuf_tensor(pfx + "wvband0", [128, DCH * 512], BF16))] * 2
        woband = [e_(nc.sbuf_tensor(pfx + "woband0", [128, D], BF16))] * 2
        bqc = e_(nc.sbuf_tensor(pfx + "bqc", [128, ECH], F32))
        bkc = e_(nc.sbuf_tensor(pfx + "bkc", [128, ECH], F32))
        bvrep = e_(nc.sbuf_tensor(pfx + "bvrep", [128, E2], BF16))
        borep = e_(nc.sbuf_tensor(pfx + "borep", [128, D], F32))
        qT = e_(nc.sbuf_tensor(pfx + "qT", [128, ECH * 128], BF16))
        P = e_(nc.sbuf_tensor(pfx + "P", [128, S], BF16))
        PT = wvband[0][:, 0:S]
        obf = P
        oT = PT
        outw = xc
        psS = e_(nc.psum_tensor(pfx + "psS", [128, S], F32))
        pstP = e_(nc.psum_tensor(pfx + "pstP", [128, S], BF16))
        psF = e_(nc.psum_tensor(pfx + "psF", [128, D], F32))

        def emit(c):
            c.dy(lambda e: e.dma_start(out=ident[:], in_=ident_h))
            c.dy(lambda e: e.dma_start(out=bqc[:], in_=bqh))
            c.dy(lambda e: e.dma_start(out=bkc[:], in_=bkh))
            c.dg(lambda e: e.dma_start(out=bvrep[:], in_=bcast_row(bvv, E2)))
            c.dy(lambda e: e.dma_start(out=borep[:], in_=bcast_row(bov, D)))
            scr = {"xc": xc, "sq": sq, "cols": cols}
            lnT3 = lnT[:].rearrange("p (kc t) -> p kc t", kc=DCH)
            kT3 = kT[:].rearrange("p (ec t) -> p ec t", ec=ECH)
            V3 = V[:].rearrange("p (tt e) -> p tt e", tt=NTS)
            pst1 = pstP[:, 0:D]

            for sq_i in range(2):
                T0 = sq_i * S
                for tl in range(NTS):
                    i = T0 // 128 + tl
                    c.wait("sync", "st", c.now("st"))
                    c.wait("sync", "v", c.now("v"))
                    c.dy(lambda e, i=i, tl=tl: e.dma_start(
                        out=xt[tl % 2][:], in_=x_in[i * 128:(i + 1) * 128, :]))
                    c.wait("vector", "dy")
                    c.wait("vector", "t", c.now("t"))
                    emit_ln(c, xt[tl % 2], scr, ln_bf,
                            lnT3[:, :, tl * 128:(tl + 1) * 128], pst1, ident)
                ln_done_s = c.now("s")
                for nh in range(4):
                    bi = nh % 2
                    c.wait("gpsimd", "t", c.now("t"))
                    c.dg(lambda e, nh=nh, bi=bi: e.dma_start(
                        out=wvband[bi][:].rearrange("p (kc n) -> p kc n", kc=DCH),
                        in_=wva[:, :, nh * 512:(nh + 1) * 512]))
                    c.wait("tensor", "dg")
                    c.wait("tensor", "s", ln_done_s)
                    for tl in range(NTS):
                        c.wait("tensor", "v", c.now("v"))
                        for j in range(DCH):
                            inc = [("t", 1)] if j == DCH - 1 else []
                            c.op("tensor", lambda e, j=j, tl=tl, bi=bi: e.matmul(
                                out=psS[:, 0:512],
                                lhsT=lnT3[:, j, tl * 128:(tl + 1) * 128],
                                rhs=wvband[bi][:, j * 512:(j + 1) * 512],
                                start=(j == 0), stop=(j == DCH - 1)), incs=inc)
                        c.wait("vector", "t")
                        c.wait("vector", "dg")
                        c.v(lambda e, tl=tl, nh=nh: e.tensor_add(
                            out=V3[:, tl, nh * 512:(nh + 1) * 512],
                            in0=psS[:, 0:512], in1=bvrep[:, nh * 512:(nh + 1) * 512]))
                for ech in range(ECH):
                    bi = ech % 2
                    c.wait("gpsimd", "t", c.now("t"))
                    c.dg(lambda e, ech=ech, bi=bi: e.dma_start(
                        out=wband[bi][:].rearrange("p (kc n) -> p kc n", kc=DCH),
                        in_=wkb[ech]))
                    c.wait("tensor", "dg")
                    for nh in range(4):
                        c.wait("tensor", "v", c.now("v"))
                        for j in range(DCH):
                            inc = [("t", 1)] if j == DCH - 1 else []
                            c.op("tensor", lambda e, j=j, nh=nh, bi=bi: e.matmul(
                                out=psS[:, 512:1024],
                                lhsT=wband[bi][:, j * 128:(j + 1) * 128],
                                rhs=lnT3[:, j, nh * 512:(nh + 1) * 512],
                                start=(j == 0), stop=(j == DCH - 1)), incs=inc)
                        c.wait("vector", "t")
                        c.v(lambda e, ech=ech, nh=nh: e.tensor_scalar_add(
                            out=kT3[:, ech, nh * 512:(nh + 1) * 512],
                            in0=psS[:, 512:1024], in1=None) if False else
                            e.tensor_scalar_add(
                                out=kT3[:, ech, nh * 512:(nh + 1) * 512],
                                in0=psS[:, 512:1024], scalar1=bkc[:, ech:ech + 1]))
                for tl in range(NTS):
                    c.wait("tensor", "v", c.now("v"))
                    for ech in range(ECH):
                        bi = ech % 2
                        c.wait("gpsimd", "t", c.now("t"))
                        c.dg(lambda e, ech=ech, bi=bi: e.dma_start(
                            out=wband[bi][:].rearrange("p (kc n) -> p kc n", kc=DCH),
                            in_=wqb[ech]))
                        c.wait("tensor", "dg")
                        c.wait("tensor", "v", c.now("v"))
                        for j in range(DCH):
                            inc = [("t", 1)] if j == DCH - 1 else []
                            c.op("tensor", lambda e, j=j, bi=bi, tl=tl: e.matmul(
                                out=psF[:, 0:128],
                                lhsT=wband[bi][:, j * 128:(j + 1) * 128],
                                rhs=lnT3[:, j, tl * 128:(tl + 1) * 128],
                                start=(j == 0), stop=(j == DCH - 1)), incs=inc)
                        c.wait("vector", "t")
                        c.v(lambda e, ech=ech: e.tensor_scalar_add(
                            out=qT[:, ech * 128:(ech + 1) * 128], in0=psF[:, 0:128],
                            scalar1=bqc[:, ech:ech + 1]))
                    c.wait("tensor", "v", c.now("v"))
                    for nh in range(4):
                        for ech in range(ECH):
                            inc = [("t", 1)] if (nh == 3 and ech == ECH - 1) else []
                            c.op("tensor", lambda e, nh=nh, ech=ech: e.matmul(
                                out=psS[:, nh * 512:(nh + 1) * 512],
                                lhsT=qT[:, ech * 128:(ech + 1) * 128],
                                rhs=kT3[:, ech, nh * 512:(nh + 1) * 512],
                                start=(ech == 0), stop=(ech == ECH - 1)), incs=inc)
                    c.wait("vector", "t")
                    c.v0(lambda e: e.reduce_max(out=cols[:, 8:9], in_=psS[:], axis=AX.X))
                    c.vhaz()
                    c.v(lambda e: e.tensor_scalar_mul(out=cols[:, 9:10], in0=cols[:, 8:9],
                                                      scalar1=-1.0 / 32.0))
                    c.wait("scalar", "v")
                    c.shaz()
                    c.s(lambda e: e.activation(out=P[:], in_=psS[:], func=AF.Exp,
                                               scale=1.0 / 32.0, bias=cols[:, 9:10],
                                               accum_out=cols[:, 10:11]))
                    c.wait("vector", "s")
                    c.v0(lambda e: e.reciprocal(out=cols[:, 11:12], in_=cols[:, 10:11]))
                    c.vhaz()
                    c.wait("tensor", "s")
                    for tch in range(NTS):
                        inc = [("t", 1)] if tch == NTS - 1 else []
                        c.op("tensor", lambda e, tch=tch: e.transpose(
                            out=pstP[:, tch * 128:(tch + 1) * 128],
                            in_=P[:, tch * 128:(tch + 1) * 128], identity=ident[:]),
                            incs=inc)
                    c.wait("scalar", "t")
                    c.s(lambda e: e.copy(out=PT, in_=pstP[:]))
                    c.wait("tensor", "s")
                    for nh in range(4):
                        for tch in range(NTS):
                            inc = [("t", 1)] if (nh == 3 and tch == NTS - 1) else []
                            c.op("tensor", lambda e, nh=nh, tch=tch: e.matmul(
                                out=psS[:, nh * 512:(nh + 1) * 512],
                                lhsT=PT[:, tch * 128:(tch + 1) * 128],
                                rhs=V3[:, tch, nh * 512:(nh + 1) * 512],
                                start=(tch == 0), stop=(tch == NTS - 1)), incs=inc)
                    c.wait("vector", "t")
                    c.v(lambda e: e.tensor_scalar_mul(out=obf[:], in0=psS[:],
                                                      scalar1=cols[:, 11:12]))
                    if DEBUG and sq_i == 0 and tl == 0:
                        c.wait("sync", "v", c.now("v"))
                        c.wait("sync", "s", c.now("s"))
                        c.store(lambda e: e.dma_start(out=d_kT, in_=kT3[:, 0, :]))
                        c.store(lambda e: e.dma_start(out=d_V, in_=V3[:, 0, :]))
                        c.store(lambda e: e.dma_start(out=d_qT, in_=qT[:]))
                        c.store(lambda e: e.dma_start(out=d_P, in_=P[:]))
                        c.store(lambda e: e.dma_start(out=d_o, in_=obf[:]))
                        c.store(lambda e: e.dma_start(out=d_cols, in_=cols[:]))
                    c.wait("tensor", "v")
                    for ech in range(ECH):
                        inc = [("t", 1)] if ech == ECH - 1 else []
                        c.op("tensor", lambda e, ech=ech: e.transpose(
                            out=pstP[:, ech * 128:(ech + 1) * 128],
                            in_=obf[:, ech * 128:(ech + 1) * 128], identity=ident[:]),
                            incs=inc)
                    c.wait("scalar", "t")
                    c.s(lambda e: e.copy(out=oT, in_=pstP[:]))
                    ot_s = c.now("s")
                    for ech in range(ECH):
                        bi = ech % 2
                        c.wait("gpsimd", "t", c.now("t"))
                        c.dg(lambda e, ech=ech, bi=bi: e.dma_start(
                            out=woband[bi][:], in_=woa[:, ech, :]))
                        c.wait("tensor", "dg")
                        c.wait("tensor", "s", ot_s)
                        for nh in range(2):
                            inc = [("t", 1)] if nh == 1 else []
                            c.op("tensor", lambda e, ech=ech, nh=nh, bi=bi: e.matmul(
                                out=psF[:, nh * 512:(nh + 1) * 512],
                                lhsT=oT[:, ech * 128:(ech + 1) * 128],
                                rhs=woband[bi][:, nh * 512:(nh + 1) * 512],
                                start=(ech == 0), stop=(ech == ECH - 1)), incs=inc)
                    i = T0 // 128 + tl
                    c.wait("sync", "v", c.now("v"))
                    c.wait("sync", "st", c.now("st"))
                    c.dy(lambda e, i=i: e.dma_start(
                        out=xt[0][:], in_=x_in[i * 128:(i + 1) * 128, :]))
                    c.wait("vector", "dy")
                    c.wait("vector", "t")
                    c.v(lambda e: e.tensor_add(out=outw[:], in0=psF[:], in1=borep[:]))
                    c.v(lambda e: e.tensor_add(out=xt[0][:], in0=outw[:], in1=xt[0][:]))
                    c.wait("sync", "v")
                    c.store(lambda e, i=i: e.dma_start(
                        out=x_out[i * 128:(i + 1) * 128, :], in_=xt[0][:]))
            c.wait("sync", "st", c.now("st"))

        if fio is None:
            run_phases(nc, emit)
        else:
            run_stage(nc, ctx, emit)
    return nc



NH1 = 16
AH = 64


def build_l1_attn(fio=None):
    if fio is None:
        nc = _mk()
        ctx = None
        io = None
        pfx = ""
    else:
        nc, ctx, io, pfx = fio
    x_in = nc.dram_tensor("x_in", [TPC, D], F32, kind="ExternalInput").ap() if fio is None else io["x_in"]
    wqb = nc.dram_tensor("wqb", [DCH, 128, DCH, 128], F32, kind="ExternalInput").ap() if fio is None else io["wqb"]
    wkb = nc.dram_tensor("wkb", [DCH, 128, DCH, 128], F32, kind="ExternalInput").ap() if fio is None else io["wkb"]
    wva = nc.dram_tensor("wva", [128, DCH, D], F32, kind="ExternalInput").ap() if fio is None else io["wva"]
    woa = nc.dram_tensor("woa", [128, DCH, D], F32, kind="ExternalInput").ap() if fio is None else io["woa"]
    bqh = nc.dram_tensor("bqh", [128, DCH], F32, kind="ExternalInput").ap() if fio is None else io["bqh"]
    bkh = nc.dram_tensor("bkh", [128, DCH], F32, kind="ExternalInput").ap() if fio is None else io["bkh"]
    bvv = nc.dram_tensor("bvv", [D], F32, kind="ExternalInput").ap() if fio is None else io["bvv"]
    bov = nc.dram_tensor("bov", [D], F32, kind="ExternalInput").ap() if fio is None else io["bov"]
    cmask_h = nc.dram_tensor("cmask", [128, 128], F32, kind="ExternalInput").ap() if fio is None else io["cmask"]
    ident_h = nc.dram_tensor("ident", [128, 128], BF16, kind="ExternalInput").ap() if fio is None else io["ident"]
    x_out = nc.dram_tensor("x_out", [TPC, D], F32, kind="ExternalOutput").ap() if fio is None else io["x_out"]

    NTS = S // 128

    with contextlib.ExitStack() as stk:
        e_ = stk.enter_context
        xt = [e_(nc.sbuf_tensor(pfx + f"xt{i}", [128, D], F32)) for i in range(2)]
        ident = e_(nc.sbuf_tensor(pfx + "identt", [128, 128], BF16))
        cmask = e_(nc.sbuf_tensor(pfx + "cmaskt", [128, 128], F32))
        xc = e_(nc.sbuf_tensor(pfx + "xc", [128, D], F32))
        sq = e_(nc.sbuf_tensor(pfx + "sq", [128, D], F32))
        cols = e_(nc.sbuf_tensor(pfx + "cols", [128, 16], F32))
        ln_bf = e_(nc.sbuf_tensor(pfx + "ln_bf", [128, D], BF16))
        lnT = e_(nc.sbuf_tensor(pfx + "lnT", [128, DCH * S], BF16))
        qT = e_(nc.sbuf_tensor(pfx + "qTs", [128, DCH * S], BF16))
        kT = e_(nc.sbuf_tensor(pfx + "kTs", [128, DCH * S], BF16))
        V = e_(nc.sbuf_tensor(pfx + "V", [128, NTS * D], BF16))
        woT = e_(nc.sbuf_tensor(pfx + "woT", [128, DCH * D], BF16))
        wband = [e_(nc.sbuf_tensor(pfx + "wband0", [128, DCH * 128], BF16))] * 2
        wvband = [e_(nc.sbuf_tensor(pfx + "wvband0", [128, DCH * 512], BF16))] * 2
        bqc = e_(nc.sbuf_tensor(pfx + "bqc", [128, DCH], F32))
        bkc = e_(nc.sbuf_tensor(pfx + "bkc", [128, DCH], F32))
        bvrep = e_(nc.sbuf_tensor(pfx + "bvrep", [128, D], BF16))
        borep = e_(nc.sbuf_tensor(pfx + "borep", [128, D], F32))
        P = e_(nc.sbuf_tensor(pfx + "P", [128, S], BF16))
        PT = e_(nc.sbuf_tensor(pfx + "PT", [128, S], BF16))
        Z = e_(nc.sbuf_tensor(pfx + "Z", [128, D], BF16))
        zT = e_(nc.sbuf_tensor(pfx + "zT", [128, D], BF16))
        psS = e_(nc.psum_tensor(pfx + "psS", [128, S], F32))
        pstP = e_(nc.psum_tensor(pfx + "pstP", [128, S], BF16))
        psF = e_(nc.psum_tensor(pfx + "psF", [128, D], F32))

        def emit(c):
            c.dy(lambda e: e.dma_start(out=ident[:], in_=ident_h))
            c.dy(lambda e: e.dma_start(out=cmask[:], in_=cmask_h))
            c.dy(lambda e: e.dma_start(out=bqc[:], in_=bqh))
            c.dy(lambda e: e.dma_start(out=bkc[:], in_=bkh))
            c.dg(lambda e: e.dma_start(out=bvrep[:], in_=bcast_row(bvv, D)))
            c.dy(lambda e: e.dma_start(out=borep[:], in_=bcast_row(bov, D)))
            c.dg(lambda e: e.dma_start(
                out=woT[:].rearrange("p (kc n) -> p kc n", kc=DCH), in_=woa))
            scr = {"xc": xc, "sq": sq, "cols": cols}
            lnT3 = lnT[:].rearrange("p (kc t) -> p kc t", kc=DCH)
            qT3 = qT[:].rearrange("p (kc t) -> p kc t", kc=DCH)
            kT3 = kT[:].rearrange("p (kc t) -> p kc t", kc=DCH)
            V3 = V[:].rearrange("p (tt e) -> p tt e", tt=NTS)
            pst1 = pstP[:, 0:D]

            for sq_i in range(2):
                T0 = sq_i * S
                for tl in range(NTS):
                    i = T0 // 128 + tl
                    c.wait("sync", "st", c.now("st"))
                    c.wait("sync", "v", c.now("v"))
                    c.dy(lambda e, i=i, tl=tl: e.dma_start(
                        out=xt[tl % 2][:], in_=x_in[i * 128:(i + 1) * 128, :]))
                    c.wait("vector", "dy")
                    c.wait("vector", "t", c.now("t"))
                    emit_ln(c, xt[tl % 2], scr, ln_bf,
                            lnT3[:, :, tl * 128:(tl + 1) * 128], pst1, ident)
                ln_done_s = c.now("s")
                for nh in range(2):
                    bi = nh % 2
                    c.wait("gpsimd", "t", c.now("t"))
                    c.dg(lambda e, nh=nh, bi=bi: e.dma_start(
                        out=wvband[bi][:].rearrange("p (kc n) -> p kc n", kc=DCH),
                        in_=wva[:, :, nh * 512:(nh + 1) * 512]))
                    c.wait("tensor", "dg")
                    c.wait("tensor", "s", ln_done_s)
                    for tl in range(NTS):
                        c.wait("tensor", "v", c.now("v"))
                        for j in range(DCH):
                            inc = [("t", 1)] if j == DCH - 1 else []
                            c.op("tensor", lambda e, j=j, tl=tl, bi=bi: e.matmul(
                                out=psS[:, 0:512],
                                lhsT=lnT3[:, j, tl * 128:(tl + 1) * 128],
                                rhs=wvband[bi][:, j * 512:(j + 1) * 512],
                                start=(j == 0), stop=(j == DCH - 1)), incs=inc)
                        c.wait("vector", "t")
                        c.wait("vector", "dg")
                        c.v(lambda e, tl=tl, nh=nh: e.tensor_add(
                            out=V3[:, tl, nh * 512:(nh + 1) * 512],
                            in0=psS[:, 0:512], in1=bvrep[:, nh * 512:(nh + 1) * 512]))
                for which, wsrc, dst3, bcol in (
                        (0, wqb, qT3, bqc), (1, wkb, kT3, bkc)):
                    for ech in range(DCH):
                        bi = ech % 2
                        c.wait("gpsimd", "t", c.now("t"))
                        c.dg(lambda e, ech=ech, bi=bi, wsrc=wsrc: e.dma_start(
                            out=wband[bi][:].rearrange("p (kc n) -> p kc n", kc=DCH),
                            in_=wsrc[ech]))
                        c.wait("tensor", "dg")
                        for nh in range(4):
                            c.wait("tensor", "v", c.now("v"))
                            for j in range(DCH):
                                inc = [("t", 1)] if j == DCH - 1 else []
                                c.op("tensor", lambda e, j=j, nh=nh, bi=bi: e.matmul(
                                    out=psS[:, 512:1024],
                                    lhsT=wband[bi][:, j * 128:(j + 1) * 128],
                                    rhs=lnT3[:, j, nh * 512:(nh + 1) * 512],
                                    start=(j == 0), stop=(j == DCH - 1)), incs=inc)
                            c.wait("vector", "t")
                            c.v(lambda e, ech=ech, nh=nh, dst3=dst3, bcol=bcol: e.tensor_scalar_add(
                                out=dst3[:, ech, nh * 512:(nh + 1) * 512],
                                in0=psS[:, 512:1024], scalar1=bcol[:, ech:ech + 1]))
                for tl in range(NTS):
                    TQ = (tl + 1) * 128
                    nkt = tl + 1
                    for h in range(NH1):
                        ech = h // 2
                        po = 64 * (h % 2)
                        c.wait("tensor", "v", c.now("v"))
                        c.wait("tensor", "s", c.now("s"))
                        nch = (TQ + 511) // 512
                        for kc5 in range(nch):
                            t0c = kc5 * 512
                            t1c = min(TQ, t0c + 512)
                            inc = [("t", 1)] if kc5 == nch - 1 else []
                            c.op("tensor", lambda e, t0c=t0c, t1c=t1c, ech=ech, po=po, tl=tl: e.matmul(
                                out=psS[:, t0c:t1c],
                                lhsT=qT3[po:po + 64, ech, tl * 128:(tl + 1) * 128],
                                rhs=kT3[po:po + 64, ech, t0c:t1c],
                                start=True, stop=True), incs=inc)
                        c.wait("vector", "t")
                        c.v(lambda e, tl=tl: e.tensor_add(
                            out=psS[:, tl * 128:(tl + 1) * 128],
                            in0=psS[:, tl * 128:(tl + 1) * 128], in1=cmask[:]))
                        c.v0(lambda e, TQ=TQ: e.reduce_max(out=cols[:, 8:9],
                                                           in_=psS[:, 0:TQ], axis=AX.X))
                        c.vhaz()
                        c.v(lambda e: e.tensor_scalar_mul(out=cols[:, 9:10],
                                                          in0=cols[:, 8:9], scalar1=-1.0 / 8.0))
                        c.wait("scalar", "v")
                        c.shaz()
                        c.s(lambda e, TQ=TQ: e.activation(
                            out=P[:, 0:TQ], in_=psS[:, 0:TQ], func=AF.Exp,
                            scale=1.0 / 8.0, bias=cols[:, 9:10],
                            accum_out=cols[:, 10:11]))
                        c.wait("vector", "s")
                        c.v0(lambda e: e.reciprocal(out=cols[:, 11:12], in_=cols[:, 10:11]))
                        c.vhaz()
                        c.wait("tensor", "s")
                        for kt in range(nkt):
                            inc = [("t", 1)] if kt == nkt - 1 else []
                            c.op("tensor", lambda e, kt=kt: e.transpose(
                                out=pstP[:, kt * 128:(kt + 1) * 128],
                                in_=P[:, kt * 128:(kt + 1) * 128], identity=ident[:]),
                                incs=inc)
                        c.wait("scalar", "t")
                        c.s(lambda e, TQ=TQ: e.copy(out=PT[:, 0:TQ], in_=pstP[:, 0:TQ]))
                        c.wait("tensor", "s")
                        for kt in range(nkt):
                            inc = [("t", 1)] if kt == nkt - 1 else []
                            c.op("tensor", lambda e, kt=kt, h=h: e.matmul(
                                out=psF[:, 0:64],
                                lhsT=PT[:, kt * 128:(kt + 1) * 128],
                                rhs=V3[:, kt, h * 64:(h + 1) * 64],
                                start=(kt == 0), stop=(kt == nkt - 1)), incs=inc)
                        c.wait("vector", "t")
                        c.v(lambda e, h=h: e.tensor_scalar_mul(
                            out=Z[:, h * 64:(h + 1) * 64], in0=psF[:, 0:64],
                            scalar1=cols[:, 11:12]))
                    c.wait("tensor", "v")
                    for j in range(DCH):
                        inc = [("t", 1)] if j == DCH - 1 else []
                        c.op("tensor", lambda e, j=j: e.transpose(
                            out=pstP[:, j * 128:(j + 1) * 128],
                            in_=Z[:, j * 128:(j + 1) * 128], identity=ident[:]), incs=inc)
                    c.wait("scalar", "t")
                    c.s(lambda e: e.copy(out=zT[:], in_=pstP[:, 0:D]))
                    c.wait("tensor", "s", c.now("s"))
                    c.wait("tensor", "dg", 32)
                    for j in range(DCH):
                        for nh in range(2):
                            inc = [("t", 1)] if (j == DCH - 1 and nh == 1) else []
                            c.op("tensor", lambda e, j=j, nh=nh: e.matmul(
                                out=psF[:, nh * 512:(nh + 1) * 512],
                                lhsT=zT[:, j * 128:(j + 1) * 128],
                                rhs=woT[:, j * D + nh * 512:j * D + nh * 512 + 512],
                                start=(j == 0), stop=(j == DCH - 1)), incs=inc)
                    i = T0 // 128 + tl
                    c.wait("sync", "v", c.now("v"))
                    c.wait("sync", "st", c.now("st"))
                    c.dy(lambda e, i=i: e.dma_start(
                        out=xt[0][:], in_=x_in[i * 128:(i + 1) * 128, :]))
                    c.wait("vector", "dy")
                    c.wait("vector", "t")
                    c.v(lambda e: e.tensor_add(out=xc[:], in0=psF[:], in1=borep[:]))
                    c.v(lambda e: e.tensor_add(out=xt[0][:], in0=xc[:], in1=xt[0][:]))
                    c.wait("sync", "v")
                    c.store(lambda e, i=i: e.dma_start(
                        out=x_out[i * 128:(i + 1) * 128, :], in_=xt[0][:]))
            c.wait("sync", "st", c.now("st"))

        if fio is None:
            run_phases(nc, emit)
        else:
            run_stage(nc, ctx, emit)
    return nc



CH2 = 1024
E30 = float(np.exp(-30.0))


def build_l2_attn(fio=None):
    if fio is None:
        nc = _mk()
        ctx = None
        io = None
        pfx = ""
    else:
        nc, ctx, io, pfx = fio
    x_in = nc.dram_tensor("x_in", [TPC, D], F32, kind="ExternalInput").ap() if fio is None else io["x_in"]
    wkbh = nc.dram_tensor("wkbh", [DCH, 128, DCH, 128], F32, kind="ExternalInput").ap() if fio is None else io["wkbh"]
    wvbh = nc.dram_tensor("wvbh", [DCH, 128, DCH, 128], F32, kind="ExternalInput").ap() if fio is None else io["wvbh"]
    wrbh = nc.dram_tensor("wrbh", [DCH, 128, DCH, 128], F32, kind="ExternalInput").ap() if fio is None else io["wrbh"]
    woa = nc.dram_tensor("woa", [128, DCH, D], F32, kind="ExternalInput").ap() if fio is None else io["woa"]
    muk = nc.dram_tensor("muk", [128, DCH], F32, kind="ExternalInput").ap() if fio is None else io["muk"]
    muv = nc.dram_tensor("muv", [128, DCH], F32, kind="ExternalInput").ap() if fio is None else io["muv"]
    mur = nc.dram_tensor("mur", [128, DCH], F32, kind="ExternalInput").ap() if fio is None else io["mur"]
    wdec = nc.dram_tensor("wdec", [128, DCH], F32, kind="ExternalInput").ap() if fio is None else io["wdec"]
    ufir = nc.dram_tensor("ufir", [128, DCH], F32, kind="ExternalInput").ap() if fio is None else io["ufir"]
    ident_h = nc.dram_tensor("ident", [128, 128], BF16, kind="ExternalInput").ap() if fio is None else io["ident"]
    x_out = nc.dram_tensor("x_out", [TPC, D], F32, kind="ExternalOutput").ap() if fio is None else io["x_out"]

    NTT = CH2 // 128

    with contextlib.ExitStack() as stk:
        e_ = stk.enter_context
        xt = [e_(nc.sbuf_tensor(pfx + f"xt{i}", [128, D], F32)) for i in range(2)]
        ident = e_(nc.sbuf_tensor(pfx + "identt", [128, 128], BF16))
        xc = e_(nc.sbuf_tensor(pfx + "xc", [128, D], F32))
        sq = e_(nc.sbuf_tensor(pfx + "sq", [128, D], F32))
        cols = e_(nc.sbuf_tensor(pfx + "cols", [128, 16], F32))
        ln_bf = e_(nc.sbuf_tensor(pfx + "ln_bf", [128, D], BF16))
        lnTe = e_(nc.sbuf_tensor(pfx + "lnTe", [128, DCH * (CH2 + 1)], BF16))
        xkT = e_(nc.sbuf_tensor(pfx + "xkT", [128, DCH * CH2], BF16))
        xvT = e_(nc.sbuf_tensor(pfx + "xvT", [128, DCH * CH2], BF16))
        xrT = e_(nc.sbuf_tensor(pfx + "xrT", [128, DCH * CH2], BF16))
        kTf = e_(nc.sbuf_tensor(pfx + "kTf", [128, DCH * CH2], F32))
        vTf = e_(nc.sbuf_tensor(pfx + "vTf", [128, DCH * CH2], F32))
        rsig = xvT
        ybuf = xkT
        mucol = e_(nc.sbuf_tensor(pfx + "mucol", [128, 3 * DCH], F32))
        wcol = e_(nc.sbuf_tensor(pfx + "wcol", [128, DCH], F32))
        ucol = e_(nc.sbuf_tensor(pfx + "ucol", [128, DCH], F32))
        carry = e_(nc.sbuf_tensor(pfx + "carry", [128, 2 * DCH], F32))
        wrep = e_(nc.sbuf_tensor(pfx + "wrep", [128, CH2], F32))
        p_ext = e_(nc.sbuf_tensor(pfx + "p_ext", [128, CH2 + 1], F32))
        q_ext = e_(nc.sbuf_tensor(pfx + "q_ext", [128, CH2 + 1], F32))
        sa = e_(nc.sbuf_tensor(pfx + "sa", [128, CH2], F32))
        sb = e_(nc.sbuf_tensor(pfx + "sb", [128, CH2], F32))
        sm = e_(nc.sbuf_tensor(pfx + "sm", [128, CH2], F32))
        se1 = e_(nc.sbuf_tensor(pfx + "se1", [128, CH2], F32))
        snum = e_(nc.sbuf_tensor(pfx + "snum", [128, CH2], F32))
        st5 = e_(nc.sbuf_tensor(pfx + "st5", [128, CH2], F32))
        wband = [e_(nc.sbuf_tensor(pfx + "wband0", [128, DCH * 128], BF16))] * 2
        woT = e_(nc.sbuf_tensor(pfx + "woT", [128, DCH * D], BF16))
        pst1 = e_(nc.psum_tensor(pfx + "pst1", [128, D], BF16))
        psP = e_(nc.psum_tensor(pfx + "psP", [128, 512], F32))
        psF = e_(nc.psum_tensor(pfx + "psF", [128, D], F32))

        def emit(c):
            c.dy(lambda e: e.dma_start(out=ident[:], in_=ident_h))
            c.dy(lambda e: e.dma_start(out=mucol[:, 0:DCH], in_=muk))
            c.dy(lambda e: e.dma_start(out=mucol[:, DCH:2 * DCH], in_=muv))
            c.dy(lambda e: e.dma_start(out=mucol[:, 2 * DCH:3 * DCH], in_=mur))
            c.dy(lambda e: e.dma_start(out=wcol[:], in_=wdec))
            c.dy(lambda e: e.dma_start(out=ucol[:], in_=ufir))
            c.dg(lambda e: e.dma_start(
                out=woT[:].rearrange("p (kc n) -> p kc n", kc=DCH), in_=woa))
            scr = {"xc": xc, "sq": sq, "cols": cols}
            lnTe3 = lnTe[:].rearrange("p (kc t) -> p kc t", kc=DCH)
            xkT3 = xkT[:].rearrange("p (kc t) -> p kc t", kc=DCH)
            xvT3 = xvT[:].rearrange("p (kc t) -> p kc t", kc=DCH)
            xrT3 = xrT[:].rearrange("p (kc t) -> p kc t", kc=DCH)
            kTf3 = kTf[:].rearrange("p (kc t) -> p kc t", kc=DCH)
            vTf3 = vTf[:].rearrange("p (kc t) -> p kc t", kc=DCH)
            rs3 = rsig[:].rearrange("p (kc t) -> p kc t", kc=DCH)
            yb3 = ybuf[:].rearrange("p (kc t) -> p kc t", kc=DCH)

            for sq_i in range(2):
                for ch in range(S // CH2):
                    T0 = sq_i * S + ch * CH2
                    if ch == 0:
                        c.wait("vector", "s", c.now("s"))
                        c.wait("vector", "t", c.now("t"))
                        c.v(lambda e: e.memset(lnTe3[:, :, 0:1], 0.0))
                    else:
                        c.wait("vector", "t", c.now("t"))
                        c.v(lambda e: e.tensor_copy(out=lnTe3[:, :, 0:1],
                                                    in_=lnTe3[:, :, CH2:CH2 + 1]))
                    for tl in range(NTT):
                        i = T0 // 128 + tl
                        c.wait("sync", "st", c.now("st"))
                        c.wait("sync", "v", c.now("v"))
                        c.dy(lambda e, i=i, tl=tl: e.dma_start(
                            out=xt[tl % 2][:], in_=x_in[i * 128:(i + 1) * 128, :]))
                        c.wait("vector", "dy")
                        c.wait("vector", "t", c.now("t"))
                        emit_ln(c, xt[tl % 2], scr, ln_bf,
                                lnTe3[:, :, 1 + tl * 128:1 + (tl + 1) * 128], pst1, ident)
                    c.wait("vector", "s")
                    for kc in range(DCH):
                        for which, dst3, mof in ((0, xkT3, 0), (1, xvT3, DCH), (2, xrT3, 2 * DCH)):
                            c.v0(lambda e, kc=kc, dst3=dst3, mof=mof: e.tensor_scalar_mul(
                                out=dst3[:, kc, :], in0=lnTe3[:, kc, 1:CH2 + 1],
                                scalar1=mucol[:, mof + kc:mof + kc + 1]))
                    c.vhaz()
                    for kc in range(DCH):
                        for which, dst3, mof in ((0, xkT3, 0), (1, xvT3, DCH), (2, xrT3, 2 * DCH)):
                            c.v0(lambda e, kc=kc, dst3=dst3, mof=mof: e.scalar_tensor_tensor(
                                out=st5[:], in0=lnTe3[:, kc, 0:CH2],
                                scalar=mucol[:, mof + kc:mof + kc + 1],
                                in1=lnTe3[:, kc, 0:CH2], op0=ALU.mult, op1=ALU.subtract))
                            c.v0(lambda e, kc=kc, dst3=dst3: e.tensor_sub(
                                out=dst3[:, kc, :], in0=dst3[:, kc, :], in1=st5[:]))
                    c.op("vector", lambda e: e.tensor_copy(out=cols[:, 15:16], in_=cols[:, 15:16]),
                         incs=[("v", 1)])
                    mix_v = c.now("v")
                    for which, wsrc, ev in ((0, wkbh, 0), (1, wvbh, 1), (2, wrbh, 2)):
                        for ech in range(DCH):
                            bi = 0
                            c.wait("gpsimd", "t", c.now("t"))
                            c.dg(lambda e, ech=ech, wsrc=wsrc: e.dma_start(
                                out=wband[0][:].rearrange("p (kc n) -> p kc n", kc=DCH),
                                in_=wsrc[ech]))
                            src3 = (xkT3, xvT3, xrT3)[which]
                            for nh in range(2):
                                c.wait("tensor", "dg")
                                c.wait("tensor", "v", c.now("v"))
                                c.wait("tensor", "s", c.now("s"))
                                for j in range(DCH):
                                    inc = [("t", 1)] if j == DCH - 1 else []
                                    c.op("tensor", lambda e, j=j, nh=nh, src3=src3: e.matmul(
                                        out=psP[:],
                                        lhsT=wband[0][:, j * 128:(j + 1) * 128],
                                        rhs=src3[:, j, nh * 512:(nh + 1) * 512],
                                        start=(j == 0), stop=(j == DCH - 1)), incs=inc)
                                if which == 0:
                                    c.wait("vector", "t")
                                    c.v(lambda e, ech=ech, nh=nh: e.tensor_copy(
                                        out=kTf3[:, ech, nh * 512:(nh + 1) * 512], in_=psP[:]))
                                elif which == 1:
                                    c.wait("vector", "t")
                                    c.v(lambda e, ech=ech, nh=nh: e.tensor_copy(
                                        out=vTf3[:, ech, nh * 512:(nh + 1) * 512], in_=psP[:]))
                                else:
                                    c.wait("scalar", "t")
                                    c.s(lambda e, ech=ech, nh=nh: e.activation(
                                        out=rs3[:, ech, nh * 512:(nh + 1) * 512], in_=psP[:],
                                        func=AF.Sigmoid))
                    for ech in range(DCH):
                        kv = kTf3[:, ech, :]
                        vv = vTf3[:, ech, :]
                        c.wait("vector", "s", c.now("s"))
                        c.v0(lambda e: e.memset(wrep[:], 0.0))
                        c.vhaz()
                        c.v0(lambda e, ech=ech: e.tensor_scalar_add(
                            out=wrep[:], in0=wrep[:], scalar1=wcol[:, ech:ech + 1]))
                        if ch == 0:
                            c.v0(lambda e, ech=ech: e.memset(carry[:, ech:ech + 1], -1e30))
                            c.v0(lambda e, ech=ech: e.memset(carry[:, DCH + ech:DCH + ech + 1], 0.0))
                        c.vhaz()
                        c.v0(lambda e, kv=kv, ech=ech: e.tensor_tensor_scan(
                            out=p_ext[:, 1:CH2 + 1], data0=wrep[:], data1=kv,
                            initial=carry[:, ech:ech + 1], op0=ALU.add, op1=ALU.max))
                        c.vhaz()
                        c.v0(lambda e, ech=ech: e.tensor_copy(out=p_ext[:, 0:1],
                                                              in_=carry[:, ech:ech + 1]))
                        c.vhaz()
                        c.v0(lambda e: e.tensor_sub(out=sa[:], in0=p_ext[:, 0:CH2],
                                                    in1=p_ext[:, 1:CH2 + 1]))
                        c.v0(lambda e, ech=ech: e.tensor_scalar(
                            out=sa[:], in0=sa[:], scalar1=wcol[:, ech:ech + 1],
                            scalar2=-30.0, op0=ALU.add, op1=ALU.max))
                        c.wait("scalar", "v")
                        c.s(lambda e: e.activation(out=sa[:], in_=sa[:], func=AF.Exp))
                        c.wait("vector", "s")
                        c.v0(lambda e, kv=kv: e.tensor_sub(out=sb[:], in0=kv,
                                                           in1=p_ext[:, 1:CH2 + 1]))
                        c.v0(lambda e: e.tensor_scalar_max(out=sb[:], in0=sb[:], scalar1=-30.0))
                        c.wait("scalar", "v")
                        c.s(lambda e: e.activation(out=sb[:], in_=sb[:], func=AF.Exp))
                        c.wait("vector", "s")
                        c.v0(lambda e, vv=vv: e.tensor_mul(out=sb[:], in0=sb[:], in1=vv))
                        c.vhaz()
                        c.v0(lambda e, ech=ech: e.tensor_tensor_scan(
                            out=q_ext[:, 1:CH2 + 1], data0=sa[:], data1=sb[:],
                            initial=carry[:, DCH + ech:DCH + ech + 1], op0=ALU.mult, op1=ALU.add))
                        c.vhaz()
                        c.v0(lambda e, ech=ech: e.tensor_copy(out=q_ext[:, 0:1],
                                                              in_=carry[:, DCH + ech:DCH + ech + 1]))
                        c.vhaz()
                        c.v0(lambda e, ech=ech: e.tensor_copy(
                            out=carry[:, ech:ech + 1], in_=p_ext[:, CH2:CH2 + 1]))
                        c.v0(lambda e, ech=ech: e.tensor_copy(
                            out=carry[:, DCH + ech:DCH + ech + 1], in_=q_ext[:, CH2:CH2 + 1]))
                        c.v0(lambda e, kv=kv, ech=ech: e.tensor_scalar_add(
                            out=sm[:], in0=kv, scalar1=ucol[:, ech:ech + 1]))
                        c.v0(lambda e: e.tensor_max(out=sm[:], in0=sm[:], in1=p_ext[:, 0:CH2]))
                        c.v0(lambda e: e.tensor_sub(out=se1[:], in0=p_ext[:, 0:CH2], in1=sm[:]))
                        c.v0(lambda e: e.tensor_scalar_max(out=se1[:], in0=se1[:], scalar1=-30.0))
                        c.wait("scalar", "v")
                        c.s(lambda e: e.activation(out=se1[:], in_=se1[:], func=AF.Exp))
                        c.v0(lambda e, kv=kv, ech=ech: e.tensor_scalar_add(
                            out=st5[:], in0=kv, scalar1=ucol[:, ech:ech + 1]))
                        c.v0(lambda e: e.tensor_sub(out=st5[:], in0=st5[:], in1=sm[:]))
                        c.v0(lambda e: e.tensor_scalar_max(out=st5[:], in0=st5[:], scalar1=-30.0))
                        c.wait("scalar", "v")
                        c.s(lambda e: e.activation(out=st5[:], in_=st5[:], func=AF.Exp))
                        c.wait("vector", "s")
                        c.v0(lambda e, vv=vv: e.tensor_mul(out=snum[:], in0=st5[:], in1=vv))
                        c.v0(lambda e: e.tensor_mul(out=sm[:], in0=se1[:], in1=q_ext[:, 0:CH2]))
                        c.v0(lambda e: e.tensor_add(out=snum[:], in0=snum[:], in1=sm[:]))
                        c.v0(lambda e: e.tensor_add(out=se1[:], in0=se1[:], in1=st5[:]))
                        c.v0(lambda e: e.tensor_scalar_add(out=se1[:], in0=se1[:], scalar1=1e-9))
                        c.wait("scalar", "v")
                        c.s(lambda e: e.activation(out=se1[:], in_=se1[:], func=AF.Ln))
                        c.s(lambda e: e.activation(out=se1[:], in_=se1[:], func=AF.Exp, scale=-1.0))
                        c.wait("vector", "s")
                        c.v0(lambda e: e.tensor_mul(out=snum[:], in0=snum[:], in1=se1[:]))
                        c.v(lambda e, ech=ech: e.tensor_mul(
                            out=yb3[:, ech, :], in0=snum[:], in1=rs3[:, ech, :]))
                    y_v = c.now("v")
                    c.wait("tensor", "dg", 16)
                    for tl in range(NTT):
                        i = T0 // 128 + tl
                        c.wait("tensor", "v", c.now("v"))
                        for j in range(DCH):
                            for nh in range(2):
                                inc = [("t", 1)] if (j == DCH - 1 and nh == 1) else []
                                c.op("tensor", lambda e, j=j, nh=nh, tl=tl: e.matmul(
                                    out=psF[:, nh * 512:(nh + 1) * 512],
                                    lhsT=yb3[:, j, tl * 128:(tl + 1) * 128],
                                    rhs=woT[:, j * D + nh * 512:j * D + nh * 512 + 512],
                                    start=(j == 0), stop=(j == DCH - 1)), incs=inc)
                        c.wait("sync", "v", c.now("v"))
                        c.wait("sync", "st", c.now("st"))
                        c.dy(lambda e, i=i: e.dma_start(
                            out=xt[0][:], in_=x_in[i * 128:(i + 1) * 128, :]))
                        c.wait("vector", "dy")
                        c.wait("vector", "t")
                        c.v(lambda e: e.tensor_add(out=xt[0][:], in0=psF[:], in1=xt[0][:]))
                        c.wait("sync", "v")
                        c.store(lambda e, i=i: e.dma_start(
                            out=x_out[i * 128:(i + 1) * 128, :], in_=xt[0][:]))
            c.wait("sync", "st", c.now("st"))

        if fio is None:
            run_phases(nc, emit)
        else:
            run_stage(nc, ctx, emit)
    return nc



NEXP = 8
NTT_M = 4
MST = 512
NMST = TPC // MST


def build_l1_ffn(fio=None):
    if fio is None:
        nc = _mk()
        ctx = None
        io = None
        pfx = ""
    else:
        nc, ctx, io, pfx = fio
    x_in = nc.dram_tensor("x_in", [TPC, D], F32, kind="ExternalInput").ap() if fio is None else io["x_in"]
    w1b = nc.dram_tensor("w1b", [NEXP, 128, HCH, DCH, 128], F32, kind="ExternalInput").ap() if fio is None else io["w1b"]
    w2a = nc.dram_tensor("w2a", [NEXP, 128, HCH, D], F32, kind="ExternalInput").ap() if fio is None else io["w2a"]
    wrt = nc.dram_tensor("wrt", [128, DCH, NEXP], F32, kind="ExternalInput").ap() if fio is None else io["wrt"]
    b1h = nc.dram_tensor("b1h", [128, NEXP * HCH], F32, kind="ExternalInput").ap() if fio is None else io["b1h"]
    b2h = nc.dram_tensor("b2h", [NEXP, D], F32, kind="ExternalInput").ap() if fio is None else io["b2h"]
    ident_h = nc.dram_tensor("ident", [128, 128], BF16, kind="ExternalInput").ap() if fio is None else io["ident"]
    x_out = nc.dram_tensor("x_out", [TPC, D], F32, kind="ExternalOutput").ap() if fio is None else io["x_out"]
    ln_scr = nc.dram_tensor(pfx + "lnscr", [D, TPC], BF16, kind="Internal").ap()
    acc_scr = nc.dram_tensor(pfx + "accscr", [TPC, D], F32, kind="Internal").ap()

    TS = 512
    NTS_ = NT // 4

    with contextlib.ExitStack() as stk:
        e_ = stk.enter_context
        xt = [e_(nc.sbuf_tensor(pfx + f"xt{i}", [128, D], F32)) for i in range(2)]
        ident = e_(nc.sbuf_tensor(pfx + "identt", [128, 128], BF16))
        xc = e_(nc.sbuf_tensor(pfx + "xc", [128, D], F32))
        sq = e_(nc.sbuf_tensor(pfx + "sq", [128, D], F32))
        cols = e_(nc.sbuf_tensor(pfx + "cols", [128, 32], F32))
        ln_bf = e_(nc.sbuf_tensor(pfx + "ln_bf", [128, D], BF16))
        ln_f = e_(nc.sbuf_tensor(pfx + "ln_f", [128, D], F32))
        lo_bf = e_(nc.sbuf_tensor(pfx + "lo_bf", [128, D], BF16))
        lnT = e_(nc.sbuf_tensor(pfx + "lnT", [128, D], BF16))
        loT = e_(nc.sbuf_tensor(pfx + "loT", [128, D], BF16))
        lnsl = e_(nc.sbuf_tensor(pfx + "lnsl", [128, DCH * TS], BF16))
        w1sb = e_(nc.sbuf_tensor(pfx + "w1sb", [128, HCH * DCH * 128], BF16))
        w2sb = e_(nc.sbuf_tensor(pfx + "w2sb", [128, HCH * D], BF16))
        hT = e_(nc.sbuf_tensor(pfx + "hT", [128, HCH * TS], BF16))
        gate = e_(nc.sbuf_tensor(pfx + "gate", [128, NT * NEXP], F32))
        wrf = e_(nc.sbuf_tensor(pfx + "wrf", [128, DCH * NEXP], F32))
        wrhi = e_(nc.sbuf_tensor(pfx + "wrhi", [128, DCH * NEXP], BF16))
        wrlo = e_(nc.sbuf_tensor(pfx + "wrlo", [128, DCH * NEXP], BF16))
        wrtmp = e_(nc.sbuf_tensor(pfx + "wrtmp", [128, DCH * NEXP], F32))
        b1c = e_(nc.sbuf_tensor(pfx + "b1c", [128, NEXP * HCH], F32))
        b2rep = e_(nc.sbuf_tensor(pfx + "b2rep", [128, D], F32))
        zero8 = e_(nc.sbuf_tensor(pfx + "zero8", [128, NEXP], F32))
        r1 = e_(nc.sbuf_tensor(pfx + "r1", [128, NEXP], F32))
        r2 = e_(nc.sbuf_tensor(pfx + "r2", [128, NEXP], F32))
        acct = [xc, sq]
        outw = ln_f
        pst1 = e_(nc.psum_tensor(pfx + "pst1", [128, D], BF16))
        psH = [e_(nc.psum_tensor(pfx + f"psH{i}", [128, TS], F32)) for i in range(2)]
        psF = [e_(nc.psum_tensor(pfx + f"psF{i}", [128, D], F32)) for i in range(2)]
        psR = e_(nc.psum_tensor(pfx + "psR", [128, NEXP], F32))

        def emit(c):
            c.dy(lambda e: e.dma_start(out=ident[:], in_=ident_h))
            c.dy(lambda e: e.dma_start(out=b1c[:], in_=b1h))
            c.dy(lambda e: e.dma_start(
                out=wrf[:].rearrange("p (kc n) -> p kc n", kc=DCH), in_=wrt))
            c.wait("vector", "dy")
            c.v(lambda e: e.tensor_copy(out=wrhi[:], in_=wrf[:]))
            c.v(lambda e: e.tensor_copy(out=wrtmp[:], in_=wrhi[:]))
            c.v(lambda e: e.tensor_sub(out=wrtmp[:], in0=wrf[:], in1=wrtmp[:]))
            c.v(lambda e: e.tensor_copy(out=wrlo[:], in_=wrtmp[:]))
            c.v(lambda e: e.memset(zero8[:], 0.0))
            scr = {"xc": xc, "sq": sq, "cols": cols}
            lnT3 = lnT[:].rearrange("p (kc t) -> p kc t", kc=DCH)
            loT3 = loT[:].rearrange("p (kc t) -> p kc t", kc=DCH)
            ln_scr3 = ln_scr.rearrange("(kc p) t -> p kc t", p=128)
            lnsl3 = lnsl[:].rearrange("p (kc t) -> p kc t", kc=DCH)
            hT3 = hT[:].rearrange("p (kc t) -> p kc t", kc=HCH)
            w1s4 = w1sb[:].rearrange("p (o kc n) -> p o kc n", o=HCH, kc=DCH)

            for i in range(NT):
                c.wait("sync", "st", c.now("st"))
                c.wait("sync", "v", c.now("v"))
                c.dy(lambda e, i=i: e.dma_start(
                    out=xt[i % 2][:], in_=x_in[i * 128:(i + 1) * 128, :]))
                c.wait("vector", "dy")
                c.wait("vector", "t", c.now("t"))
                c.wait("vector", "st", c.now("st"))
                emit_ln(c, xt[i % 2], scr, ln_bf, lnT3[:, :, :], pst1, ident, ln_f32=ln_f)
                c.v(lambda e: e.tensor_copy(out=xc[:], in_=ln_bf[:]))
                c.v(lambda e: e.tensor_sub(out=xc[:], in0=ln_f[:], in1=xc[:]))
                c.v(lambda e: e.tensor_copy(out=lo_bf[:], in_=xc[:]))
                c.wait("tensor", "v")
                c.wait("tensor", "s")
                for j in range(DCH):
                    inc = [("t", 1)] if j == DCH - 1 else []
                    c.op("tensor", lambda e, j=j: e.transpose(
                        out=pst1[:, j * 128:(j + 1) * 128],
                        in_=lo_bf[:, j * 128:(j + 1) * 128], identity=ident[:]), incs=inc)
                c.wait("scalar", "t")
                c.s(lambda e: e.copy(out=loT3[:, :, :],
                                     in_=pst1[:].rearrange("p (kc t) -> p kc t", kc=DCH)))
                c.wait("sync", "s", c.now("s"))
                c.store(lambda e, i=i: e.dma_start(
                    out=ln_scr3[:, :, i * 128:(i + 1) * 128], in_=lnT3[:, :, :]))
                c.wait("tensor", "s")
                for pi, (lt, wt_) in enumerate(((lnT3, wrhi), (loT3, wrhi), (lnT3, wrlo))):
                    for j in range(DCH):
                        inc = [("t", 1)] if (pi == 2 and j == DCH - 1) else []
                        c.op("tensor", lambda e, j=j, lt=lt, wt_=wt_: e.matmul(
                            out=psR[:], lhsT=lt[:, j, :],
                            rhs=wt_[:, j * NEXP:(j + 1) * NEXP],
                            start=(pi == 0 and j == 0), stop=(pi == 2 and j == DCH - 1)),
                            incs=inc)
                c.wait("vector", "t")
                lg = sq[:, 0:NEXP]
                c.v(lambda e, lg=lg: e.tensor_copy(out=lg, in_=psR[:]))
                c.v(lambda e, lg=lg: e.reduce_max(out=cols[:, 16:17], in_=lg, axis=AX.X))
                c.v(lambda e, lg=lg: e.tensor_scalar(out=r1[:], in0=lg,
                                                     scalar1=cols[:, 16:17], scalar2=None,
                                                     op0=ALU.is_equal))
                c.v(lambda e: e.tensor_tensor_scan(out=r2[:], data0=r1[:], data1=zero8[:],
                                                   initial=0.0, op0=ALU.add, op1=ALU.max))
                c.v(lambda e: e.tensor_single_scalar(out=r2[:], in_=r2[:], scalar=1.0,
                                                     op=ALU.is_equal))
                c.v(lambda e: e.tensor_mul(out=r1[:], in0=r1[:], in1=r2[:]))
                c.v(lambda e, lg=lg: e.scalar_tensor_tensor(
                    out=r2[:], in0=r1[:], scalar=-1e30, in1=lg, op0=ALU.mult, op1=ALU.add))
                c.v(lambda e: e.reduce_max(out=cols[:, 17:18], in_=r2[:], axis=AX.X))
                c.v(lambda e: e.tensor_scalar(out=r2[:], in0=r2[:],
                                              scalar1=cols[:, 17:18], scalar2=None,
                                              op0=ALU.is_equal))
                c.v(lambda e: e.tensor_tensor_scan(out=sq[:, 8:16], data0=r2[:],
                                                   data1=zero8[:], initial=0.0,
                                                   op0=ALU.add, op1=ALU.max))
                c.v(lambda e: e.tensor_single_scalar(out=sq[:, 8:16], in_=sq[:, 8:16],
                                                     scalar=1.0, op=ALU.is_equal))
                c.v(lambda e: e.tensor_mul(out=r2[:], in0=r2[:], in1=sq[:, 8:16]))
                c.v(lambda e: e.tensor_sub(out=cols[:, 18:19], in0=cols[:, 16:17],
                                           in1=cols[:, 17:18]))
                c.wait("scalar", "v")
                c.s(lambda e: e.activation(out=cols[:, 19:20], in_=cols[:, 18:19],
                                           func=AF.Sigmoid))
                c.wait("vector", "s")
                c.v(lambda e: e.tensor_scalar(out=cols[:, 20:21], in0=cols[:, 19:20],
                                              scalar1=-1.0, scalar2=1.0,
                                              op0=ALU.mult, op1=ALU.add))
                c.v(lambda e, i=i: e.tensor_scalar_mul(
                    out=gate[:, i * NEXP:(i + 1) * NEXP], in0=r1[:],
                    scalar1=cols[:, 19:20]))
                c.v(lambda e, i=i: e.scalar_tensor_tensor(
                    out=gate[:, i * NEXP:(i + 1) * NEXP], in0=r2[:],
                    scalar=cols[:, 20:21], in1=gate[:, i * NEXP:(i + 1) * NEXP],
                    op0=ALU.mult, op1=ALU.add))

            for ex in range(NEXP):
                c.wait("gpsimd", "t", c.now("t"))
                c.dg(lambda e, ex=ex: e.dma_start(
                    out=w1s4[:, :, :, :], in_=w1b[ex]))
                c.dg(lambda e, ex=ex: e.dma_start(
                    out=w2sb[:].rearrange("p (kc n) -> p kc n", kc=HCH), in_=w2a[ex]))
                c.wait("gpsimd", "v", c.now("v"))
                c.dg(lambda e, ex=ex: e.dma_start(out=b2rep[:], in_=bcast_row(b2h[ex], D)))
                for ts in range(NTS_):
                    c.wait("sync", "t", c.now("t"))
                    c.dy(lambda e, ts=ts: e.dma_start(
                        out=lnsl3[:, :, :], in_=ln_scr3[:, :, ts * TS:(ts + 1) * TS]))
                    hmark = {}
                    for och in range(HCH):
                        bi = och % 2
                        c.wait("tensor", "dg")
                        c.wait("tensor", "dy")
                        c.wait("tensor", "s", hmark.get(och - 2, 0))
                        for j in range(DCH):
                            inc = [("t", 1)] if j == DCH - 1 else []
                            c.op("tensor", lambda e, j=j, bi=bi, och=och: e.matmul(
                                out=psH[bi][:], lhsT=w1s4[:, och, j, :],
                                rhs=lnsl3[:, j, :],
                                start=(j == 0), stop=(j == DCH - 1)), incs=inc)
                        c.wait("scalar", "t")
                        c.s(lambda e, och=och, ex=ex, bi=bi: e.activation(
                            out=hT3[:, och, :], in_=psH[bi][:], func=AF.Gelu,
                            bias=b1c[:, ex * HCH + och:ex * HCH + och + 1]))
                        hmark[och] = c.now("s")
                    h_s = c.now("s")
                    for tl in range(4):
                        gi = ts * 4 + tl
                        bi = tl % 2
                        if ex > 0:
                            c.wait("sync", "v", c.now("v"))
                            c.dy(lambda e, gi=gi, bi=bi: e.dma_start(
                                out=acct[bi][:], in_=acc_scr[gi * 128:(gi + 1) * 128, :]))
                        c.wait("tensor", "s", h_s)
                        c.wait("tensor", "v", c.now("v"))
                        for j in range(HCH):
                            for nh in range(2):
                                inc = [("t", 1)] if (j == HCH - 1 and nh == 1) else []
                                c.op("tensor", lambda e, j=j, nh=nh, tl=tl, bi=bi: e.matmul(
                                    out=psF[bi][:, nh * 512:(nh + 1) * 512],
                                    lhsT=hT3[:, j, tl * 128:(tl + 1) * 128],
                                    rhs=w2sb[:, j * D + nh * 512:j * D + nh * 512 + 512],
                                    start=(j == 0), stop=(j == HCH - 1)), incs=inc)
                        c.wait("vector", "t")
                        c.wait("vector", "dg")
                        c.wait("vector", "dy")
                        c.wait("vector", "st", c.now("st"))
                        c.v(lambda e, bi=bi: e.tensor_add(out=outw[:], in0=psF[bi][:],
                                                          in1=b2rep[:]))
                        if ex > 0:
                            c.v(lambda e, gi=gi, ex=ex, bi=bi: e.scalar_tensor_tensor(
                                out=acct[bi][:], in0=outw[:],
                                scalar=gate[:, gi * NEXP + ex:gi * NEXP + ex + 1],
                                in1=acct[bi][:], op0=ALU.mult, op1=ALU.add))
                        else:
                            c.v(lambda e, gi=gi, ex=ex, bi=bi: e.tensor_scalar_mul(
                                out=acct[bi][:], in0=outw[:],
                                scalar1=gate[:, gi * NEXP + ex:gi * NEXP + ex + 1]))
                        c.wait("sync", "v")
                        c.store(lambda e, gi=gi, bi=bi: e.dma_start(
                            out=acc_scr[gi * 128:(gi + 1) * 128, :], in_=acct[bi][:]))
            for i in range(NT):
                c.wait("sync", "st", c.now("st"))
                c.wait("sync", "v", c.now("v"))
                c.dy(lambda e, i=i: e.dma_start(
                    out=xt[0][:], in_=x_in[i * 128:(i + 1) * 128, :]))
                c.dy(lambda e, i=i: e.dma_start(
                    out=xt[1][:], in_=acc_scr[i * 128:(i + 1) * 128, :]))
                c.wait("vector", "dy")
                c.v(lambda e: e.scalar_tensor_tensor(
                    out=xt[0][:], in0=xt[1][:], scalar=0.5, in1=xt[0][:],
                    op0=ALU.mult, op1=ALU.add))
                c.wait("sync", "v")
                c.store(lambda e, i=i: e.dma_start(
                    out=x_out[i * 128:(i + 1) * 128, :], in_=xt[0][:]))
            c.wait("sync", "st", c.now("st"))

        if fio is None:
            run_phases(nc, emit)
        else:
            run_stage(nc, ctx, emit)
    return nc



CAP2 = int(1.25 * B * S / NEXP)


def build_l2_ffn_route(fio=None):
    if fio is None:
        nc = _mk()
        ctx = None
        io = None
        pfx = ""
    else:
        nc, ctx, io, pfx = fio
    x_in = nc.dram_tensor("x_in", [TPC, D], F32, kind="ExternalInput").ap() if fio is None else io["x_in"]
    wrt = nc.dram_tensor("wrt", [128, DCH, NEXP], F32, kind="ExternalInput").ap() if fio is None else io["wrt"]
    ident_h = nc.dram_tensor("ident", [128, 128], BF16, kind="ExternalInput").ap() if fio is None else io["ident"]
    triu_h = nc.dram_tensor("triu", [128, 128], F32, kind="ExternalInput").ap() if fio is None else io["triu"]
    oh_out = nc.dram_tensor("oh_out", [TPC, NEXP], F32, kind="ExternalOutput").ap() if fio is None else io["oh_out"]
    pm_out = nc.dram_tensor("pm_out", [TPC, 1], F32, kind="ExternalOutput").ap() if fio is None else io["pm_out"]
    pos_out = nc.dram_tensor("pos_out", [TPC, 1], F32, kind="ExternalOutput").ap() if fio is None else io["pos_out"]
    cnt_out = nc.dram_tensor("cnt_out", [1, NEXP], F32, kind="ExternalOutput").ap() if fio is None else io["cnt_out"]
    scr_dram = nc.dram_tensor(pfx + "scr", [16], F32, kind="Internal").ap()

    with contextlib.ExitStack() as stk:
        e_ = stk.enter_context
        xt = [e_(nc.sbuf_tensor(pfx + f"xt{i}", [128, D], F32)) for i in range(2)]
        ident = e_(nc.sbuf_tensor(pfx + "identt", [128, 128], BF16))
        triu = e_(nc.sbuf_tensor(pfx + "triut", [128, 128], F32))
        xc = e_(nc.sbuf_tensor(pfx + "xc", [128, D], F32))
        sq = e_(nc.sbuf_tensor(pfx + "sq", [128, D], F32))
        cols = e_(nc.sbuf_tensor(pfx + "cols", [128, 32], F32))
        ln_bf = e_(nc.sbuf_tensor(pfx + "ln_bf", [128, D], BF16))
        ln_f = e_(nc.sbuf_tensor(pfx + "ln_f", [128, D], F32))
        lo_bf = e_(nc.sbuf_tensor(pfx + "lo_bf", [128, D], BF16))
        lnT = e_(nc.sbuf_tensor(pfx + "lnT", [128, D], BF16))
        loT = e_(nc.sbuf_tensor(pfx + "loT", [128, D], BF16))
        wrf = e_(nc.sbuf_tensor(pfx + "wrf", [128, DCH * NEXP], F32))
        wrhi = e_(nc.sbuf_tensor(pfx + "wrhi", [128, DCH * NEXP], BF16))
        wrlo = e_(nc.sbuf_tensor(pfx + "wrlo", [128, DCH * NEXP], BF16))
        wrtmp = e_(nc.sbuf_tensor(pfx + "wrtmp", [128, DCH * NEXP], F32))
        lg = e_(nc.sbuf_tensor(pfx + "lg", [128, NEXP], F32))
        r1 = e_(nc.sbuf_tensor(pfx + "r1", [128, NEXP], F32))
        r2 = e_(nc.sbuf_tensor(pfx + "r2", [128, NEXP], F32))
        zero8 = e_(nc.sbuf_tensor(pfx + "zero8", [128, NEXP], F32))
        pref = e_(nc.sbuf_tensor(pfx + "pref", [128, NEXP], F32))
        brep = e_(nc.sbuf_tensor(pfx + "brep", [128, NEXP], F32))
        btot = e_(nc.sbuf_tensor(pfx + "btot", [1, NEXP], F32))
        ctot = e_(nc.sbuf_tensor(pfx + "ctot", [1, NEXP], F32))
        onesc = e_(nc.sbuf_tensor(pfx + "onesc", [128, 1], F32))
        pst1 = e_(nc.psum_tensor(pfx + "pst1", [128, D], BF16))
        psR = e_(nc.psum_tensor(pfx + "psR", [128, NEXP], F32))
        psPr = e_(nc.psum_tensor(pfx + "psPr", [128, NEXP], F32))
        psT = e_(nc.psum_tensor(pfx + "psT", [1, NEXP], F32))

        def emit(c):
            c.dy(lambda e: e.dma_start(out=ident[:], in_=ident_h))
            c.dy(lambda e: e.dma_start(out=triu[:], in_=triu_h))
            c.dy(lambda e: e.dma_start(
                out=wrf[:].rearrange("p (kc n) -> p kc n", kc=DCH), in_=wrt))
            c.wait("vector", "dy")
            c.v(lambda e: e.tensor_copy(out=wrhi[:], in_=wrf[:]))
            c.v(lambda e: e.tensor_copy(out=wrtmp[:], in_=wrhi[:]))
            c.v(lambda e: e.tensor_sub(out=wrtmp[:], in0=wrf[:], in1=wrtmp[:]))
            c.v(lambda e: e.tensor_copy(out=wrlo[:], in_=wrtmp[:]))
            c.v(lambda e: e.memset(zero8[:], 0.0))
            c.v(lambda e: e.memset(brep[:], 0.0))
            c.v(lambda e: e.memset(ctot[:], 0.0))
            c.v(lambda e: e.memset(onesc[:], 1.0))
            scr = {"xc": xc, "sq": sq, "cols": cols}
            lnT3 = lnT[:].rearrange("p (kc t) -> p kc t", kc=DCH)
            loT3 = loT[:].rearrange("p (kc t) -> p kc t", kc=DCH)

            for i in range(NT):
                c.wait("sync", "st", c.now("st"))
                c.wait("sync", "v", c.now("v"))
                c.dy(lambda e, i=i: e.dma_start(
                    out=xt[i % 2][:], in_=x_in[i * 128:(i + 1) * 128, :]))
                c.wait("vector", "dy")
                c.wait("vector", "t", c.now("t"))
                emit_ln(c, xt[i % 2], scr, ln_bf, lnT3[:, :, :], pst1, ident,
                        ln_f32=ln_f)
                c.v(lambda e: e.tensor_copy(out=xc[:], in_=ln_bf[:]))
                c.v(lambda e: e.tensor_sub(out=xc[:], in0=ln_f[:], in1=xc[:]))
                c.v(lambda e: e.tensor_copy(out=lo_bf[:], in_=xc[:]))
                c.wait("tensor", "v")
                c.wait("tensor", "s")
                for j in range(DCH):
                    inc = [("t", 1)] if j == DCH - 1 else []
                    c.op("tensor", lambda e, j=j: e.transpose(
                        out=pst1[:, j * 128:(j + 1) * 128],
                        in_=lo_bf[:, j * 128:(j + 1) * 128], identity=ident[:]), incs=inc)
                c.wait("scalar", "t")
                c.s(lambda e: e.copy(out=loT3[:, :, :],
                                     in_=pst1[:].rearrange("p (kc t) -> p kc t", kc=DCH)))
                c.wait("tensor", "s")
                for pi, (lt, wt_) in enumerate(((lnT3, wrhi), (loT3, wrhi), (lnT3, wrlo))):
                    for j in range(DCH):
                        inc = [("t", 1)] if (pi == 2 and j == DCH - 1) else []
                        c.op("tensor", lambda e, j=j, lt=lt, wt_=wt_: e.matmul(
                            out=psR[:], lhsT=lt[:, j, :],
                            rhs=wt_[:, j * NEXP:(j + 1) * NEXP],
                            start=(pi == 0 and j == 0), stop=(pi == 2 and j == DCH - 1)),
                            incs=inc)
                c.wait("vector", "t")
                c.v(lambda e: e.tensor_copy(out=lg[:], in_=psR[:]))
                c.v(lambda e: e.reduce_max(out=cols[:, 16:17], in_=lg[:], axis=AX.X))
                c.v(lambda e: e.tensor_scalar_mul(out=cols[:, 17:18], in0=cols[:, 16:17],
                                                  scalar1=-1.0))
                c.wait("scalar", "v")
                c.s(lambda e: e.activation(out=r2[:], in_=lg[:], func=AF.Exp,
                                           bias=cols[:, 17:18], accum_out=cols[:, 18:19]))
                c.wait("vector", "s")
                c.v(lambda e: e.reciprocal(out=cols[:, 19:20], in_=cols[:, 18:19]))
                c.v(lambda e: e.tensor_scalar(out=r1[:], in0=lg[:],
                                              scalar1=cols[:, 16:17], scalar2=None,
                                              op0=ALU.is_equal))
                c.v(lambda e: e.tensor_tensor_scan(out=r2[:], data0=r1[:], data1=zero8[:],
                                                   initial=0.0, op0=ALU.add, op1=ALU.max))
                c.v(lambda e: e.tensor_single_scalar(out=r2[:], in_=r2[:], scalar=1.0,
                                                     op=ALU.is_equal))
                c.v(lambda e: e.tensor_mul(out=r1[:], in0=r1[:], in1=r2[:]))
                c.wait("tensor", "v")
                c.t(lambda e: e.matmul(out=psPr[:], lhsT=triu[:], rhs=r1[:],
                                       start=True, stop=True))
                c.t(lambda e: e.matmul(out=psT[:], lhsT=onesc[:], rhs=r1[:],
                                       start=True, stop=True))
                c.wait("vector", "t")
                c.v(lambda e: e.tensor_add(out=pref[:], in0=psPr[:], in1=brep[:]))
                c.v(lambda e: e.tensor_mul(out=pref[:], in0=pref[:], in1=r1[:]))
                c.v(lambda e: e.reduce_sum(out=cols[:, 20:21], in_=pref[:], axis=AX.X))
                c.v(lambda e: e.tensor_add(out=ctot[:], in0=ctot[:], in1=psT[:]))
                c.v(lambda e: e.tensor_copy(out=btot[:], in_=psT[:]))
                c.wait("sync", "v")
                c.dy(lambda e: e.dma_start(out=scr_dram[0:NEXP].unsqueeze(0), in_=btot[0:1, :]))
                c.wait("sync", "dy")
                c.dy(lambda e: e.dma_start(
                    out=r2[:], in_=scr_dram[0:NEXP].unsqueeze(0).broadcast_to([128, NEXP])))
                c.wait("vector", "dy")
                c.v(lambda e: e.tensor_add(out=brep[:], in0=brep[:], in1=r2[:]))
                c.wait("sync", "v")
                c.store(lambda e, i=i: e.dma_start(
                    out=oh_out[i * 128:(i + 1) * 128, :], in_=r1[:]))
                c.store(lambda e, i=i: e.dma_start(
                    out=pm_out[i * 128:(i + 1) * 128, :], in_=cols[:, 19:20]))
                c.store(lambda e, i=i: e.dma_start(
                    out=pos_out[i * 128:(i + 1) * 128, :], in_=cols[:, 20:21]))
            c.wait("sync", "v")
            c.store(lambda e: e.dma_start(out=cnt_out, in_=ctot[:]))
            c.wait("sync", "st", c.now("st"))

        if fio is None:
            run_phases(nc, emit)
        else:
            run_stage(nc, ctx, emit)
    return nc


def build_l2_ffn_main(fio=None):
    if fio is None:
        nc = _mk()
        ctx = None
        io = None
        pfx = ""
    else:
        nc, ctx, io, pfx = fio
    x_in = nc.dram_tensor("x_in", [TPC, D], F32, kind="ExternalInput").ap() if fio is None else io["x_in"]
    w1b = nc.dram_tensor("w1b", [NEXP, 128, HCH, DCH, 128], F32, kind="ExternalInput").ap() if fio is None else io["w1b"]
    w2a = nc.dram_tensor("w2a", [NEXP, 128, HCH, D], F32, kind="ExternalInput").ap() if fio is None else io["w2a"]
    oh_in = nc.dram_tensor("oh_in", [TPC, NEXP], F32, kind="ExternalInput").ap() if fio is None else io["oh_in"]
    pm_in = nc.dram_tensor("pm_in", [TPC, 1], F32, kind="ExternalInput").ap() if fio is None else io["pm_in"]
    pos_in = nc.dram_tensor("pos_in", [TPC, 1], F32, kind="ExternalInput").ap() if fio is None else io["pos_in"]
    prevT = nc.dram_tensor("prevT", [1, NEXP * NCORES], F32, kind="ExternalInput").ap() if fio is None else None
    ident_h = nc.dram_tensor("ident", [128, 128], BF16, kind="ExternalInput").ap() if fio is None else io["ident"]
    x_out = nc.dram_tensor("x_out", [TPC, D], F32, kind="ExternalOutput").ap() if fio is None else io["x_out"]
    ln_scr = nc.dram_tensor(pfx + "lnscr", [D, TPC], BF16, kind="Internal").ap()
    acc_scr = nc.dram_tensor(pfx + "accscr", [TPC, D], F32, kind="Internal").ap()
    scr_dram = nc.dram_tensor(pfx + "scr", [16], F32, kind="Internal").ap()

    TS = 512
    NTS_ = NT // 4

    with contextlib.ExitStack() as stk:
        e_ = stk.enter_context
        xt = [e_(nc.sbuf_tensor(pfx + f"xt{i}", [128, D], F32)) for i in range(2)]
        ident = e_(nc.sbuf_tensor(pfx + "identt", [128, 128], BF16))
        xc = e_(nc.sbuf_tensor(pfx + "xc", [128, D], F32))
        sq = e_(nc.sbuf_tensor(pfx + "sq", [128, D], F32))
        cols = e_(nc.sbuf_tensor(pfx + "cols", [128, 32], F32))
        ln_bf = e_(nc.sbuf_tensor(pfx + "ln_bf", [128, D], BF16))
        ln_f = e_(nc.sbuf_tensor(pfx + "ln_f", [128, D], F32))
        lnT = e_(nc.sbuf_tensor(pfx + "lnT", [128, D], BF16))
        lnsl = e_(nc.sbuf_tensor(pfx + "lnsl", [128, DCH * TS], BF16))
        w1sb = e_(nc.sbuf_tensor(pfx + "w1sb", [128, HCH * DCH * 128], BF16))
        w2sb = e_(nc.sbuf_tensor(pfx + "w2sb", [128, HCH * D], BF16))
        hT = e_(nc.sbuf_tensor(pfx + "hT", [128, HCH * TS], BF16))
        gate = e_(nc.sbuf_tensor(pfx + "gate", [128, NT * NEXP], F32))
        ptc = e_(nc.sbuf_tensor(pfx + "ptc", [128, NT], F32))
        prevsb = e_(nc.sbuf_tensor(pfx + "prevsb", [1, NEXP * NCORES], F32))
        offrep = e_(nc.sbuf_tensor(pfx + "offrep", [128, NEXP], F32))
        ohm = e_(nc.sbuf_tensor(pfx + "ohm", [128, NEXP], F32))
        acct = [xc, sq]
        outw = ln_f
        pst1 = e_(nc.psum_tensor(pfx + "pst1", [128, D], BF16))
        psH = [e_(nc.psum_tensor(pfx + f"psH{i}", [128, TS], F32)) for i in range(2)]
        psF = [e_(nc.psum_tensor(pfx + f"psF{i}", [128, D], F32)) for i in range(2)]

        def emit(c):
            c.dy(lambda e: e.dma_start(out=ident[:], in_=ident_h))
            if fio is None:
                c.dy(lambda e: e.dma_start(out=prevsb[:], in_=prevT))
                c.wait("vector", "dy")
                c.v(lambda e: e.reduce_sum(
                    out=ohm[0:1, 0:NEXP],
                    in_=prevsb[:].rearrange("p (e c) -> p e c", e=NEXP), axis=AX.X))
            else:
                c.wait("gpsimd", "st", c.now("st"))
                c.op("gpsimd", lambda e: e.collective_compute(
                    "AllGather", ALU.bypass,
                    replica_groups=[[0, 1, 2, 3, 4, 5, 6, 7]],
                    ins=[io["cnt_loc"].opt()], outs=[io["cnt_all"].opt()]),
                    incs=[("cc", 1)])
                c.wait("sync", "cc", c.now("cc"))
                c.dy(lambda e: e.dma_start(
                    out=prevsb[:], in_=io["cnt_all"].rearrange("c e -> (c e)").unsqueeze(0)))
                c.dy(lambda e: e.dma_start(out=xc[0:1, 0:64], in_=io["maskrep"]))
                c.wait("vector", "dy")
                c.v(lambda e: e.tensor_mul(out=prevsb[:], in0=prevsb[:], in1=xc[0:1, 0:64]))
                c.v(lambda e: e.reduce_sum(
                    out=ohm[0:1, 0:NEXP],
                    in_=prevsb[:].rearrange("p (c e) -> p e c", c=NCORES), axis=AX.X))
            c.wait("sync", "v")
            c.dy(lambda e: e.dma_start(out=scr_dram[0:NEXP].unsqueeze(0), in_=ohm[0:1, 0:NEXP]))
            c.wait("sync", "dy")
            c.dy(lambda e: e.dma_start(
                out=offrep[:], in_=scr_dram[0:NEXP].unsqueeze(0).broadcast_to([128, NEXP])))
            scr = {"xc": xc, "sq": sq, "cols": cols}
            lnT3 = lnT[:].rearrange("p (kc t) -> p kc t", kc=DCH)
            ln_scr3 = ln_scr.rearrange("(kc p) t -> p kc t", p=128)
            lnsl3 = lnsl[:].rearrange("p (kc t) -> p kc t", kc=DCH)
            hT3 = hT[:].rearrange("p (kc t) -> p kc t", kc=HCH)
            w1s4 = w1sb[:].rearrange("p (o kc n) -> p o kc n", o=HCH, kc=DCH)

            for i in range(NT):
                c.wait("sync", "st", c.now("st"))
                c.wait("sync", "v", c.now("v"))
                c.dy(lambda e, i=i: e.dma_start(
                    out=xt[i % 2][:], in_=x_in[i * 128:(i + 1) * 128, :]))
                c.dy(lambda e, i=i: e.dma_start(
                    out=ohm[:], in_=oh_in[i * 128:(i + 1) * 128, :]))
                c.dy(lambda e, i=i: e.dma_start(
                    out=cols[:, 21:22], in_=pm_in[i * 128:(i + 1) * 128, :]))
                c.dy(lambda e, i=i: e.dma_start(
                    out=cols[:, 22:23], in_=pos_in[i * 128:(i + 1) * 128, :]))
                c.wait("vector", "dy")
                c.wait("vector", "t", c.now("t"))
                c.wait("vector", "st", c.now("st"))
                emit_ln(c, xt[i % 2], scr, ln_bf, lnT3[:, :, :], pst1, ident)
                c.wait("sync", "s", c.now("s"))
                c.store(lambda e, i=i: e.dma_start(
                    out=ln_scr3[:, :, i * 128:(i + 1) * 128], in_=lnT3[:, :, :]))
                c.v(lambda e: e.tensor_mul(out=sq[:, 0:NEXP], in0=ohm[:], in1=offrep[:]))
                c.v(lambda e: e.reduce_sum(out=cols[:, 23:24], in_=sq[:, 0:NEXP], axis=AX.X))
                c.v(lambda e: e.tensor_add(out=cols[:, 23:24], in0=cols[:, 23:24],
                                           in1=cols[:, 22:23]))
                c.v(lambda e: e.tensor_scalar(out=cols[:, 24:25], in0=cols[:, 23:24],
                                              scalar1=1.0, scalar2=float(CAP2),
                                              op0=ALU.subtract, op1=ALU.is_lt))
                c.v(lambda e: e.tensor_scalar(out=cols[:, 25:26], in0=cols[:, 24:25],
                                              scalar1=cols[:, 21:22], scalar2=None,
                                              op0=ALU.mult))
                c.v(lambda e, i=i: e.tensor_scalar_mul(
                    out=gate[:, i * NEXP:(i + 1) * NEXP], in0=ohm[:],
                    scalar1=cols[:, 25:26]))
                c.v(lambda e, i=i: e.tensor_scalar(
                    out=ptc[:, i:i + 1], in0=cols[:, 24:25],
                    scalar1=-1.0, scalar2=1.0, op0=ALU.mult, op1=ALU.add))
            for ex in range(NEXP):
                c.wait("gpsimd", "t", c.now("t"))
                c.dg(lambda e, ex=ex: e.dma_start(out=w1s4[:, :, :, :], in_=w1b[ex]))
                c.dg(lambda e, ex=ex: e.dma_start(
                    out=w2sb[:].rearrange("p (kc n) -> p kc n", kc=HCH), in_=w2a[ex]))
                for ts in range(NTS_):
                    c.wait("sync", "t", c.now("t"))
                    c.dy(lambda e, ts=ts: e.dma_start(
                        out=lnsl3[:, :, :], in_=ln_scr3[:, :, ts * TS:(ts + 1) * TS]))
                    hmark = {}
                    for och in range(HCH):
                        bi = och % 2
                        c.wait("tensor", "dg")
                        c.wait("tensor", "dy")
                        c.wait("tensor", "s", hmark.get(och - 2, 0))
                        for j in range(DCH):
                            inc = [("t", 1)] if j == DCH - 1 else []
                            c.op("tensor", lambda e, j=j, bi=bi, och=och: e.matmul(
                                out=psH[bi][:], lhsT=w1s4[:, och, j, :],
                                rhs=lnsl3[:, j, :],
                                start=(j == 0), stop=(j == DCH - 1)), incs=inc)
                        c.wait("scalar", "t")
                        c.s(lambda e, och=och, bi=bi: e.activation(
                            out=hT3[:, och, :], in_=psH[bi][:], func=AF.Relu))
                        hmark[och] = c.now("s")
                    h_s = c.now("s")
                    for tl in range(4):
                        gi = ts * 4 + tl
                        bi = tl % 2
                        if ex > 0:
                            c.wait("sync", "v", c.now("v"))
                            c.dy(lambda e, gi=gi, bi=bi: e.dma_start(
                                out=acct[bi][:], in_=acc_scr[gi * 128:(gi + 1) * 128, :]))
                        c.wait("tensor", "s", h_s)
                        c.wait("tensor", "v", c.now("v"))
                        for j in range(HCH):
                            for nh in range(2):
                                inc = [("t", 1)] if (j == HCH - 1 and nh == 1) else []
                                c.op("tensor", lambda e, j=j, nh=nh, tl=tl, bi=bi: e.matmul(
                                    out=psF[bi][:, nh * 512:(nh + 1) * 512],
                                    lhsT=hT3[:, j, tl * 128:(tl + 1) * 128],
                                    rhs=w2sb[:, j * D + nh * 512:j * D + nh * 512 + 512],
                                    start=(j == 0), stop=(j == HCH - 1)), incs=inc)
                        c.wait("vector", "t")
                        c.wait("vector", "dy")
                        c.wait("vector", "st", c.now("st"))
                        if ex > 0:
                            c.v(lambda e, gi=gi, ex=ex, bi=bi: e.scalar_tensor_tensor(
                                out=acct[bi][:], in0=psF[bi][:],
                                scalar=gate[:, gi * NEXP + ex:gi * NEXP + ex + 1],
                                in1=acct[bi][:], op0=ALU.mult, op1=ALU.add))
                        else:
                            c.v(lambda e, gi=gi, ex=ex, bi=bi: e.tensor_scalar_mul(
                                out=acct[bi][:], in0=psF[bi][:],
                                scalar1=gate[:, gi * NEXP + ex:gi * NEXP + ex + 1]))
                        c.wait("sync", "v")
                        c.store(lambda e, gi=gi, bi=bi: e.dma_start(
                            out=acc_scr[gi * 128:(gi + 1) * 128, :], in_=acct[bi][:]))
            for i in range(NT):
                c.wait("sync", "st", c.now("st"))
                c.wait("sync", "v", c.now("v"))
                c.wait("sync", "s", c.now("s"))
                c.dy(lambda e, i=i: e.dma_start(
                    out=xt[0][:], in_=x_in[i * 128:(i + 1) * 128, :]))
                c.dy(lambda e, i=i: e.dma_start(
                    out=xt[1][:], in_=acc_scr[i * 128:(i + 1) * 128, :]))
                c.wait("vector", "dy")
                c.wait("vector", "st", c.now("st"))
                emit_ln(c, xt[0], scr, ln_bf, None, None, ident, ln_f32=ln_f)
                c.v(lambda e, i=i: e.scalar_tensor_tensor(
                    out=xt[1][:], in0=ln_f[:], scalar=ptc[:, i:i + 1],
                    in1=xt[1][:], op0=ALU.mult, op1=ALU.add))
                c.v(lambda e: e.scalar_tensor_tensor(
                    out=xt[0][:], in0=xt[1][:], scalar=0.5, in1=xt[0][:],
                    op0=ALU.mult, op1=ALU.add))
                c.wait("sync", "v")
                c.store(lambda e, i=i: e.dma_start(
                    out=x_out[i * 128:(i + 1) * 128, :], in_=xt[0][:]))
            c.wait("sync", "st", c.now("st"))

        if fio is None:
            run_phases(nc, emit)
        else:
            run_stage(nc, ctx, emit)
    return nc



DI = 2048
ICH = DI // 128
TC3 = 512
NC3 = S // TC3
NDB = DI // 16


def build_l3_attn(fio=None):
    if fio is None:
        nc = _mk()
        ctx = None
        io = None
        pfx = ""
    else:
        nc, ctx, io, pfx = fio
    x_in = nc.dram_tensor("x_in", [TPC, D], F32, kind="ExternalInput").ap() if fio is None else io["x_in"]
    inwb = nc.dram_tensor("inwb", [2, 2 * ICH, 128, DCH, 128], F32, kind="ExternalInput").ap() if fio is None else io["inwb"]
    convw = nc.dram_tensor("convw", [2, 128, ICH * 4], F32, kind="ExternalInput").ap() if fio is None else io["convw"]
    convb = nc.dram_tensor("convb", [2, 128, ICH], F32, kind="ExternalInput").ap() if fio is None else io["convb"]
    xprjb = nc.dram_tensor("xprjb", [2, 2, 128, ICH, 128], F32, kind="ExternalInput").ap() if fio is None else io["xprjb"]
    dtwb = nc.dram_tensor("dtwb", [2, ICH, 128, 1, 128], F32, kind="ExternalInput").ap() if fio is None else io["dtwb"]
    dtbh = nc.dram_tensor("dtbh", [2, 128, ICH], F32, kind="ExternalInput").ap() if fio is None else io["dtbh"]
    alogt = nc.dram_tensor("alogt", [2, 2, 128, NDB], F32, kind="ExternalInput").ap() if fio is None else io["alogt"]
    dph = nc.dram_tensor("dph", [2, 128, ICH], F32, kind="ExternalInput").ap() if fio is None else io["dph"]
    outwa = nc.dram_tensor("outwa", [2, 128, ICH, D], F32, kind="ExternalInput").ap() if fio is None else io["outwa"]
    normw = nc.dram_tensor("normw", [2, D], F32, kind="ExternalInput").ap() if fio is None else io["normw"]
    normf = nc.dram_tensor("normf", [D], F32, kind="ExternalInput").ap() if fio is None else io["normf"]
    selpad_h = nc.dram_tensor("selpad", [128, 240], BF16, kind="ExternalInput").ap() if fio is None else io["selpad"]
    ident_h = nc.dram_tensor("ident", [128, 128], BF16, kind="ExternalInput").ap() if fio is None else io["ident"]
    x_out = nc.dram_tensor("x_out", [TPC, D], F32, kind="ExternalOutput").ap() if fio is None else io["x_out"]
    xm_scr = nc.dram_tensor(pfx + "xm_scr", [TPC, D], F32, kind="Internal").ap()
    dl_scr = nc.dram_tensor(pfx + "dl_scr", [DI, TC3], F32, kind="Internal").ap()
    du_scr = nc.dram_tensor(pfx + "du_scr", [DI, TC3], F32, kind="Internal").ap()
    bc_scr = nc.dram_tensor(pfx + "bc_scr", [32, TC3], F32, kind="Internal").ap()

    NTT = TC3 // 128

    with contextlib.ExitStack() as stk:
        e_ = stk.enter_context
        xt = [e_(nc.sbuf_tensor(pfx + f"xt{i}", [128, D], F32)) for i in range(2)]
        ident = e_(nc.sbuf_tensor(pfx + "identt", [128, 128], BF16))
        selp = e_(nc.sbuf_tensor(pfx + "selp", [128, 240], BF16))
        xc = e_(nc.sbuf_tensor(pfx + "xc", [128, D], F32))
        sq = e_(nc.sbuf_tensor(pfx + "sq", [128, D], F32))
        cols = e_(nc.sbuf_tensor(pfx + "cols", [128, 16], F32))
        ln_bf = e_(nc.sbuf_tensor(pfx + "ln_bf", [128, D], BF16))
        rnT = e_(nc.sbuf_tensor(pfx + "rnT", [128, DCH * TC3], BF16))
        gam = e_(nc.sbuf_tensor(pfx + "gam", [128, D], F32))
        uc_ext = e_(nc.sbuf_tensor(pfx + "uc_ext", [128, ICH * (TC3 + 3)], F32))
        ucv = e_(nc.sbuf_tensor(pfx + "ucv", [128, ICH * TC3], F32))
        zT = e_(nc.sbuf_tensor(pfx + "zT", [128, ICH * TC3], BF16))
        ucbs = e_(nc.sbuf_tensor(pfx + "ucbs", [128, TC3], BF16))
        yT = e_(nc.sbuf_tensor(pfx + "yT", [128, ICH * TC3], BF16))
        stg = e_(nc.sbuf_tensor(pfx + "stg", [128, TC3], F32))
        stg2 = e_(nc.sbuf_tensor(pfx + "stg2", [128, TC3], F32))
        xdb = e_(nc.sbuf_tensor(pfx + "xdb", [128, TC3], F32))
        cwcol = e_(nc.sbuf_tensor(pfx + "cwcol", [128, ICH * 4], F32))
        cbcol = e_(nc.sbuf_tensor(pfx + "cbcol", [128, ICH], F32))
        dtbc = e_(nc.sbuf_tensor(pfx + "dtbc", [128, ICH], F32))
        dpc = e_(nc.sbuf_tensor(pfx + "dpc", [128, ICH], F32))
        atab = e_(nc.sbuf_tensor(pfx + "atab", [128, 2 * NDB], F32))
        carr = e_(nc.sbuf_tensor(pfx + "carr", [128, 2 * NDB], F32))
        drep = e_(nc.sbuf_tensor(pfx + "drep", [128, TC3], F32))
        durep = e_(nc.sbuf_tensor(pfx + "durep", [128, TC3], F32))
        brep = e_(nc.sbuf_tensor(pfx + "brep2", [128, 2 * TC3], F32))
        crep = e_(nc.sbuf_tensor(pfx + "crep2", [128, 2 * TC3], F32))
        da = e_(nc.sbuf_tensor(pfx + "da", [128, TC3], F32))
        dub = e_(nc.sbuf_tensor(pfx + "dub", [128, TC3], F32))
        hst = e_(nc.sbuf_tensor(pfx + "hst", [128, TC3], F32))
        wcc = e_(nc.sbuf_tensor(pfx + "wcc", [128, TC3], BF16))
        wband = [e_(nc.sbuf_tensor(pfx + "wband0", [128, DCH * 128], BF16))] * 2
        xpb = e_(nc.sbuf_tensor(pfx + "xpb", [128, 2 * ICH * 128], BF16))
        dtwsb = e_(nc.sbuf_tensor(pfx + "dtwsb", [128, ICH * 128], BF16))
        owT = e_(nc.sbuf_tensor(pfx + "owT", [128, ICH * D], BF16))
        pst1 = e_(nc.psum_tensor(pfx + "pst1", [128, D], BF16))
        psP = [e_(nc.psum_tensor(pfx + f"psP{i}", [128, TC3], F32)) for i in range(2)]
        psY = e_(nc.psum_tensor(pfx + "psY", [128, TC3], F32))
        psF = e_(nc.psum_tensor(pfx + "psF", [128, D], F32))

        def emit(c):
            c.dy(lambda e: e.dma_start(out=ident[:], in_=ident_h))
            c.dy(lambda e: e.dma_start(out=selp[:], in_=selpad_h))
            scr = {"xc": xc, "sq": sq, "cols": cols}
            rnT3 = rnT[:].rearrange("p (kc t) -> p kc t", kc=DCH)
            uce3 = uc_ext[:].rearrange("p (kc t) -> p kc t", kc=ICH)
            ucv3 = ucv[:].rearrange("p (kc t) -> p kc t", kc=ICH)
            zT3 = zT[:].rearrange("p (kc t) -> p kc t", kc=ICH)
            yT3 = yT[:].rearrange("p (kc t) -> p kc t", kc=ICH)

            for i in range(NT):
                c.wait("sync", "st", c.now("st"))
                c.wait("sync", "v", c.now("v"))
                c.dy(lambda e, i=i: e.dma_start(
                    out=xt[i % 2][:], in_=x_in[i * 128:(i + 1) * 128, :]))
                c.wait("vector", "dy")
                c.wait("vector", "st", c.now("st"))
                emit_ln(c, xt[i % 2], scr, ln_bf, None, None, ident, ln_f32=xc)
                c.wait("sync", "v")
                c.store(lambda e, i=i: e.dma_start(
                    out=xm_scr[i * 128:(i + 1) * 128, :], in_=xc[:]))

            for li in range(2):
                c.dy(lambda e, li=li: e.dma_start(out=gam[:], in_=bcast_row(normw[li], D)))
                c.dy(lambda e, li=li: e.dma_start(out=cwcol[:], in_=convw[li]))
                c.dy(lambda e, li=li: e.dma_start(out=cbcol[:], in_=convb[li]))
                c.dy(lambda e, li=li: e.dma_start(out=dtbc[:], in_=dtbh[li]))
                c.dy(lambda e, li=li: e.dma_start(out=dpc[:], in_=dph[li]))
                c.dy(lambda e, li=li: e.dma_start(
                    out=atab[:].rearrange("p (h n) -> p h n", h=2), in_=alogt[li]))
                c.wait("gpsimd", "t", c.now("t"))
                c.dg(lambda e, li=li: e.dma_start(
                    out=owT[:].rearrange("p (kc n) -> p kc n", kc=ICH), in_=outwa[li]))
                c.dg(lambda e, li=li: e.dma_start(
                    out=xpb[:].rearrange("p (m j n) -> p m j n", m=2, j=ICH),
                    in_=xprjb[li]))
                c.dg(lambda e, li=li: e.dma_start(
                    out=dtwsb[:].rearrange("p (j o n) -> p j o n", j=ICH, o=1),
                    in_=dtwb[li]))
                c.wait("scalar", "dy")
                c.s(lambda e: e.activation(out=atab[:], in_=atab[:], func=AF.Exp))
                c.wait("vector", "s")
                c.v(lambda e: e.tensor_scalar_mul(out=atab[:], in0=atab[:], scalar1=-1.0))

                for sq_i in range(2):
                    c.wait("vector", "dy")
                    c.v(lambda e: e.memset(carr[:], 0.0))
                    for kc in range(ICH):
                        c.v0(lambda e, kc=kc: e.memset(uce3[:, kc, 0:3], 0.0))
                    for ch in range(NC3):
                        T0 = sq_i * S + ch * TC3
                        for tl in range(NTT):
                            i = T0 // 128 + tl
                            c.wait("sync", "st", c.now("st"))
                            c.wait("sync", "v", c.now("v"))
                            c.wait("sync", "s", c.now("s"))
                            c.dy(lambda e, i=i, tl=tl: e.dma_start(
                                out=xt[tl % 2][:], in_=xm_scr[i * 128:(i + 1) * 128, :]))
                            c.wait("vector", "dy")
                            c.wait("vector", "t", c.now("t"))
                            emit_ln(c, xt[tl % 2], scr, ln_bf,
                                    rnT3[:, :, tl * 128:(tl + 1) * 128], pst1, ident,
                                    rms=True, gamma_rep=gam)
                        if ch > 0:
                            c.wait("vector", "t", c.now("t"))
                            for kc in range(ICH):
                                c.v0(lambda e, kc=kc: e.tensor_copy(
                                    out=uce3[:, kc, 0:3], in_=uce3[:, kc, TC3:TC3 + 3]))
                        for och in range(2 * ICH):
                            bi = och % 2
                            c.wait("gpsimd", "t", c.now("t"))
                            c.dg(lambda e, li=li, och=och: e.dma_start(
                                out=wband[0][:].rearrange("p (kc n) -> p kc n", kc=DCH),
                                in_=inwb[li, och]))
                            c.wait("tensor", "dg")
                            c.wait("tensor", "s", c.now("s"))
                            c.wait("tensor", "v", c.now("v"))
                            for j in range(DCH):
                                inc = [("t", 1)] if j == DCH - 1 else []
                                c.op("tensor", lambda e, j=j, bi=bi: e.matmul(
                                    out=psP[bi][:], lhsT=wband[0][:, j * 128:(j + 1) * 128],
                                    rhs=rnT[:, j * TC3:(j + 1) * TC3],
                                    start=(j == 0), stop=(j == DCH - 1)), incs=inc)
                            if och < ICH:
                                c.wait("vector", "t")
                                c.v(lambda e, och=och, bi=bi: e.tensor_copy(
                                    out=uce3[:, och, 3:TC3 + 3], in_=psP[bi][:]))
                            else:
                                c.wait("scalar", "t")
                                c.s(lambda e, och=och, bi=bi: e.copy(
                                    out=zT3[:, och - ICH, :], in_=psP[bi][:]))
                        for kc in range(ICH):
                            c.v(lambda e, kc=kc: e.tensor_scalar_mul(
                                out=ucv3[:, kc, :], in0=uce3[:, kc, 0:TC3],
                                scalar1=cwcol[:, 4 * kc:4 * kc + 1]))
                            for k in range(1, 4):
                                c.v(lambda e, kc=kc, k=k: e.scalar_tensor_tensor(
                                    out=ucv3[:, kc, :], in0=uce3[:, kc, k:TC3 + k],
                                    scalar=cwcol[:, 4 * kc + k:4 * kc + k + 1],
                                    in1=ucv3[:, kc, :], op0=ALU.mult, op1=ALU.add))
                            c.v(lambda e, kc=kc: e.tensor_scalar_add(
                                out=ucv3[:, kc, :], in0=ucv3[:, kc, :],
                                scalar1=cbcol[:, kc:kc + 1]))
                            c.wait("scalar", "v")
                            c.s(lambda e, kc=kc: e.activation(
                                out=ucv3[:, kc, :], in_=ucv3[:, kc, :], func=AF.Silu))
                        c.wait("tensor", "s", c.now("s"))
                        c.wait("tensor", "v", c.now("v"))
                        for m in range(2):
                            c.wait("tensor", "v", c.now("v"))
                            for j in range(ICH):
                                c.wait("scalar", "t", c.now("t"))
                                c.s(lambda e, j=j: e.copy(out=ucbs[:], in_=ucv3[:, j, :]))
                                c.wait("tensor", "s", c.now("s"))
                                c.op("tensor", lambda e, m=m, j=j: e.matmul(
                                    out=psY[:],
                                    lhsT=xpb[:, (m * ICH + j) * 128:(m * ICH + j) * 128 + 128],
                                    rhs=ucbs[:],
                                    start=(j == 0), stop=(j == ICH - 1)), incs=[("t", 1)])
                            if m == 0:
                                c.wait("vector", "t")
                                c.v(lambda e: e.tensor_copy(out=wcc[:], in_=psY[:]))
                            else:
                                c.wait("vector", "t")
                                c.v(lambda e: e.tensor_copy(out=xdb[:], in_=psY[:]))
                        c.wait("sync", "v", c.now("v"))
                        c.dy(lambda e: e.dma_start(out=bc_scr[:, :], in_=xdb[0:32, :]))
                        c.wait("sync", "dg", c.now("dg"))
                        for och in range(ICH):
                            c.wait("tensor", "v", c.now("v"))
                            c.t(lambda e, och=och: e.matmul(
                                out=psP[0][:], lhsT=dtwsb[:, och * 128:(och + 1) * 128],
                                rhs=wcc[:], start=True, stop=True))
                            c.wait("vector", "t")
                            c.wait("vector", "dy", c.now("dy"))
                            c.v(lambda e, och=och: e.tensor_scalar_add(
                                out=stg[:], in0=psP[0][:], scalar1=dtbc[:, och:och + 1]))
                            c.v(lambda e: e.tensor_single_scalar(
                                out=stg2[:], in_=stg[:], scalar=0.0, op=ALU.max))
                            c.wait("scalar", "v")
                            c.s(lambda e: e.activation(out=da[:], in_=stg[:], func=AF.Abs))
                            c.s(lambda e: e.activation(out=da[:], in_=da[:], func=AF.Exp,
                                                       scale=-1.0))
                            c.s(lambda e: e.activation(out=da[:], in_=da[:], func=AF.Ln,
                                                       bias=1.0))
                            c.wait("vector", "s")
                            c.v(lambda e: e.tensor_add(out=stg[:], in0=stg2[:], in1=da[:]))
                            c.v(lambda e, och=och: e.tensor_mul(
                                out=stg2[:], in0=stg[:], in1=ucv3[:, och, :]))
                            c.wait("sync", "v", c.now("v"))
                            c.dy(lambda e, och=och: e.dma_start(
                                out=dl_scr[och * 128:(och + 1) * 128, :], in_=stg[:]))
                            c.dy(lambda e, och=och: e.dma_start(
                                out=du_scr[och * 128:(och + 1) * 128, :], in_=stg2[:]))
                        c.wait("sync", "dy", c.now("dy"))
                        c.wait("gpsimd", "dy", c.now("dy"))
                        for half in range(2):
                            c.dy(lambda e, half=half: e.dma_start(
                                out=brep[:, half * TC3:(half + 1) * TC3],
                                in_=bc_scr[(half * 8):(half * 8 + 8), :]
                                    .unsqueeze(1).broadcast_to([8, 16, TC3])))
                            c.dy(lambda e, half=half: e.dma_start(
                                out=crep[:, half * TC3:(half + 1) * TC3],
                                in_=bc_scr[(16 + half * 8):(16 + half * 8 + 8), :]
                                    .unsqueeze(1).broadcast_to([8, 16, TC3])))
                        dr2 = [drep, xc[:, TC3:2 * TC3]]
                        du2 = [durep, sq[:, 0:TC3]]
                        da_h = [da, xc[:, 0:TC3]]
                        v0_ = c.now("v")
                        s0_ = c.now("s")
                        vmk = {-2: v0_, -1: v0_}
                        smk = {-2: s0_, -1: s0_}
                        scm2 = [{-1: v0_}, {-1: v0_}]
                        it = 0
                        for g in range(NDB // 8):
                            for k in range(8):
                                db = g * 8 + k
                                bi2 = it % 2
                                q = c.dy if (k % 2 == 0) else c.dg
                                qe = "sync" if k % 2 == 0 else "gpsimd"
                                c.wait(qe, "v", vmk[it - 2])
                                c.wait(qe, "s", smk[it - 2])
                                q(lambda e, db=db, bi2=bi2: e.dma_start(
                                    out=dr2[bi2][:],
                                    in_=dl_scr[db * 16:(db + 1) * 16, :]
                                        .unsqueeze(0).broadcast_to([8, 16, TC3])))
                                q(lambda e, db=db, bi2=bi2: e.dma_start(
                                    out=du2[bi2][:],
                                    in_=du_scr[db * 16:(db + 1) * 16, :]
                                        .unsqueeze(0).broadcast_to([8, 16, TC3])))
                                for half in range(2):
                                    tile = half * NDB + db
                                    c.wait("scalar", "dy")
                                    c.wait("scalar", "dg")
                                    c.wait("scalar", "v", scm2[half][it - 1])
                                    c.s(lambda e, half=half, db=db, bi2=bi2: e.activation(
                                        out=da_h[half][:], in_=dr2[bi2][:], func=AF.Exp,
                                        scale=atab[:, half * NDB + db:half * NDB + db + 1]))
                                    if half == 1:
                                        smk[it] = c.now("s")
                                    c.wait("vector", "dy")
                                    c.wait("vector", "dg")
                                    c.v(lambda e, half=half, bi2=bi2: e.tensor_mul(
                                        out=dub[:], in0=du2[bi2][:],
                                        in1=brep[:, half * TC3:(half + 1) * TC3]))
                                    if half == 1:
                                        vmk[it] = c.now("v")
                                    c.wait("vector", "s")
                                    c.v(lambda e, tile=tile, half=half: e.tensor_tensor_scan(
                                        out=hst[:], data0=da_h[half][:], data1=dub[:],
                                        initial=carr[:, tile:tile + 1],
                                        op0=ALU.mult, op1=ALU.add))
                                    scm2[half][it] = c.now("v")
                                    c.v(lambda e, tile=tile: e.tensor_copy(
                                        out=carr[:, tile:tile + 1], in_=hst[:, TC3 - 1:TC3]))
                                    c.wait("vector", "t", c.now("t"))
                                    c.v(lambda e, half=half: e.tensor_mul(
                                        out=wcc[:], in0=hst[:],
                                        in1=crep[:, half * TC3:(half + 1) * TC3]))
                                    c.wait("tensor", "v", c.now("v"))
                                    c.op("tensor", lambda e, k=k, half=half: e.matmul(
                                        out=psY[:],
                                        lhsT=selp[:, 112 - 16 * k:240 - 16 * k],
                                        rhs=wcc[:],
                                        start=(k == 0 and half == 0),
                                        stop=(k == 7 and half == 1)), incs=[("t", 1)])
                                it += 1
                            c.wait("vector", "t")
                            c.v(lambda e, g=g: e.scalar_tensor_tensor(
                                out=stg[:], in0=ucv3[:, g, :], scalar=dpc[:, g:g + 1],
                                in1=psY[:], op0=ALU.mult, op1=ALU.add))
                            c.wait("scalar", "v")
                            c.s(lambda e, g=g: e.activation(
                                out=stg2[:], in_=zT3[:, g, :], func=AF.Silu))
                            c.wait("vector", "s")
                            c.v(lambda e, g=g: e.tensor_mul(
                                out=yT3[:, g, :], in0=stg[:], in1=stg2[:]))
                        y_v = c.now("v")
                        for tl in range(NTT):
                            i = T0 // 128 + tl
                            c.wait("tensor", "v", c.now("v"))
                            c.wait("tensor", "dg", c.now("dg"))
                            for j in range(ICH):
                                for nh in range(2):
                                    inc = [("t", 1)] if (j == ICH - 1 and nh == 1) else []
                                    c.op("tensor", lambda e, j=j, nh=nh, tl=tl: e.matmul(
                                        out=psF[:, nh * 512:(nh + 1) * 512],
                                        lhsT=yT3[:, j, tl * 128:(tl + 1) * 128],
                                        rhs=owT[:, j * D + nh * 512:j * D + nh * 512 + 512],
                                        start=(j == 0), stop=(j == ICH - 1)), incs=inc)
                            c.wait("sync", "st", c.now("st"))
                            c.wait("sync", "v", c.now("v"))
                            c.dy(lambda e, i=i: e.dma_start(
                                out=xt[0][:], in_=xm_scr[i * 128:(i + 1) * 128, :]))
                            c.wait("vector", "dy")
                            c.wait("vector", "t")
                            c.v(lambda e: e.tensor_add(out=xt[0][:], in0=psF[:], in1=xt[0][:]))
                            c.wait("sync", "v")
                            c.store(lambda e, i=i: e.dma_start(
                                out=xm_scr[i * 128:(i + 1) * 128, :], in_=xt[0][:]))
            c.wait("sync", "dy", c.now("dy"))
            c.dy(lambda e: e.dma_start(out=gam[:], in_=bcast_row(normf, D)))
            for i in range(NT):
                c.wait("sync", "st", c.now("st"))
                c.wait("sync", "v", c.now("v"))
                c.wait("sync", "s", c.now("s"))
                c.dy(lambda e, i=i: e.dma_start(
                    out=xt[0][:], in_=xm_scr[i * 128:(i + 1) * 128, :]))
                c.dy(lambda e, i=i: e.dma_start(
                    out=xt[1][:], in_=x_in[i * 128:(i + 1) * 128, :]))
                c.wait("vector", "dy")
                c.wait("vector", "s", c.now("s"))
                c.wait("vector", "st", c.now("st"))
                emit_ln(c, xt[0], scr, ln_bf, None, None, ident, ln_f32=xc,
                        rms=True, gamma_rep=gam)
                c.v(lambda e: e.tensor_add(out=xt[1][:], in0=xc[:], in1=xt[1][:]))
                c.wait("sync", "v")
                c.store(lambda e, i=i: e.dma_start(
                    out=x_out[i * 128:(i + 1) * 128, :], in_=xt[1][:]))
            c.wait("sync", "st", c.now("st"))

        if fio is None:
            run_phases(nc, emit)
        else:
            run_stage(nc, ctx, emit)
    return nc



IDENT = np.eye(128, dtype=ml_dtypes.bfloat16)


def prep_b(w):
    N, Kk = w.shape
    return np.ascontiguousarray(
        w.reshape(N // 128, 128, Kk // 128, 128).transpose(0, 3, 2, 1))


def prep_a(w):
    N, Kk = w.shape
    return np.ascontiguousarray(w.T.reshape(Kk // 128, 128, N).transpose(1, 0, 2))


def prep_b_eo(w):
    return np.ascontiguousarray(prep_b(w).transpose(1, 0, 2, 3))


def prep_scol(s_vec):
    return np.ascontiguousarray(s_vec.reshape(-1, 128).T)


CMASK = np.where(np.tril(np.ones((128, 128))) > 0, 0.0, -1e6).astype(np.float32)
TRIU = np.triu(np.ones((128, 128), np.float32))
SELPAD = np.zeros((128, 240), np.float32)
SELPAD[np.arange(128), 112 + np.arange(128) % 16] = 1.0
SELPAD = SELPAD.astype(ml_dtypes.bfloat16)


def _shards(x):
    return [np.ascontiguousarray(x[c * TPC:(c + 1) * TPC]) for c in range(NCORES)]


def _run(nc, in_maps, outkey="x_out"):
    res = run_bass_kernel_spmd(nc, in_maps, core_ids=list(range(NCORES)))
    return res.results


def _prep_l3_attn(I):
    li_in = []
    for li in range(2):
        xp = np.zeros((2048, 256), np.float32)
        xp[:, :160] = I["L3_attn_xproj_w"][li]
        alog = np.asarray(I["L3_attn_A_log"][li], np.float32)
        tab = np.zeros((2, 128, 128), np.float32)
        for h in range(2):
            p = np.arange(128)
            db = np.arange(128)
            tab[h] = alog[(db[None, :] * 16 + (p % 16)[:, None]), (h * 8 + p // 16)[:, None]]
        li_in.append(dict(
            inwb=prep_b(np.asarray(I["L3_attn_in_w"][li], np.float32).T),
            convw=np.ascontiguousarray(np.asarray(I["L3_attn_conv_w"][li], np.float32)
                                       .reshape(16, 128, 4).transpose(1, 0, 2).reshape(128, 64)),
            convb=prep_scol(np.asarray(I["L3_attn_conv_b"][li], np.float32)),
            xprjb=prep_b(xp.T),
            dtwb=prep_b(np.asarray(I["L3_attn_dt_w"][li], np.float32).T),
            dtbh=prep_scol(np.asarray(I["L3_attn_dt_b"][li], np.float32)),
            alogt=tab, dph=prep_scol(np.asarray(I["L3_attn_D_p"][li], np.float32)),
            outwa=prep_a(np.asarray(I["L3_attn_out_w"][li], np.float32).T)))
    st = {k: np.stack([li_in[0][k], li_in[1][k]]) for k in li_in[0]}
    st.update(normw=np.asarray(I["L3_attn_norm_w"], np.float32),
              normf=np.asarray(I["L3_attn_normf_w"], np.float32),
              selpad=SELPAD, ident=IDENT)
    return st




def build_fused():
    nc = bass.Bass("TRN2", target_bir_lowering=False, num_devices=8)
    F, Bq = F32, BF16
    di = lambda n, sh, dt=F32: nc.dram_tensor(n, sh, dt, kind="ExternalInput").ap()
    x_ext = di("x_in", [TPC, D])
    out_ext = nc.dram_tensor("x_out", [TPC, D], F32, kind="ExternalOutput").ap()
    ident_h = di("ident", [128, 128], BF16)
    cmask_h = di("cmask", [128, 128])
    triu_h = di("triu", [128, 128])
    selpad_h = di("selpad", [128, 240], BF16)
    maskrep_h = di("maskrep", [1, 64])
    xa = nc.dram_tensor("xa", [TPC, D], F32, kind="Internal").ap()
    xb = nc.dram_tensor("xb", [TPC, D], F32, kind="Internal").ap()

    io_l0f = dict(wT=di("l0f_wT", [D, D]), alpha=di("l0f_alpha", [1]),
                  bias=di("l0f_bias", [D]), ident=ident_h)
    io_l0a = dict(wqb=di("l0a_wqb", [ECH, 128, DCH, 128]), wkb=di("l0a_wkb", [ECH, 128, DCH, 128]),
                  wva=di("l0a_wva", [128, DCH, E2]), woa=di("l0a_woa", [128, ECH, D]),
                  bqh=di("l0a_bqh", [128, ECH]), bkh=di("l0a_bkh", [128, ECH]),
                  bvv=di("l0a_bvv", [E2]), bov=di("l0a_bov", [D]), ident=ident_h)
    io_l1f = dict(w1b=di("l1f_w1b", [NEXP, 128, HCH, DCH, 128]),
                  w2a=di("l1f_w2a", [NEXP, 128, HCH, D]),
                  wrt=di("l1f_wrt", [128, DCH, NEXP]),
                  b1h=di("l1f_b1h", [128, NEXP * HCH]), b2h=di("l1f_b2h", [NEXP, D]),
                  ident=ident_h)
    io_l1a = dict(wqb=di("l1a_wqb", [DCH, 128, DCH, 128]), wkb=di("l1a_wkb", [DCH, 128, DCH, 128]),
                  wva=di("l1a_wva", [128, DCH, D]), woa=di("l1a_woa", [128, DCH, D]),
                  bqh=di("l1a_bqh", [128, DCH]), bkh=di("l1a_bkh", [128, DCH]),
                  bvv=di("l1a_bvv", [D]), bov=di("l1a_bov", [D]),
                  cmask=cmask_h, ident=ident_h)
    io_l2r = dict(wrt=di("l2r_wrt", [128, DCH, NEXP]), ident=ident_h, triu=triu_h)
    io_l2m = dict(w1b=di("l2m_w1b", [NEXP, 128, HCH, DCH, 128]),
                  w2a=di("l2m_w2a", [NEXP, 128, HCH, D]),
                  maskrep=maskrep_h, ident=ident_h)
    io_l2a = dict(wkbh=di("l2a_wkbh", [DCH, 128, DCH, 128]), wvbh=di("l2a_wvbh", [DCH, 128, DCH, 128]),
                  wrbh=di("l2a_wrbh", [DCH, 128, DCH, 128]), woa=di("l2a_woa", [128, DCH, D]),
                  muk=di("l2a_muk", [128, DCH]), muv=di("l2a_muv", [128, DCH]),
                  mur=di("l2a_mur", [128, DCH]), wdec=di("l2a_wdec", [128, DCH]),
                  ufir=di("l2a_ufir", [128, DCH]), ident=ident_h)
    io_l3f = dict(w1b=di("l3f_w1b", [HCH, 128, DCH, 128]), w2b=di("l3f_w2b", [HCH, 128, DCH, 128]),
                  h1b=di("l3f_h1b", [HCH, 128, HCH, 128]), h2b=di("l3f_h2b", [HCH, 128, HCH, 128]),
                  w3a=di("l3f_w3a", [128, HCH, D]),
                  s1h=di("l3f_s1h", [128, HCH]), s2h=di("l3f_s2h", [128, HCH]),
                  hs1h=di("l3f_hs1h", [128, HCH]), hs2h=di("l3f_hs2h", [128, HCH]),
                  s3=di("l3f_s3", [D]), ident=ident_h)
    io_l3a = dict(inwb=di("l3a_inwb", [2, 2 * ICH, 128, DCH, 128]),
                  convw=di("l3a_convw", [2, 128, ICH * 4]), convb=di("l3a_convb", [2, 128, ICH]),
                  xprjb=di("l3a_xprjb", [2, 2, 128, ICH, 128]),
                  dtwb=di("l3a_dtwb", [2, ICH, 128, 1, 128]), dtbh=di("l3a_dtbh", [2, 128, ICH]),
                  alogt=di("l3a_alogt", [2, 2, 128, NDB]), dph=di("l3a_dph", [2, 128, ICH]),
                  outwa=di("l3a_outwa", [2, 128, ICH, D]),
                  normw=di("l3a_normw", [2, D]), normf=di("l3a_normf", [D]),
                  selpad=selpad_h, ident=ident_h)
    route_scr = []
    for tag in ("A", "B"):
        route_scr.append(dict(
            oh=nc.dram_tensor(f"oh{tag}", [TPC, NEXP], F32, kind="Internal").ap(),
            pm=nc.dram_tensor(f"pm{tag}", [TPC, 1], F32, kind="Internal").ap(),
            pos=nc.dram_tensor(f"pos{tag}", [TPC, 1], F32, kind="Internal").ap(),
            cnt=nc.dram_tensor(f"cnt{tag}", [1, NEXP], F32, kind="Internal").ap(),
            cnt_all=nc.dram_tensor(f"cntall{tag}", [NCORES, NEXP], F32,
                                   kind="Internal", addr_space="Shared").ap()))

    with contextlib.ExitStack() as st_:
        sems = {k: st_.enter_context(nc.semaphore(k)) for k in SEMS}
        ctx = Ctx(nc, sems)
        ctx.cur = None

        def st_io(xi, xo, extra):
            d = dict(extra)
            d["x_in"] = xi
            d["x_out"] = xo
            return d

        build_l0_ffn(fio=(nc, ctx, st_io(x_ext, xa, io_l0f), "a_"))
        build_l0_attn(fio=(nc, ctx, st_io(xa, xb, io_l0a), "b_"))
        build_l0_ffn(fio=(nc, ctx, st_io(xb, xa, io_l0f), "c_"))
        build_l1_ffn(fio=(nc, ctx, st_io(xa, xb, io_l1f), "d_"))
        build_l1_attn(fio=(nc, ctx, st_io(xb, xa, io_l1a), "e_"))
        build_l1_ffn(fio=(nc, ctx, st_io(xa, xb, io_l1f), "f_"))
        for ci, (xi, xo) in enumerate(((xb, xa), (xb, xa))):
            pass
        r = route_scr[0]
        build_l2_ffn_route(fio=(nc, ctx, dict(io_l2r, x_in=xb, oh_out=r["oh"],
                                              pm_out=r["pm"], pos_out=r["pos"],
                                              cnt_out=r["cnt"]), "g_"))
        build_l2_ffn_main(fio=(nc, ctx, dict(io_l2m, x_in=xb, x_out=xa,
                                             oh_in=r["oh"], pm_in=r["pm"], pos_in=r["pos"],
                                             cnt_loc=r["cnt"], cnt_all=r["cnt_all"]), "h_"))
        build_l2_attn(fio=(nc, ctx, st_io(xa, xb, io_l2a), "i_"))
        r = route_scr[1]
        build_l2_ffn_route(fio=(nc, ctx, dict(io_l2r, x_in=xb, oh_out=r["oh"],
                                              pm_out=r["pm"], pos_out=r["pos"],
                                              cnt_out=r["cnt"]), "j_"))
        build_l2_ffn_main(fio=(nc, ctx, dict(io_l2m, x_in=xb, x_out=xa,
                                             oh_in=r["oh"], pm_in=r["pm"], pos_in=r["pos"],
                                             cnt_loc=r["cnt"], cnt_all=r["cnt_all"]), "k_"))
        build_l3_ffn(fio=(nc, ctx, st_io(xa, xb, io_l3f), "l_"))
        build_l3_attn(fio=(nc, ctx, st_io(xb, xa, io_l3a), "m_"))
        build_l3_ffn(fio=(nc, ctx, st_io(xa, out_ext, io_l3f), "n_"))
    return nc


FUSED = True


def kernel(**inputs):
    I = {k: np.asarray(v, np.float32) for k, v in inputs.items()}
    x = np.ascontiguousarray(I["L0_attn_hidden_states"].reshape(B * S, D))
    cores = list(range(NCORES))

    nc_l0f = build_l0_ffn()
    nc_l0a = build_l0_attn()
    nc_l1f = build_l1_ffn()
    nc_l1a = build_l1_attn()
    nc_l2r = build_l2_ffn_route()
    nc_l2m = build_l2_ffn_main()
    nc_l2a = build_l2_attn()
    nc_l3f = build_l3_ffn()
    nc_l3a = build_l3_attn()

    l0f_w = dict(wT=np.ascontiguousarray(I["L0_ffn_weight"].T),
                 alpha=I["L0_ffn_alpha"].reshape(1), bias=I["L0_ffn_bias"], ident=IDENT)
    l0a_w = dict(wqb=prep_b(I["L0_attn_Wq"]), wkb=prep_b(I["L0_attn_Wk"]),
                 wva=prep_a(I["L0_attn_Wv"]), woa=prep_a(I["L0_attn_Wo"]),
                 bqh=prep_scol(I["L0_attn_bq"]), bkh=prep_scol(I["L0_attn_bk"]),
                 bvv=I["L0_attn_bv"], bov=I["L0_attn_bo"], ident=IDENT)
    l1f_w = dict(w1b=np.stack([prep_b_eo(I["L1_ffn_W1"][e]) for e in range(8)]),
                 w2a=np.stack([prep_a(I["L1_ffn_W2"][e]) for e in range(8)]),
                 wrt=prep_a(I["L1_ffn_Wr"]),
                 b1h=np.concatenate([prep_scol(I["L1_ffn_b1"][e]) for e in range(8)], axis=1),
                 b2h=I["L1_ffn_b2"], ident=IDENT)
    wq1 = np.ascontiguousarray(I["L1_attn_Wq"].transpose(0, 2, 1).reshape(1024, 1024))
    wk1 = np.ascontiguousarray(I["L1_attn_Wk"].transpose(0, 2, 1).reshape(1024, 1024))
    wv1 = np.ascontiguousarray(I["L1_attn_Wv"].transpose(0, 2, 1).reshape(1024, 1024))
    wo1 = np.ascontiguousarray(I["L1_attn_Wo"].reshape(1024, 1024))
    l1a_w = dict(wqb=prep_b(wq1), wkb=prep_b(wk1), wva=prep_a(wv1), woa=prep_a(wo1.T),
                 bqh=prep_scol(I["L1_attn_bq"].reshape(-1)),
                 bkh=prep_scol(I["L1_attn_bk"].reshape(-1)),
                 bvv=I["L1_attn_bv"].reshape(-1), bov=I["L1_attn_bo"],
                 cmask=CMASK, ident=IDENT)
    l2f_wr = prep_a(I["L2_ffn_Wr"].T)
    l2f_w1 = np.stack([prep_b_eo(I["L2_ffn_W1"][e].T) for e in range(8)])
    l2f_w2 = np.stack([prep_a(I["L2_ffn_W2"][e].T) for e in range(8)])
    l2a_w = dict(wkbh=prep_b(I["L2_attn_W_k"]), wvbh=prep_b(I["L2_attn_W_v"]),
                 wrbh=prep_b(I["L2_attn_W_r"]), woa=prep_a(I["L2_attn_W_o"]),
                 muk=prep_scol(I["L2_attn_time_mix_k"]), muv=prep_scol(I["L2_attn_time_mix_v"]),
                 mur=prep_scol(I["L2_attn_time_mix_r"]), wdec=prep_scol(I["L2_attn_time_decay"]),
                 ufir=prep_scol(I["L2_attn_time_first"]), ident=IDENT)
    l3f_w = dict(w1b=prep_b(I["L3_ffn_w1_w"]), w2b=prep_b(I["L3_ffn_w2_w"]),
                 h1b=prep_b(I["L3_ffn_h1_w"]), h2b=prep_b(I["L3_ffn_h2_w"]),
                 w3a=prep_a(I["L3_ffn_w3_w"]),
                 s1h=prep_scol(I["L3_ffn_w1_s"]), s2h=prep_scol(I["L3_ffn_w2_s"]),
                 hs1h=prep_scol(I["L3_ffn_h1_s"]), hs2h=prep_scol(I["L3_ffn_h2_s"]),
                 s3=I["L3_ffn_w3_s"], ident=IDENT)
    l3a_w = _prep_l3_attn(I)

    if FUSED:
        nc_f = build_fused()
        base = {}
        for pfx, wd in (("l0f_", l0f_w), ("l0a_", l0a_w), ("l1f_", l1f_w),
                        ("l1a_", l1a_w), ("l2a_", l2a_w), ("l3f_", l3f_w),
                        ("l3a_", l3a_w)):
            for k, v in wd.items():
                if k in ("ident", "cmask", "selpad"):
                    continue
                base[pfx + k] = v
        base["l2r_wrt"] = l2f_wr
        base["l2m_w1b"] = l2f_w1
        base["l2m_w2a"] = l2f_w2
        base["ident"] = IDENT
        base["cmask"] = CMASK
        base["triu"] = TRIU
        base["selpad"] = SELPAD
        xs = _shards(x)
        in_maps = []
        for c in cores:
            mask = np.zeros(8, np.float32)
            mask[0:c] = 1.0
            in_maps.append({**base, "x_in": xs[c],
                            "maskrep": np.repeat(mask, 8).reshape(1, 64)})
        res = run_bass_kernel_spmd(nc_f, in_maps, core_ids=cores)
        xo = np.concatenate([r["x_out"] for r in res.results], 0)
        return xo.reshape(B, S, D)

    def step(nc, w):
        nonlocal x
        xs = _shards(x)
        res = _run(nc, [{"x_in": xs[c], **w} for c in cores])
        x = np.concatenate([r["x_out"] for r in res], 0)

    def step_l2ffn():
        nonlocal x
        xs = _shards(x)
        r1 = _run(nc_l2r, [{"x_in": xs[c], "wrt": l2f_wr, "ident": IDENT, "triu": TRIU}
                           for c in cores])
        counts = np.stack([r["cnt_out"][0] for r in r1])
        im2 = []
        for c in cores:
            prevT = np.zeros((8, 8), np.float32)
            if c > 0:
                prevT[:, 0:c] = counts[0:c].T
            im2.append({"x_in": xs[c], "w1b": l2f_w1, "w2a": l2f_w2,
                        "oh_in": r1[c]["oh_out"], "pm_in": r1[c]["pm_out"],
                        "pos_in": r1[c]["pos_out"], "prevT": prevT.reshape(1, 64),
                        "ident": IDENT})
        res = _run(nc_l2m, im2)
        x = np.concatenate([r["x_out"] for r in res], 0)

    step(nc_l0f, l0f_w)
    step(nc_l0a, l0a_w)
    step(nc_l0f, l0f_w)
    step(nc_l1f, l1f_w)
    step(nc_l1a, l1a_w)
    step(nc_l1f, l1f_w)
    step_l2ffn()
    step(nc_l2a, l2a_w)
    step_l2ffn()
    step(nc_l3f, l3f_w)
    step(nc_l3a, l3a_w)
    step(nc_l3f, l3f_w)
    return x.reshape(B, S, D)



